# version 04
# baseline (speedup 1.0000x reference)
import numpy as np
import ml_dtypes
from contextlib import ExitStack

import concourse.bass as bass
import concourse.mybir as mybir
from concourse.bass_utils import run_bass_kernel_spmd

F32 = mybir.dt.float32
BF16 = mybir.dt.bfloat16
U32 = mybir.dt.uint32
AF = mybir.ActivationFunctionType
ALU = mybir.AluOpType
AX = mybir.AxisListType
BF = ml_dtypes.bfloat16

T = 2048
D = 2048
NCH = 16
NEG = -1.0e30
EPS = 1e-5
RSQ = 1.0 / float(np.sqrt(128.0))
SAME_ENGINE_SYNC = True


class Prog:
    ENG = {"t": "tensor", "v": "vector", "s": "scalar", "g": "gpsimd", "y": "sync"}

    def __init__(self, nc, ctx, same_engine_sync=SAME_ENGINE_SYNC):
        self.nc = nc
        self.ctx = ctx
        self.same = same_engine_sync
        self.ops = []
        self.esem = {e: ctx.enter_context(nc.semaphore("se_" + e)) for e in self.ENG}
        self.chsem = {}
        self.chn = {}
        self.wr = {}
        self.rd = {}
        self.last_eng = {}
        self.last_ch = {}

    def ch(self, name):
        if name not in self.chsem:
            self.chsem[name] = self.ctx.enter_context(self.nc.semaphore("ch_" + name))
            self.chn[name] = 0
        return name

    def op(self, eng, fn, r=(), w=(), ch=None, extra=()):
        idx = len(self.ops)
        deps = set(extra)
        for k in r:
            deps.update(self.wr.get(k, ()))
            self.rd.setdefault(k, []).append(idx)
        for k in w:
            rs = self.rd.get(k)
            if rs:
                deps.update(rs)
                deps.update(self.wr.get(k, ()))
                self.rd[k] = []
                self.wr[k] = [idx]
            else:
                self.wr.setdefault(k, []).append(idx)
        deps.discard(idx)
        chw = {}
        for dd in deps:
            dch = self.ops[dd]["ch"]
            if dch is not None:
                chw[dch] = 16 * self.chn[dch]
        rec = {"i": idx, "e": eng, "fn": fn, "d": sorted(deps), "ch": ch, "ci": None,
               "inc": None, "w8": (), "ec": 0, "chw": chw}
        if ch is not None:
            self.ch(ch)
            self.chn[ch] += 1
            rec["ci"] = self.chn[ch]
            self.last_ch[ch] = idx
        self.last_eng[eng] = idx
        self.ops.append(rec)
        return idx

    def barrier(self):
        prev = list(self.last_eng.values()) + list(self.last_ch.values())
        for e in self.ENG:
            self.op(e, lambda eng: eng.nop(), extra=prev)

    def materialize(self):
        dep_tgt = set()
        for rec in self.ops:
            dep_tgt.update(rec["d"])
        cnt = {e: 0 for e in self.ENG}
        for rec in self.ops:
            if rec["ch"] is None:
                if rec["i"] in dep_tgt:
                    cnt[rec["e"]] += 1
                    rec["inc"] = cnt[rec["e"]]
                rec["ec"] = cnt[rec["e"]]
        guar = {e: {} for e in self.ENG}
        snap = {}
        for rec in self.ops:
            e = rec["e"]
            g = guar[e]
            need = {}
            for d in rec["d"]:
                dep = self.ops[d]
                if dep["ch"] is not None:
                    src, val = ("D", dep["ch"]), rec["chw"][dep["ch"]]
                else:
                    if dep["e"] == e and (e == "t" or not self.same):
                        continue
                    src, val = ("E", dep["e"]), dep["inc"]
                if need.get(src, 0) < val:
                    need[src] = val
            waits = []
            for src, val in need.items():
                if g.get(src, 0) >= val:
                    continue
                waits.append((src, val))
                g[src] = val
            if waits:
                for d in rec["d"]:
                    for s2, v2 in snap.get(d, {}).items():
                        if g.get(s2, 0) < v2:
                            g[s2] = v2
            rec["w8"] = waits
            if rec["i"] in dep_tgt:
                gg = dict(g)
                if rec["ch"] is None:
                    gg[("E", e)] = rec["ec"]
                else:
                    gg[("D", rec["ch"])] = 16 * rec["ci"]
                snap[rec["i"]] = gg

        per_eng = {e: [rec for rec in self.ops if rec["e"] == e] for e in self.ENG}
        final_ch = dict(self.chn)
        blk = self.ctx.enter_context(self.nc.Block())
        esem, chsem = self.esem, self.chsem

        def mk(e, recs):
            def body(eng):
                for rec in recs:
                    w8 = rec["w8"]
                    for (src, val) in w8[1:]:
                        sem = chsem[src[1]] if src[0] == "D" else esem[src[1]]
                        eng.wait_ge(sem, val)
                    inst = rec["fn"](eng)
                    if w8:
                        src, val = w8[0]
                        sem = chsem[src[1]] if src[0] == "D" else esem[src[1]]
                        inst._wait_ge(sem, val)
                    if rec["ch"] is not None:
                        inst.then_inc(chsem[rec["ch"]], 16)
                    elif rec["inc"] is not None:
                        inst.then_inc(esem[rec["e"]], 1)
                if e == "y":
                    for c, n in final_ch.items():
                        if n:
                            eng.wait_ge(chsem[c], 16 * n)
            return body

        for e, attr in self.ENG.items():
            getattr(blk, attr)(mk(e, per_eng[e]))



def mm(p, out, l, r, start, stop, rd, wr):
    p.op("t", lambda eng, o=out, ll=l, rr=r, a=start, b=stop:
         eng.matmul(o, ll, rr, start=a, stop=b), r=rd, w=wr)


def tr(p, out, in_, ident, rd, wr):
    p.op("t", lambda eng, o=out, i=in_, e=ident: eng.transpose(o, i, e), r=rd, w=wr)


def act(p, out, in_, func, rd, wr, bias=0.0, scale=1.0, accum=None):
    def f(eng, o=out, i=in_, fu=func, b=bias, s=scale, a=accum):
        if a is None:
            return eng.activation(out=o, in_=i, func=fu, bias=b, scale=s)
        return eng.activation(out=o, in_=i, func=fu, bias=b, scale=s, accum_out=a)
    p.op("s", f, r=rd, w=wr)


def vop(p, meth, args, rd, wr):
    p.op("v", lambda eng, m=meth, a=args: getattr(eng, m)(*a), r=rd, w=wr)


def img3(sb2d, dr2d, width):
    o = sb2d.rearrange("p (c h) -> p c h", h=width)
    i = dr2d.rearrange("(c p) h -> p c h", p=128)
    return o, i


def dma(p, eng, out, in_, ch, rd, wr, transpose=False):
    if transpose:
        p.op(eng, lambda e, o=out, i=in_: e.dma_start_transpose(out=o, in_=i),
             r=rd, w=wr, ch=ch)
    else:
        p.op(eng, lambda e, o=out, i=in_: e.dma_start(out=o, in_=i),
             r=rd, w=wr, ch=ch)



def emit_load_x(p, C):
    for c in range(NCH):
        dma(p, "y", C["x"][c], C["xin"][128 * c:128 * (c + 1), :], "xin",
            rd=(), wr=(f"x{c}",))


def emit_store_x(p, C):
    for c in range(NCH):
        dma(p, "y", C["xout"][128 * c:128 * (c + 1), :], C["x"][c], "xout",
            rd=(f"x{c}",), wr=())


def emit_ln(p, C, sc, need_lo):
    stats, mv = sc["lnstat"], sc["lnmv"]
    vv, rbuf, nmr, t1 = sc["lnv"], sc["lnr"], sc["lnnmr"], sc["lnt1"]
    for c in range(NCH):
        sk = f"lnst{c % 2}"
        st = sc["lnstat"] if c % 2 == 0 else sc["lnstat2"]
        for j in range(4):
            vop(p, "bn_stats", (st[:, 6 * j:6 * j + 6], C["x"][c][:, 512 * j:512 * (j + 1)]),
                rd=(f"x{c}",), wr=(sk,))
        vop(p, "bn_aggr", (mv[:, 2 * c:2 * c + 2], st), rd=(sk,), wr=("lnmv",))
    for c in range(NCH):
        vop(p, "tensor_scalar_add", (vv[:, c:c + 1], mv[:, 2 * c + 1:2 * c + 2], EPS),
            rd=("lnmv",), wr=("lnv",))
    act(p, t1, vv, AF.Sqrt, rd=("lnv",), wr=("lnt1",))
    vop(p, "reciprocal", (rbuf, t1), rd=("lnt1",), wr=("lnr",))
    vop(p, "tensor_mul", (t1, rbuf, rbuf), rd=("lnr",), wr=("lnt1",))
    vop(p, "tensor_mul", (t1, t1, vv), rd=("lnt1", "lnv"), wr=("lnt1",))
    vop(p, "tensor_scalar", (t1, t1, -0.5, 1.5, ALU.mult, ALU.add), rd=("lnt1",), wr=("lnt1",))
    vop(p, "tensor_mul", (rbuf, rbuf, t1), rd=("lnr", "lnt1"), wr=("lnr",))
    for c in range(NCH):
        vop(p, "scalar_tensor_tensor",
            (nmr[:, c:c + 1], mv[:, 2 * c:2 * c + 1], -1.0, rbuf[:, c:c + 1], ALU.mult, ALU.mult),
            rd=("lnmv", "lnr"), wr=("lnnmr",))
    for c in range(NCH):
        k = c % 2
        xf, hi = sc["lnxf"][k], sc["lnhi"][k]
        act(p, xf, C["x"][c], AF.Identity, rd=(f"x{c}", "lnr", "lnnmr"),
            wr=(f"lnxf{k}",), bias=nmr[:, c:c + 1], scale=rbuf[:, c:c + 1])
        vop(p, "tensor_copy", (hi, xf), rd=(f"lnxf{k}",), wr=(f"lnhi{k}",))
        dma(p, "y", C["xnh"][128 * c:128 * (c + 1), :], hi, f"st_xnh{k}",
            rd=(f"lnhi{k}",), wr=("xnh_d",))
        if need_lo:
            lo = sc["lnlo"][k]
            vop(p, "tensor_sub", (lo, xf, hi), rd=(f"lnxf{k}", f"lnhi{k}"), wr=(f"lnlo{k}",))
            dma(p, "y", C["xnl"][128 * c:128 * (c + 1), :], lo, f"st_xnl{k}",
                rd=(f"lnlo{k}",), wr=("xnl_d",))


def emit_route(p, C, sc, kind):
    PC = C["PC"]
    ghi, glo = sc["ghi"], sc["glo"]
    vop(p, "tensor_copy", (ghi, sc["gsb"]), rd=("gsb",), wr=("ghi",))
    vop(p, "tensor_sub", (glo, sc["gsb"], ghi), rd=("gsb", "ghi"), wr=("glo",))
    lgt, lgc = sc["lgt"], sc["lgc"]
    hi5, lo5 = sc["rsl_hi"], sc["rsl_lo"]
    for t5 in range(4):
        t0 = 512 * t5
        for d in range(NCH):
            dma(p, "y", hi5[:, 512 * d:512 * (d + 1)],
                C["xnh"][t0:t0 + 512, 128 * d:128 * (d + 1)], "r_hi",
                rd=("xnh_d",), wr=("rslh",), transpose=True)
            dma(p, "y", lo5[:, 512 * d:512 * (d + 1)],
                C["xnl"][t0:t0 + 512, 128 * d:128 * (d + 1)], "r_lo",
                rd=("xnl_d",), wr=("rsll",), transpose=True)
        n3 = 3 * NCH
        k = 0
        for (gg, xx, gk, xk) in ((ghi, hi5, "ghi", "rslh"), (glo, hi5, "glo", "rslh"),
                                 (ghi, lo5, "ghi", "rsll")):
            for d in range(NCH):
                mm(p, PC[0:8, 0:512], gg[:, 8 * d:8 * d + 8], xx[:, 512 * d:512 * (d + 1)],
                   start=(k == 0), stop=(k == n3 - 1), rd=(gk, xk), wr=("PC0",))
                k += 1
        vop(p, "tensor_copy", (lgt, PC[0:8, 0:512]), rd=("PC0",), wr=("lgt",))
        for j in range(4):
            c = 4 * t5 + j
            tr(p, PC[:, 512:520], lgt[:, 128 * j:128 * (j + 1)], sc["eyef"][0:8, 0:8],
               rd=("lgt", "eyef"), wr=("PC1",))
            vop(p, "tensor_copy", (lgc, PC[:, 512:520]), rd=("PC1",), wr=("lgc",))
            mx8, idx = sc["mx8"], sc["idx8"]
            if kind == "sigmoid":
                sco = sc["sco"]
                act(p, sco, lgc, AF.Sigmoid, rd=("lgc",), wr=("sco",))
                vop(p, "max", (mx8, sco), rd=("sco",), wr=("mx8",))
                vop(p, "max_index", (idx, mx8, sco), rd=("sco", "mx8"), wr=("idx8",))
                vop(p, "tensor_copy", (sc["s0c"][:, c:c + 1], mx8[:, 0:1]), rd=("mx8",), wr=("s0c",))
                vop(p, "tensor_copy", (sc["s1c"][:, c:c + 1], mx8[:, 1:2]), rd=("mx8",), wr=("s1c",))
            else:
                mrow = sc["rm1"]
                vop(p, "tensor_reduce", (mrow, lgc, AX.X, ALU.max, False, False, True),
                    rd=("lgc",), wr=("rm1",))
                sco = sc["sco"]
                act(p, sco, lgc, AF.Exp, rd=("lgc", "rm1"), wr=("sco",), bias=mrow)
                vop(p, "tensor_reduce", (sc["rs1"], sco, AX.X, ALU.add, False, False, False),
                    rd=("sco",), wr=("rs1",))
                vop(p, "reciprocal", (sc["ri1"], sc["rs1"]), rd=("rs1",), wr=("ri1",))
                vop(p, "max", (mx8, sco), rd=("sco",), wr=("mx8",))
                vop(p, "max_index", (idx, mx8, sco), rd=("sco", "mx8"), wr=("idx8",))
                vop(p, "scalar_tensor_tensor",
                    (sc["s0c"][:, c:c + 1], mx8[:, 0:1], 0.5, sc["ri1"], ALU.mult, ALU.mult),
                    rd=("mx8", "ri1"), wr=("s0c",))
                vop(p, "scalar_tensor_tensor",
                    (sc["s1c"][:, c:c + 1], mx8[:, 1:2], 0.5, sc["ri1"], ALU.mult, ALU.mult),
                    rd=("mx8", "ri1"), wr=("s1c",))
            vop(p, "tensor_copy", (sc["e0c"][:, c:c + 1], idx[:, 0:1]), rd=("idx8",), wr=("e0c",))
            vop(p, "tensor_copy", (sc["e1c"][:, c:c + 1], idx[:, 1:2]), rd=("idx8",), wr=("e1c",))


def emit_ffn(p, C, sc, cfg):
    PA, PB, PC = C["PA"], C["PB"], C["PC"]
    tb = cfg["tb"]
    ntb = T // tb

    for pe, ex in enumerate(cfg["experts"]):
        H = ex["H"]
        nh = H // 128
        groups = ex["groups"]
        sig = ex.get("sigma")
        comb = ex.get("comb")
        if sig is not None:
            sgr = sc["sigr"]
            vop(p, "tensor_single_scalar", (sc["cm0"], sc["e0c"], float(sig), ALU.is_equal),
                rd=("e0c",), wr=("cm0",))
            vop(p, "tensor_mul", (sc["cm0"], sc["cm0"], sc["s0c"]), rd=("cm0", "s0c"), wr=("cm0",))
            vop(p, "tensor_single_scalar", (sc["cm1"], sc["e1c"], float(sig), ALU.is_equal),
                rd=("e1c",), wr=("cm1",))
            vop(p, "tensor_mul", (sc["cm1"], sc["cm1"], sc["s1c"]), rd=("cm1", "s1c"), wr=("cm1",))
            vop(p, "tensor_add", (sc["sigc"], sc["cm0"], sc["cm1"]), rd=("cm0", "cm1"), wr=("sigc",))
            for c in range(NCH):
                tr(p, PC[0:1, 512:640], sc["sigc"][:, c:c + 1], sc["eyef"],
                   rd=("sigc", "eyef"), wr=("PC1",))
                vop(p, "tensor_copy", (sgr[0:1, 128 * c:128 * (c + 1)], PC[0:1, 512:640]),
                    rd=("PC1",), wr=("sigr",))
            for t5 in range(4):
                mm(p, PC[:, 512:1024], sc["onesf"], sgr[0:1, 512 * t5:512 * (t5 + 1)],
                   start=True, stop=True, rd=("sigr", "onesf"), wr=("PC1",))
                vop(p, "tensor_copy", (sc["sigb"][:, 512 * t5:512 * (t5 + 1)], PC[:, 512:1024]),
                    rd=("PC1",), wr=("sigb",))
        if comb is not None:
            cht = sc["chat"]
            vop(p, "tensor_single_scalar", (sc["cm0"], sc["e0c"], float(comb), ALU.is_equal),
                rd=("e0c",), wr=("cm0",))
            vop(p, "tensor_mul", (sc["cm0"], sc["cm0"], sc["s0c"]), rd=("cm0", "s0c"), wr=("cm0",))
            vop(p, "tensor_single_scalar", (sc["cm1"], sc["e1c"], float(comb), ALU.is_equal),
                rd=("e1c",), wr=("cm1",))
            vop(p, "tensor_mul", (sc["cm1"], sc["cm1"], sc["s1c"]), rd=("cm1", "s1c"), wr=("cm1",))
            vop(p, "tensor_add", (cht, sc["cm0"], sc["cm1"]), rd=("cm0", "cm1"), wr=("chat",))

        pz1 = [PA[:, 0:512], PA[:, 512:1024], PA[:, 1024:1536], PB[:, 0:512]]
        pz3 = [PB[:, 512:1024], PC[:, 0:512], PC[:, 512:1024], C["PD"]]
        zk1 = ["PA0", "PA1", "PA2", "PB0"]
        zk3 = ["PB1", "PC0", "PC1", "PD0"]
        for tbi in range(ntb):
            for gi, (h0, gw) in enumerate(groups):
                nci = gw // 128
                w1s, w3s = sc["wsl1"], sc["wsl3"]
                dma(p, "g", w1s[:, 0:16 * gw], ex["w1s"](gi), "w1c", rd=(), wr=("w1s",))
                dma(p, "g", w3s[:, 0:16 * gw], ex["w3s"](gi), "w3c", rd=(), wr=("w3s",))
                for t5 in range(tb // 512):
                    t0 = tbi * tb + 512 * t5
                    for d in range(NCH):
                        xb = sc["xbuf"][d % 2]
                        dma(p, "y", xb, C["xnh"][t0:t0 + 512, 128 * d:128 * (d + 1)],
                            f"xb{d % 2}", rd=("xnh_d",), wr=(f"xb{d % 2}",), transpose=True)
                        for ci in range(nci):
                            lsl = w1s[:, d * gw + 128 * ci: d * gw + 128 * (ci + 1)]
                            mm(p, pz1[ci], lsl, xb, start=(d == 0), stop=(d == NCH - 1),
                               rd=("w1s", f"xb{d % 2}"), wr=(zk1[ci],))
                            ls3 = w3s[:, d * gw + 128 * ci: d * gw + 128 * (ci + 1)]
                            mm(p, pz3[ci], ls3, xb, start=(d == 0), stop=(d == NCH - 1),
                               rd=("w3s", f"xb{d % 2}"), wr=(zk3[ci],))
                    for ci in range(nci):
                        hc = h0 // 128 + ci
                        hs = sc["hsum"][:, hc * tb + 512 * t5: hc * tb + 512 * (t5 + 1)]
                        if sig is not None:
                            sgs = sc["sigb"][:, t0:t0 + 512]
                            vop(p, "tensor_mul", (sc["zt1"], pz1[ci], sgs),
                                rd=(zk1[ci], "sigb"), wr=("zt1",))
                            act(p, sc["zt1"], sc["zt1"], AF.Silu, rd=("zt1",), wr=("zt1",))
                            vop(p, "tensor_mul", (sc["zt2"], pz3[ci], sgs),
                                rd=(zk3[ci], "sigb"), wr=("zt2",))
                            vop(p, "tensor_mul", (hs, sc["zt1"], sc["zt2"]),
                                rd=("zt1", "zt2"), wr=("hsum",))
                        else:
                            act(p, sc["zt1"], pz1[ci], AF.Silu, rd=(zk1[ci],), wr=("zt1",))
                            vop(p, "tensor_mul", (hs, sc["zt1"], pz3[ci]),
                                rd=("zt1", zk3[ci]), wr=("hsum",))
            for dt in range(4):
                w2s = sc["wsl2"]
                dma(p, "g", w2s[:, 0:nh * 512], ex["w2s"](dt), "w2c", rd=(), wr=("w2s",))
                for ts in range(tb // 128):
                    tc = (tbi * tb) // 128 + ts
                    py = PA[:, 0:512]
                    for hc in range(nh):
                        mm(p, py, sc["hsum"][:, hc * tb + 128 * ts: hc * tb + 128 * (ts + 1)],
                           w2s[:, 512 * hc:512 * (hc + 1)], start=(hc == 0), stop=(hc == nh - 1),
                           rd=("hsum", "w2s"), wr=("PA0",))
                    xs = C["x"][tc][:, 512 * dt:512 * (dt + 1)]
                    if comb is not None:
                        scalar = sc["chat"][:, tc:tc + 1]
                        p.op("v", lambda eng, o=xs, i=py, s=scalar:
                             eng.scalar_tensor_tensor(o, i, s, o, ALU.mult, ALU.add),
                             r=("PA0", f"x{tc}", "chat"), w=(f"x{tc}",))
                    else:
                        p.op("v", lambda eng, o=xs, i=py:
                             eng.scalar_tensor_tensor(o, i, 0.5, o, ALU.mult, ALU.add),
                             r=("PA0", f"x{tc}"), w=(f"x{tc}",))


def emit_attn_phaseA(p, C, sc, cfg):
    PA, PB = C["PA"], C["PB"]
    xsl = sc["xnslab"]
    pq = [PA[:, 0:512], PA[:, 512:1024], PA[:, 1024:1536], PB[:, 0:512]]
    pk = ["PA0", "PA1", "PA2", "PB0"]
    for half in range(2):
        toff = 1024 * half
        for d in range(NCH):
            dma(p, "y", xsl[:, 1024 * d:1024 * (d + 1)],
                C["xnh"][toff:toff + 1024, 128 * d:128 * (d + 1)], "xnsl",
                rd=("xnh_d",), wr=("xnsl",), transpose=True)
        for proj, wap, bias, bk, scl, dst in (
                ("q", cfg["wqi"], cfg.get("bqs"), "bqs", RSQ, C["qd"]),
                ("k", cfg["wki"], cfg.get("bkc"), "bkc", 1.0, C["kd"])):
            for n in range(NCH):
                wb = n % 2
                ws = sc["wpj"][wb]
                dma(p, "g", ws, wap[n], f"wp{wb}", rd=(), wr=(f"wpj{wb}",))
                for t5 in range(2):
                    ps, psk = pq[t5], pk[t5]
                    for d in range(NCH):
                        mm(p, ps, ws[:, 128 * d:128 * (d + 1)],
                           xsl[:, 1024 * d + 512 * t5: 1024 * d + 512 * (t5 + 1)],
                           start=(d == 0), stop=(d == NCH - 1),
                           rd=(f"wpj{wb}", "xnsl"), wr=(psk,))
                    stg = sc["qkstg"][t5 % 2]
                    b = bias[:, n:n + 1] if bias is not None else 0.0
                    brd = (psk, bk) if bias is not None else (psk,)
                    act(p, stg, ps, AF.Identity, rd=brd, wr=(f"qkst{t5 % 2}",),
                        bias=b, scale=scl)
                    dma(p, "y", dst[n, :, toff + 512 * t5: toff + 512 * (t5 + 1)], stg,
                        f"qk{t5 % 2}", rd=(f"qkst{t5 % 2}",), wr=(proj + "_d",))
        for nt in range(4):
            wb = 0
            wv = sc["wvsl"][wb]
            dma(p, "g", wv, cfg["wvi"][nt], f"wv{wb}", rd=(), wr=(f"wvsl{wb}",))
            for ts in range(8):
                ps, psk = pq[ts % 4], pk[ts % 4]
                has_b = cfg.get("bvr") is not None
                for d in range(NCH):
                    mm(p, ps, xsl[:, 1024 * d + 128 * ts: 1024 * d + 128 * (ts + 1)],
                       wv[:, 512 * d:512 * (d + 1)], start=(d == 0),
                       stop=(d == NCH - 1 and not has_b),
                       rd=(f"wvsl{wb}", "xnsl"), wr=(psk,))
                if has_b:
                    mm(p, ps, sc["onesf"], cfg["bvr"][0:1, 512 * nt:512 * (nt + 1)],
                       start=False, stop=True, rd=("bvr", "onesf"), wr=(psk,))
                stg = sc["qkstg"][ts % 2]
                act(p, stg, ps, AF.Identity, rd=(psk,), wr=(f"qkst{ts % 2}",))
                dma(p, "y", C["vd"][toff + 128 * ts: toff + 128 * (ts + 1),
                                    512 * nt:512 * (nt + 1)], stg,
                    f"qk{ts % 2}", rd=(f"qkst{ts % 2}",), wr=("v_d",))


def emit_attn_phaseB(p, C, sc, cfg):
    PA, PB, PC, PEb = C["PA"], C["PB"], C["PC"], C["PEb"]
    alibi = cfg["alibi"]
    for st in range(4):
        qst = sc["qst"]
        dma(p, "y", qst.rearrange("p (h t) -> p h t", t=512),
            C["qd"][:, :, 512 * st:512 * (st + 1)].rearrange("h p t -> p h t"),
            "qstld", rd=("q_d",), wr=("qst",))
        for nt in range(4):
            pass
        for h in range(NCH):
            kmax = 512 * (st + 1)
            nkc = kmax // 128
            kh, vh = sc["kh"], sc["vh"]
            dma(p, "y", kh[:, 0:kmax], C["kd"][h, :, 0:kmax], "khld", rd=("k_d",), wr=("kh",))
            ovh, ivh = img3(vh[:, 0:kmax], C["vd"][0:kmax, 128 * h:128 * (h + 1)], 128)
            dma(p, "y", ovh, ivh, "vhld", rd=("v_d",), wr=("vh",))
            if alibi:
                dma(p, "y", sc["alr"][0:1, :], C["alibihi"][h:h + 1, :], "alr",
                    rd=(), wr=("alr",))
            spcs = [PA[:, 0:512], PA[:, 512:1024], PA[:, 1024:1536], PB[:, 0:512]]
            spk = ["PA0", "PA1", "PA2", "PB0"]
            for sub in range(4):
                q0 = 512 * st + 128 * sub
                km = q0 + 128
                nkb = (km + 511) // 512
                for kb in range(nkb):
                    w = min(512, km - 512 * kb)
                    mm(p, spcs[kb][:, 0:w], qst[:, 512 * h + 128 * sub: 512 * h + 128 * (sub + 1)],
                       kh[:, 512 * kb:512 * kb + w], start=True, stop=not alibi,
                       rd=("qst", "kh"), wr=(spk[kb],))
                    if alibi:
                        mm(p, spcs[kb][:, 0:w], sc["ones1b"],
                           sc["alr"][0:1, 512 * kb:512 * kb + w], start=False, stop=True,
                           rd=("alr", "ones1b"), wr=(spk[kb],))
                dkb = (km - 1) // 512
                off = q0 - 512 * dkb
                p.op("v", lambda eng, o=spcs[dkb][:, off:off + 128], t=sc["tri_s"]:
                     eng.tensor_add(o, o, t), r=(spk[dkb], "tri_s"), w=(spk[dkb],))
                rm = sc["rmx"]
                for kb in range(nkb):
                    w = min(512, km - 512 * kb)
                    vop(p, "tensor_reduce", (rm[:, kb:kb + 1], spcs[kb][:, 0:w], AX.X, ALU.max,
                                             False, False, False), rd=(spk[kb],), wr=("rmx",))
                vop(p, "tensor_reduce", (rm[:, 4:5], rm[:, 0:nkb], AX.X, ALU.max,
                                         False, False, False), rd=("rmx",), wr=("rmx",))
                vop(p, "tensor_scalar_mul", (sc["negm"], rm[:, 4:5], -1.0), rd=("rmx",), wr=("negm",))
                tr(p, PEb[0:1, 0:128], sc["negm"], sc["eyeb"], rd=("negm", "eyeb"), wr=("PEt",))
                vop(p, "tensor_copy", (sc["negmr"][0:1, 128 * sub:128 * (sub + 1)], PEb[0:1, 0:128]),
                    rd=("PEt",), wr=("negmr",))
            for kc in range(nkc):
                pT = sc["pT"][kc % 2]
                sTp = PB[:, 512:1024]
                mm(p, sTp, kh[:, 128 * kc:128 * (kc + 1)], qst[:, 512 * h:512 * (h + 1)],
                   start=True, stop=False, rd=("kh", "qst"), wr=("PB1",))
                mm(p, sTp, sc["ones1b"], sc["negmr"][0:1, :], start=False, stop=True,
                   rd=("negmr", "ones1b"), wr=("PB1",))
                if kc >= 4 * st:
                    m0 = 128 * kc - 512 * st
                    p.op("v", lambda eng, o=sTp[:, m0:m0 + 128], t=sc["tri_t"]:
                         eng.tensor_add(o, o, t), r=("PB1", "tri_t"), w=("PB1",))
                    if m0 > 0:
                        p.op("v", lambda eng, o=sTp[:, 0:m0]:
                             eng.tensor_scalar_add(o, o, NEG), r=("PB1",), w=("PB1",))
                else:
                    m0 = 0
                bias = sc["alc"][:, 16 * h + kc:16 * h + kc + 1] if alibi else 0.0
                act(p, pT, sTp, AF.Exp, rd=("PB1",) + (("alc",) if alibi else ()),
                    wr=(f"pT{kc % 2}",), bias=bias)
                mm(p, PC[:, 0:512], vh[:, 128 * kc:128 * (kc + 1)], pT,
                   start=(kc == 0), stop=(kc == nkc - 1), rd=("vh", f"pT{kc % 2}"), wr=("PC0",))
                mm(p, PC[0:1, 512:1024], sc["onescb"], pT,
                   start=(kc == 0), stop=(kc == nkc - 1), rd=(f"pT{kc % 2}", "onescb"), wr=("PC1",))
            vop(p, "tensor_copy", (sc["rsum"][0:1, :], PC[0:1, 512:1024]), rd=("PC1",), wr=("rsum",))
            vop(p, "reciprocal", (sc["rsum"][0:1, :], sc["rsum"][0:1, :]), rd=("rsum",), wr=("rsum",))
            mm(p, PC[:, 512:1024], sc["onesf"], sc["rsum"][0:1, :], start=True, stop=True,
               rd=("rsum", "onesf"), wr=("PC1",))
            vop(p, "tensor_copy", (sc["invb"], PC[:, 512:1024]), rd=("PC1",), wr=("invb",))
            p.op("v", lambda eng, o=sc["attnb"][:, 512 * h:512 * (h + 1)], i=PC[:, 0:512],
                 b=sc["invb"]: eng.tensor_mul(o, i, b), r=("PC0", "invb"), w=("attnb",))
        for nt in range(4):
            wb = 0
            wo = sc["wosl"][wb]
            dma(p, "g", wo, cfg["woi"][nt], f"wo{wb}", rd=(), wr=(f"wosl{wb}",))
            for ts in range(4):
                tc = 4 * st + ts
                py = PB[:, 512:1024]
                has_bo = cfg.get("bor") is not None
                for hc in range(NCH):
                    mm(p, py, sc["attnb"][:, 512 * hc + 128 * ts: 512 * hc + 128 * (ts + 1)],
                       wo[:, 512 * hc:512 * (hc + 1)], start=(hc == 0),
                       stop=(hc == NCH - 1 and not has_bo),
                       rd=("attnb", f"wosl{wb}"), wr=("PB1",))
                if has_bo:
                    mm(p, py, sc["onesf"], cfg["bor"][0:1, 512 * nt:512 * (nt + 1)],
                       start=False, stop=True, rd=("bor", "onesf"), wr=("PB1",))
                xs = C["x"][tc][:, 512 * nt:512 * (nt + 1)]
                p.op("v", lambda eng, o=xs, i=py: eng.tensor_add(o, i, o),
                     r=("PB1", f"x{tc}"), w=(f"x{tc}",))



def _psum(nc, ctx, last_bf16=None):
    return {
        "PA": ctx.enter_context(nc.psum_tensor("PA", [128, 1536], F32))[:],
        "PB": ctx.enter_context(nc.psum_tensor("PB", [128, 1024], F32))[:],
        "PC": ctx.enter_context(nc.psum_tensor("PC", [128, 1024], F32))[:],
    }


def _sb(nc, ctx, name, shape, dt):
    return ctx.enter_context(nc.sbuf_tensor(name, shape, dt))[:]


def _x_tiles(nc, ctx):
    return [_sb(nc, ctx, f"x{c}", [128, T], F32) for c in range(NCH)]


def _ln_scope(nc, ctx, need_lo, pfx=""):
    sc = {}
    sc["lnstat"] = _sb(nc, ctx, pfx + "lnstat", [128, 24], F32)
    sc["lnstat2"] = _sb(nc, ctx, pfx + "lnstat2", [128, 24], F32)
    sc["lnmv"] = _sb(nc, ctx, pfx + "lnmv", [128, 32], F32)
    sc["lnv"] = _sb(nc, ctx, pfx + "lnv", [128, 16], F32)
    sc["lnr"] = _sb(nc, ctx, pfx + "lnr", [128, 16], F32)
    sc["lnnmr"] = _sb(nc, ctx, pfx + "lnnmr", [128, 16], F32)
    sc["lnt1"] = _sb(nc, ctx, pfx + "lnt1", [128, 16], F32)
    sc["lnxf"] = [_sb(nc, ctx, pfx + f"lnxf{k}", [128, T], F32) for k in range(2)]
    sc["lnhi"] = [_sb(nc, ctx, pfx + f"lnhi{k}", [128, T], BF16) for k in range(2)]
    if need_lo:
        sc["lnlo"] = [_sb(nc, ctx, pfx + f"lnlo{k}", [128, T], BF16) for k in range(2)]
    return sc


def _ffn_sublayer(nc, p, C, layer, aps, pfx):
    with ExitStack() as lctx:
        lsc = _ln_scope(nc, lctx, need_lo=True, pfx=pfx)
        emit_ln(p, C, lsc, need_lo=True)
        p.barrier()

    with ExitStack() as fctx:
        C["PD"] = fctx.enter_context(nc.psum_tensor(pfx + "PD", [128, 512], F32))[:]
        sc = {}
        sc["eyef"] = _sb(nc, fctx, pfx + "eyefs", [128, 128], F32)
        dma(p, "y", sc["eyef"], aps["eyef"], "c_eyef", rd=(), wr=("eyef",))
        sc["onesf"] = _sb(nc, fctx, pfx + "onesf", [1, 128], F32)
        vop(p, "memset", (sc["onesf"], 1.0), rd=(), wr=("onesf",))
        for n in ("s0c", "s1c", "e0c", "e1c", "cm0", "cm1"):
            sc[n] = _sb(nc, fctx, pfx + n, [128, 16], F32)
        if layer == 0:
            sc["sigr"] = _sb(nc, fctx, pfx + "sigr", [1, T], F32)
            sc["sigb"] = _sb(nc, fctx, pfx + "sigb", [128, T], F32)
            sc["sigc"] = _sb(nc, fctx, pfx + "sigc", [128, 16], F32)
        else:
            sc["chat"] = _sb(nc, fctx, pfx + "chat", [128, 16], F32)
        kind = "sigmoid" if layer == 0 else "softmax"
        with ExitStack() as rctx:
            sc["gsb"] = _sb(nc, rctx, pfx + "gsb", [128, 128], F32)
            dma(p, "y", sc["gsb"], aps["gimg"], "c_gsb", rd=(), wr=("gsb",))
            sc["ghi"] = _sb(nc, rctx, pfx + "ghi", [128, 128], BF16)
            sc["glo"] = _sb(nc, rctx, pfx + "glo", [128, 128], BF16)
            sc["lgt"] = _sb(nc, rctx, pfx + "lgt", [8, 512], F32)
            sc["lgc"] = _sb(nc, rctx, pfx + "lgc", [128, 8], F32)
            sc["sco"] = _sb(nc, rctx, pfx + "sco", [128, 8], F32)
            sc["mx8"] = _sb(nc, rctx, pfx + "mx8", [128, 8], F32)
            sc["idx8"] = _sb(nc, rctx, pfx + "idx8", [128, 8], U32)
            sc["rm1"] = _sb(nc, rctx, pfx + "rm1", [128, 1], F32)
            sc["rs1"] = _sb(nc, rctx, pfx + "rs1", [128, 1], F32)
            sc["ri1"] = _sb(nc, rctx, pfx + "ri1", [128, 1], F32)
            sc["rsl_hi"] = _sb(nc, rctx, pfx + "rslhi", [128, 16 * 512], BF16)
            sc["rsl_lo"] = _sb(nc, rctx, pfx + "rsllo", [128, 16 * 512], BF16)
            emit_route(p, C, sc, kind)
            p.barrier()

        if layer == 0:
            tb, maxgw = 2048, 256
            hmax = 1024
            groups = [(0, 256), (256, 256), (512, 256), (768, 256)]
            experts = [dict(w1s=(lambda gi, e=e: aps["w1i"][e, gi]),
                            w3s=(lambda gi, e=e: aps["w3i"][e, gi]),
                            w2s=(lambda dt, e=e: aps["w2i"][e, dt]),
                            H=1024, groups=groups,
                            **({"sigma": e} if e < 8 else {})) for e in range(9)]
        else:
            tb, maxgw = 1024, 512
            hmax = 1408
            g14 = [(0, 512), (512, 512), (1024, 384)]
            experts = [dict(w1s=(lambda gi, e=e: aps["wgA"][e, gi] if gi < 2 else aps["wgB"][e]),
                            w3s=(lambda gi, e=e: aps["wuA"][e, gi] if gi < 2 else aps["wuB"][e]),
                            w2s=(lambda dt, e=e: aps["wdi"][e, dt]),
                            H=1408, groups=g14,
                            **({"comb": e} if e < 8 else {})) for e in range(10)]
        nhmax = hmax // 128
        sc["wsl1"] = _sb(nc, fctx, pfx + "wsl1", [128, 16 * maxgw], BF16)
        sc["wsl3"] = _sb(nc, fctx, pfx + "wsl3", [128, 16 * maxgw], BF16)
        sc["wsl2"] = _sb(nc, fctx, pfx + "wsl2", [128, nhmax * 512], BF16)
        sc["xbuf"] = [_sb(nc, fctx, pfx + f"xbuf{k}", [128, 512], BF16) for k in range(2)]
        sc["hsum"] = _sb(nc, fctx, pfx + "hsum", [128, nhmax * tb], BF16)
        sc["zt1"] = _sb(nc, fctx, pfx + "zt1", [128, 512], F32)
        sc["zt2"] = _sb(nc, fctx, pfx + "zt2", [128, 512], F32)
        cfg = dict(kind=kind, tb=tb, experts=experts)
        emit_ffn(p, C, sc, cfg)
        p.barrier()


def _attn_sublayer(nc, p, C, layer, aps, pfx):
    with ExitStack() as lctx:
        lsc = _ln_scope(nc, lctx, need_lo=False, pfx=pfx)
        emit_ln(p, C, lsc, need_lo=False)
        p.barrier()

    cfg = dict(alibi=(layer == 0), wqi=aps["wqi"], wki=aps["wki"],
               wvi=aps["wvi"], woi=aps["woi"])
    if layer == 0:
        C["alibihi"] = aps["alibihi"]
    with ExitStack() as actx:
        C["PEb"] = actx.enter_context(nc.psum_tensor(pfx + "PEb", [128, 1024], BF16))[:]
        sc = {}
        sc["onesf"] = _sb(nc, actx, pfx + "onesf", [1, 128], F32)
        vop(p, "memset", (sc["onesf"], 1.0), rd=(), wr=("onesf",))
        sc["ones1b"] = _sb(nc, actx, pfx + "ones1b", [1, 128], BF16)
        vop(p, "memset", (sc["ones1b"], 1.0), rd=(), wr=("ones1b",))
        sc["onescb"] = _sb(nc, actx, pfx + "onescb", [128, 1], BF16)
        vop(p, "memset", (sc["onescb"], 1.0), rd=(), wr=("onescb",))
        sc["tri_s"] = _sb(nc, actx, pfx + "tri_ss", [128, 128], F32)
        dma(p, "y", sc["tri_s"], aps["tri_s"], "c_tris", rd=(), wr=("tri_s",))
        sc["tri_t"] = _sb(nc, actx, pfx + "tri_ts", [128, 128], F32)
        dma(p, "y", sc["tri_t"], aps["tri_t"], "c_trit", rd=(), wr=("tri_t",))
        sc["eyeb"] = _sb(nc, actx, pfx + "eyebs", [128, 128], BF16)
        dma(p, "y", sc["eyeb"], aps["eyeb"], "c_eyeb", rd=(), wr=("eyeb",))
        if layer == 0:
            sc["bqs"] = _sb(nc, actx, pfx + "bqs", [128, 16], F32)
            dma(p, "y", sc["bqs"], aps["bqi"], "c_bq", rd=(), wr=("bqs0",))
            act(p, sc["bqs"], sc["bqs"], AF.Identity, rd=("bqs0",), wr=("bqs",), scale=RSQ)
            sc["bkc"] = _sb(nc, actx, pfx + "bkc", [128, 16], F32)
            dma(p, "y", sc["bkc"], aps["bki"], "c_bk", rd=(), wr=("bkc",))
            sc["alc"] = _sb(nc, actx, pfx + "alcs", [128, 256], F32)
            dma(p, "y", sc["alc"], aps["alc"], "c_alc", rd=(), wr=("alc",))
            sc["alr"] = _sb(nc, actx, pfx + "alrs", [1, T], BF16)
            cfg["bqs"], cfg["bkc"] = sc["bqs"], sc["bkc"]
        with ExitStack() as pactx:
            if layer == 0:
                sc["bvr"] = _sb(nc, pactx, pfx + "bvrs", [1, D], F32)
                dma(p, "y", sc["bvr"], aps["bvr"], "c_bv", rd=(), wr=("bvr",))
                cfg["bvr"] = sc["bvr"]
            sc["xnslab"] = _sb(nc, pactx, pfx + "xnslab", [128, 16 * 1024], BF16)
            sc["wpj"] = [_sb(nc, pactx, pfx + f"wpj{k}", [128, 16 * 128], BF16) for k in range(2)]
            sc["wvsl"] = [_sb(nc, pactx, pfx + "wvsl0", [128, 16 * 512], BF16)] * 2
            sc["qkstg"] = [_sb(nc, pactx, pfx + f"qkstg{k}", [128, 512], BF16) for k in range(2)]
            emit_attn_phaseA(p, C, sc, cfg)
            p.barrier()
        if layer == 0:
            sc["bor"] = _sb(nc, actx, pfx + "bors", [1, D], F32)
            dma(p, "y", sc["bor"], aps["bor"], "c_bo", rd=(), wr=("bor",))
            cfg["bor"] = sc["bor"]
        sc["qst"] = _sb(nc, actx, pfx + "qst", [128, 16 * 512], BF16)
        sc["kh"] = _sb(nc, actx, pfx + "kh", [128, T], BF16)
        sc["vh"] = _sb(nc, actx, pfx + "vh", [128, T], BF16)
        sc["pT"] = [_sb(nc, actx, pfx + f"pT{k}", [128, 512], BF16) for k in range(2)]
        sc["attnb"] = _sb(nc, actx, pfx + "attnb", [128, 16 * 512], BF16)
        sc["invb"] = _sb(nc, actx, pfx + "invb", [128, 512], F32)
        sc["rmx"] = _sb(nc, actx, pfx + "rmx", [128, 5], F32)
        sc["negm"] = _sb(nc, actx, pfx + "negm", [128, 1], BF16)
        sc["negmr"] = _sb(nc, actx, pfx + "negmr", [1, 512], BF16)
        sc["rsum"] = _sb(nc, actx, pfx + "rsum", [1, 512], F32)
        sc["wosl"] = [_sb(nc, actx, pfx + "wosl0", [128, 16 * 512], BF16)] * 2
        emit_attn_phaseB(p, C, sc, cfg)
        p.barrier()


def _decl_ffn_drams(nc, layer):
    if layer == 0:
        return dict(
            gimg=nc.dram_tensor("g0img", [128, 128], F32, kind="ExternalInput").ap(),
            eyef=nc.dram_tensor("eyef", [128, 128], F32, kind="ExternalInput").ap(),
            w1i=nc.dram_tensor("w1i", [9, 4, 128, 4096], F32, kind="ExternalInput").ap(),
            w3i=nc.dram_tensor("w3i", [9, 4, 128, 4096], F32, kind="ExternalInput").ap(),
            w2i=nc.dram_tensor("w2i", [9, 4, 128, 4096], F32, kind="ExternalInput").ap(),
        )
    return dict(
        gimg=nc.dram_tensor("g1img", [128, 128], F32, kind="ExternalInput").ap(),
        eyef=None,
        wgA=nc.dram_tensor("wgA", [10, 2, 128, 8192], F32, kind="ExternalInput").ap(),
        wgB=nc.dram_tensor("wgB", [10, 128, 6144], F32, kind="ExternalInput").ap(),
        wuA=nc.dram_tensor("wuA", [10, 2, 128, 8192], F32, kind="ExternalInput").ap(),
        wuB=nc.dram_tensor("wuB", [10, 128, 6144], F32, kind="ExternalInput").ap(),
        wdi=nc.dram_tensor("wdi", [10, 4, 128, 5632], F32, kind="ExternalInput").ap(),
    )


def _decl_attn_drams(nc, layer, shared):
    pre = "a%d_" % layer
    d = dict(
        wqi=nc.dram_tensor(pre + "wq", [16, 128, 2048], F32, kind="ExternalInput").ap(),
        wki=nc.dram_tensor(pre + "wk", [16, 128, 2048], F32, kind="ExternalInput").ap(),
        wvi=nc.dram_tensor(pre + "wv", [4, 128, 8192], F32, kind="ExternalInput").ap(),
        woi=nc.dram_tensor(pre + "wo", [4, 128, 8192], F32, kind="ExternalInput").ap(),
        tri_s=shared["tri_s"], tri_t=shared["tri_t"], eyeb=shared["eyeb"],
    )
    if layer == 0:
        d.update(
            bqi=nc.dram_tensor("bqi", [128, 16], F32, kind="ExternalInput").ap(),
            bki=nc.dram_tensor("bki", [128, 16], F32, kind="ExternalInput").ap(),
            bvr=nc.dram_tensor("bvr", [1, D], F32, kind="ExternalInput").ap(),
            bor=nc.dram_tensor("bor", [1, D], F32, kind="ExternalInput").ap(),
            alibihi=nc.dram_tensor("alibihi", [16, T], BF16, kind="ExternalInput").ap(),
            alc=nc.dram_tensor("alc", [128, 256], F32, kind="ExternalInput").ap(),
        )
    return d


def _decl_shared(nc):
    return dict(
        tri_s=nc.dram_tensor("tri_s", [128, 128], F32, kind="ExternalInput").ap(),
        tri_t=nc.dram_tensor("tri_t", [128, 128], F32, kind="ExternalInput").ap(),
        eyeb=nc.dram_tensor("eyeb", [128, 128], BF16, kind="ExternalInput").ap(),
    )


def _scratch(nc):
    return dict(
        xnh=nc.dram_tensor("xnh", [T, D], BF16, kind="Internal").ap(),
        xnl=nc.dram_tensor("xnl", [T, D], BF16, kind="Internal").ap(),
        qd=nc.dram_tensor("qd", [16, 128, T], BF16, kind="Internal").ap(),
        kd=nc.dram_tensor("kd", [16, 128, T], BF16, kind="Internal").ap(),
        vd=nc.dram_tensor("vd", [T, D], BF16, kind="Internal").ap(),
    )


def build_fused():
    nc = bass.Bass("TRN2", target_bir_lowering=False)
    xin = nc.dram_tensor("xin", [T, D], F32, kind="ExternalInput").ap()
    xout = nc.dram_tensor("xout", [T, D], F32, kind="ExternalOutput").ap()
    f0 = _decl_ffn_drams(nc, 0)
    f1 = _decl_ffn_drams(nc, 1)
    f1["eyef"] = f0["eyef"]
    shared = _decl_shared(nc)
    a0 = _decl_attn_drams(nc, 0, shared)
    a1 = _decl_attn_drams(nc, 1, shared)
    scr = _scratch(nc)
    with ExitStack() as ctx:
        C = dict(_psum(nc, ctx, last_bf16=None))
        C["x"] = _x_tiles(nc, ctx)
        C["xin"], C["xout"] = xin, xout
        C.update(scr)
        p = Prog(nc, ctx)
        emit_load_x(p, C)
        _ffn_sublayer(nc, p, C, 0, f0, "s0")
        _attn_sublayer(nc, p, C, 0, a0, "s1")
        _ffn_sublayer(nc, p, C, 0, f0, "s2")
        _ffn_sublayer(nc, p, C, 1, f1, "s3")
        _attn_sublayer(nc, p, C, 1, a1, "s4")
        _ffn_sublayer(nc, p, C, 1, f1, "s5")
        emit_store_x(p, C)
        p.materialize()
    return nc


def build_ffn_program(layer):
    nc = bass.Bass("TRN2", target_bir_lowering=False)
    xin = nc.dram_tensor("xin", [T, D], F32, kind="ExternalInput").ap()
    xout = nc.dram_tensor("xout", [T, D], F32, kind="ExternalOutput").ap()
    aps = _decl_ffn_drams(nc, layer)
    if layer == 1:
        aps["eyef"] = nc.dram_tensor("eyef", [128, 128], F32, kind="ExternalInput").ap()
    scr = _scratch(nc)
    with ExitStack() as ctx:
        C = dict(_psum(nc, ctx, last_bf16=None))
        C["x"] = _x_tiles(nc, ctx)
        C["xin"], C["xout"] = xin, xout
        C.update(scr)
        p = Prog(nc, ctx)
        emit_load_x(p, C)
        _ffn_sublayer(nc, p, C, layer, aps, "s0")
        emit_store_x(p, C)
        p.materialize()
    return nc


def build_attn_program(layer):
    nc = bass.Bass("TRN2", target_bir_lowering=False)
    xin = nc.dram_tensor("xin", [T, D], F32, kind="ExternalInput").ap()
    xout = nc.dram_tensor("xout", [T, D], F32, kind="ExternalOutput").ap()
    shared = _decl_shared(nc)
    aps = _decl_attn_drams(nc, layer, shared)
    scr = _scratch(nc)
    with ExitStack() as ctx:
        C = dict(_psum(nc, ctx, last_bf16=None))
        C["x"] = _x_tiles(nc, ctx)
        C["xin"], C["xout"] = xin, xout
        C.update(scr)
        p = Prog(nc, ctx)
        emit_load_x(p, C)
        _attn_sublayer(nc, p, C, layer, aps, "s0")
        emit_store_x(p, C)
        p.materialize()
    return nc



def _consts():
    r = np.arange(128)
    eyef = np.eye(128, dtype=np.float32)
    eyeb = np.eye(128).astype(BF)
    tri_s = np.where(r[None, :] > r[:, None], NEG, 0.0).astype(np.float32)
    tri_t = np.where(r[None, :] < r[:, None], NEG, 0.0).astype(np.float32)
    start = 2.0 ** (-2.0 ** (-(np.log2(16) - 3)))
    slopes = np.array([start * start ** i for i in range(16)], dtype=np.float32)
    kk = np.arange(T, dtype=np.float32)
    alibi = (-slopes[:, None] * (T - 1.0 - kk)[None, :]).astype(np.float32)
    alibihi = np.ascontiguousarray(alibi.astype(BF))
    alc = np.zeros((128, 256), dtype=np.float32)
    for h in range(16):
        for c in range(16):
            alc[:, 16 * h + c] = alibi[h, 128 * c:128 * (c + 1)]
    return dict(eyef=eyef, eyeb=eyeb, tri_s=tri_s, tri_t=tri_t,
                alibihi=alibihi, alc=alc)


def _img_ffn1(W):
    E, Dm, H0 = W.shape
    g = H0 // 256
    return np.ascontiguousarray(
        W.reshape(E, 16, 128, g, 256).transpose(0, 3, 2, 1, 4).reshape(E, g, 128, 16 * 256))


def _img_n512(W, nh):
    E = W.shape[0]
    return np.ascontiguousarray(
        W.reshape(E, nh, 128, 4, 512).transpose(0, 3, 2, 1, 4).reshape(E, 4, 128, nh * 512))


def _prep(inputs):
    cst = _consts()
    W1 = np.concatenate([inputs["L0_ffn_w1"].transpose(0, 2, 1),
                         inputs["L0_ffn_ws1"].T[None]], axis=0)
    W3 = np.concatenate([inputs["L0_ffn_w3"].transpose(0, 2, 1),
                         inputs["L0_ffn_ws3"].T[None]], axis=0)
    W2 = np.concatenate([inputs["L0_ffn_w2"].transpose(0, 2, 1),
                         inputs["L0_ffn_ws2"].T[None]], axis=0)
    WG = np.concatenate([inputs["L1_ffn_We_gate"],
                         inputs["L1_ffn_Ws_gate"][:, 0:1408][None],
                         inputs["L1_ffn_Ws_gate"][:, 1408:2816][None]], axis=0)
    WU = np.concatenate([inputs["L1_ffn_We_up"],
                         inputs["L1_ffn_Ws_up"][:, 0:1408][None],
                         inputs["L1_ffn_Ws_up"][:, 1408:2816][None]], axis=0)
    WDn = np.concatenate([inputs["L1_ffn_We_down"],
                          inputs["L1_ffn_Ws_down"][0:1408][None],
                          inputs["L1_ffn_Ws_down"][1408:2816][None]], axis=0)

    def imgA(W):
        return np.ascontiguousarray(
            W[:, :, 0:1024].reshape(10, 16, 128, 2, 512).transpose(0, 3, 2, 1, 4)
            .reshape(10, 2, 128, 16 * 512))

    def imgB(W):
        return np.ascontiguousarray(
            W[:, :, 1024:1408].reshape(10, 16, 128, 384).transpose(0, 2, 1, 3)
            .reshape(10, 128, 16 * 384))

    def aimg128(W):
        return np.ascontiguousarray(
            W.reshape(16, 128, 16, 128).transpose(2, 1, 0, 3).reshape(16, 128, 2048))

    def aimg512(W):
        return np.ascontiguousarray(
            W.reshape(16, 128, 4, 512).transpose(2, 1, 0, 3).reshape(4, 128, 16 * 512))

    h = dict(
        eyef=cst["eyef"], eyeb=cst["eyeb"], tri_s=cst["tri_s"], tri_t=cst["tri_t"],
        alibihi=cst["alibihi"], alc=cst["alc"],
        g0img=np.ascontiguousarray(
            inputs["L0_ffn_gate_w"].T.reshape(16, 128, 8).transpose(1, 0, 2).reshape(128, 128)),
        w1i=_img_ffn1(W1), w3i=_img_ffn1(W3), w2i=_img_n512(W2, 8),
        a0_wq=aimg128(inputs["L0_attn_Wq"]), a0_wk=aimg128(inputs["L0_attn_Wk"]),
        a0_wv=aimg512(inputs["L0_attn_Wv"]), a0_wo=aimg512(inputs["L0_attn_Wo"]),
        bqi=np.ascontiguousarray(inputs["L0_attn_bq"].reshape(16, 128).T),
        bki=np.ascontiguousarray(inputs["L0_attn_bk"].reshape(16, 128).T),
        bvr=np.ascontiguousarray(inputs["L0_attn_bv"].reshape(1, D)),
        bor=np.ascontiguousarray(inputs["L0_attn_bo"].reshape(1, D)),
        g1img=np.ascontiguousarray(
            inputs["L1_ffn_W_router"].reshape(16, 128, 8).transpose(1, 0, 2).reshape(128, 128)),
        wgA=imgA(WG), wgB=imgB(WG), wuA=imgA(WU), wuB=imgB(WU), wdi=_img_n512(WDn, 11),
        a1_wq=aimg128(np.ascontiguousarray(inputs["L1_attn_Wq"].T)),
        a1_wk=aimg128(np.ascontiguousarray(inputs["L1_attn_Wk"].T)),
        a1_wv=aimg512(np.ascontiguousarray(inputs["L1_attn_Wv"].T)),
        a1_wo=aimg512(np.ascontiguousarray(inputs["L1_attn_Wo"].T)),
    )
    return h


_PKEYS = {
    "f0": ["g0img", "eyef", "w1i", "w3i", "w2i"],
    "f1": ["g1img", "eyef", "wgA", "wgB", "wuA", "wuB", "wdi"],
    "a0": ["a0_wq", "a0_wk", "a0_wv", "a0_wo", "tri_s", "tri_t", "eyeb",
           "bqi", "bki", "bvr", "bor", "alibihi", "alc"],
    "a1": ["a1_wq", "a1_wk", "a1_wv", "a1_wo", "tri_s", "tri_t", "eyeb"],
}

_PROGS = {}


def _get_prog(name):
    if name not in _PROGS:
        if name == "fused":
            _PROGS[name] = build_fused()
        elif name == "f0":
            _PROGS[name] = build_ffn_program(0)
        elif name == "f1":
            _PROGS[name] = build_ffn_program(1)
        elif name == "a0":
            _PROGS[name] = build_attn_program(0)
        elif name == "a1":
            _PROGS[name] = build_attn_program(1)
    return _PROGS[name]


def kernel(**inputs):
    h = _prep(inputs)
    nc = _get_prog("fused")
    in_maps = [dict(h, xin=np.ascontiguousarray(inputs["L0_attn_x"][b])) for b in range(8)]
    res = run_bass_kernel_spmd(nc, in_maps, core_ids=list(range(8)))
    return np.stack([res.results[c]["xout"] for c in range(8)], axis=0).astype(np.float32)


# version 05
# speedup vs baseline: 1.1395x; 1.1395x over previous
import numpy as np
import ml_dtypes
from contextlib import ExitStack

import concourse.bass as bass
import concourse.mybir as mybir
from concourse.bass_utils import run_bass_kernel_spmd

F32 = mybir.dt.float32
BF16 = mybir.dt.bfloat16
U32 = mybir.dt.uint32
AF = mybir.ActivationFunctionType
ALU = mybir.AluOpType
AX = mybir.AxisListType
BF = ml_dtypes.bfloat16

T = 2048
D = 2048
NCH = 16
NEG = -1.0e30
EPS = 1e-5
RSQ = 1.0 / float(np.sqrt(128.0))
SAME_ENGINE_SYNC = True


class Prog:
    ENG = {"t": "tensor", "v": "vector", "s": "scalar", "g": "gpsimd", "y": "sync"}

    def __init__(self, nc, ctx, same_engine_sync=SAME_ENGINE_SYNC):
        self.nc = nc
        self.ctx = ctx
        self.same = same_engine_sync
        self.ops = []
        self.esem = {e: ctx.enter_context(nc.semaphore("se_" + e)) for e in self.ENG}
        self.chsem = {}
        self.chn = {}
        self.wr = {}
        self.rd = {}
        self.last_eng = {}
        self.last_ch = {}

    def ch(self, name):
        if name not in self.chsem:
            self.chsem[name] = self.ctx.enter_context(self.nc.semaphore("ch_" + name))
            self.chn[name] = 0
        return name

    def op(self, eng, fn, r=(), w=(), ch=None, extra=()):
        idx = len(self.ops)
        deps = set(extra)
        for k in r:
            deps.update(self.wr.get(k, ()))
            self.rd.setdefault(k, []).append(idx)
        for k in w:
            rs = self.rd.get(k)
            if rs:
                deps.update(rs)
                deps.update(self.wr.get(k, ()))
                self.rd[k] = []
                self.wr[k] = [idx]
            else:
                self.wr.setdefault(k, []).append(idx)
        deps.discard(idx)
        chw = {}
        for dd in deps:
            dch = self.ops[dd]["ch"]
            if dch is not None:
                chw[dch] = 16 * self.chn[dch]
        rec = {"i": idx, "e": eng, "fn": fn, "d": sorted(deps), "ch": ch, "ci": None,
               "inc": None, "w8": (), "ec": 0, "chw": chw}
        if ch is not None:
            self.ch(ch)
            self.chn[ch] += 1
            rec["ci"] = self.chn[ch]
            self.last_ch[ch] = idx
        self.last_eng[eng] = idx
        self.ops.append(rec)
        return idx

    def barrier(self):
        prev = list(self.last_eng.values()) + list(self.last_ch.values())
        for e in self.ENG:
            self.op(e, lambda eng: eng.nop(), extra=prev)

    def materialize(self):
        dep_tgt = set()
        for rec in self.ops:
            dep_tgt.update(rec["d"])
        cnt = {e: 0 for e in self.ENG}
        for rec in self.ops:
            if rec["ch"] is None:
                if rec["i"] in dep_tgt:
                    cnt[rec["e"]] += 1
                    rec["inc"] = cnt[rec["e"]]
                rec["ec"] = cnt[rec["e"]]
        guar = {e: {} for e in self.ENG}
        snap = {}
        for rec in self.ops:
            e = rec["e"]
            g = guar[e]
            need = {}
            for d in rec["d"]:
                dep = self.ops[d]
                if dep["ch"] is not None:
                    src, val = ("D", dep["ch"]), rec["chw"][dep["ch"]]
                else:
                    if dep["e"] == e and (e == "t" or not self.same):
                        continue
                    src, val = ("E", dep["e"]), dep["inc"]
                if need.get(src, 0) < val:
                    need[src] = val
            waits = []
            for src, val in need.items():
                if g.get(src, 0) >= val:
                    continue
                waits.append((src, val))
                g[src] = val
            if waits:
                for d in rec["d"]:
                    for s2, v2 in snap.get(d, {}).items():
                        if g.get(s2, 0) < v2:
                            g[s2] = v2
            rec["w8"] = waits
            if rec["i"] in dep_tgt:
                gg = dict(g)
                if rec["ch"] is None:
                    gg[("E", e)] = rec["ec"]
                else:
                    gg[("D", rec["ch"])] = 16 * rec["ci"]
                snap[rec["i"]] = gg

        per_eng = {e: [rec for rec in self.ops if rec["e"] == e] for e in self.ENG}
        final_ch = dict(self.chn)
        blk = self.ctx.enter_context(self.nc.Block())
        esem, chsem = self.esem, self.chsem

        def mk(e, recs):
            def body(eng):
                for rec in recs:
                    w8 = rec["w8"]
                    for (src, val) in w8[1:]:
                        sem = chsem[src[1]] if src[0] == "D" else esem[src[1]]
                        eng.wait_ge(sem, val)
                    inst = rec["fn"](eng)
                    if w8:
                        src, val = w8[0]
                        sem = chsem[src[1]] if src[0] == "D" else esem[src[1]]
                        inst._wait_ge(sem, val)
                    if rec["ch"] is not None:
                        inst.then_inc(chsem[rec["ch"]], 16)
                    elif rec["inc"] is not None:
                        inst.then_inc(esem[rec["e"]], 1)
                if e == "y":
                    for c, n in final_ch.items():
                        if n:
                            eng.wait_ge(chsem[c], 16 * n)
            return body

        for e, attr in self.ENG.items():
            getattr(blk, attr)(mk(e, per_eng[e]))



def mm(p, out, l, r, start, stop, rd, wr):
    p.op("t", lambda eng, o=out, ll=l, rr=r, a=start, b=stop:
         eng.matmul(o, ll, rr, start=a, stop=b), r=rd, w=wr)


def tr(p, out, in_, ident, rd, wr):
    p.op("t", lambda eng, o=out, i=in_, e=ident: eng.transpose(o, i, e), r=rd, w=wr)


def act(p, out, in_, func, rd, wr, bias=0.0, scale=1.0, accum=None):
    def f(eng, o=out, i=in_, fu=func, b=bias, s=scale, a=accum):
        if a is None:
            return eng.activation(out=o, in_=i, func=fu, bias=b, scale=s)
        return eng.activation(out=o, in_=i, func=fu, bias=b, scale=s, accum_out=a)
    p.op("s", f, r=rd, w=wr)


def vop(p, meth, args, rd, wr):
    p.op("v", lambda eng, m=meth, a=args: getattr(eng, m)(*a), r=rd, w=wr)


def img3(sb2d, dr2d, width):
    o = sb2d.rearrange("p (c h) -> p c h", h=width)
    i = dr2d.rearrange("(c p) h -> p c h", p=128)
    return o, i


def dma(p, eng, out, in_, ch, rd, wr, transpose=False):
    if transpose:
        p.op(eng, lambda e, o=out, i=in_: e.dma_start_transpose(out=o, in_=i),
             r=rd, w=wr, ch=ch)
    else:
        p.op(eng, lambda e, o=out, i=in_: e.dma_start(out=o, in_=i),
             r=rd, w=wr, ch=ch)



def emit_load_x(p, C):
    for c in range(NCH):
        dma(p, "y", C["x"][c], C["xin"][128 * c:128 * (c + 1), :], "xin",
            rd=(), wr=(f"x{c}",))


def emit_store_x(p, C):
    for c in range(NCH):
        dma(p, "y", C["xout"][128 * c:128 * (c + 1), :], C["x"][c], "xout",
            rd=(f"x{c}",), wr=())


def emit_ln(p, C, sc, need_lo):
    PEL = sc["PEL"]
    dma(p, "y", sc["eyebl"], C["eyeb_d"], "c_eyebl", rd=(), wr=("eyebl",))
    stats, mv = sc["lnstat"], sc["lnmv"]
    vv, rbuf, nmr, t1 = sc["lnv"], sc["lnr"], sc["lnnmr"], sc["lnt1"]
    for c in range(NCH):
        sk = f"lnst{c % 2}"
        st = sc["lnstat"] if c % 2 == 0 else sc["lnstat2"]
        for j in range(4):
            vop(p, "bn_stats", (st[:, 6 * j:6 * j + 6], C["x"][c][:, 512 * j:512 * (j + 1)]),
                rd=(f"x{c}",), wr=(sk,))
        vop(p, "bn_aggr", (mv[:, 2 * c:2 * c + 2], st), rd=(sk,), wr=("lnmv",))
    for c in range(NCH):
        vop(p, "tensor_scalar_add", (vv[:, c:c + 1], mv[:, 2 * c + 1:2 * c + 2], EPS),
            rd=("lnmv",), wr=("lnv",))
    act(p, t1, vv, AF.Sqrt, rd=("lnv",), wr=("lnt1",))
    vop(p, "reciprocal", (rbuf, t1), rd=("lnt1",), wr=("lnr",))
    vop(p, "tensor_mul", (t1, rbuf, rbuf), rd=("lnr",), wr=("lnt1",))
    vop(p, "tensor_mul", (t1, t1, vv), rd=("lnt1", "lnv"), wr=("lnt1",))
    vop(p, "tensor_scalar", (t1, t1, -0.5, 1.5, ALU.mult, ALU.add), rd=("lnt1",), wr=("lnt1",))
    vop(p, "tensor_mul", (rbuf, rbuf, t1), rd=("lnr", "lnt1"), wr=("lnr",))
    for c in range(NCH):
        vop(p, "scalar_tensor_tensor",
            (nmr[:, c:c + 1], mv[:, 2 * c:2 * c + 1], -1.0, rbuf[:, c:c + 1], ALU.mult, ALU.mult),
            rd=("lnmv", "lnr"), wr=("lnnmr",))
    for sup in range(4):
        for j in range(4):
            c = 4 * sup + j
            k = c % 2
            xf, hi = sc["lnxf"][k], sc["lnhi"][k]
            act(p, xf, C["x"][c], AF.Identity, rd=(f"x{c}", "lnr", "lnnmr"),
                wr=(f"lnxf{k}",), bias=nmr[:, c:c + 1], scale=rbuf[:, c:c + 1])
            vop(p, "tensor_copy", (hi, xf), rd=(f"lnxf{k}",), wr=(f"lnhi{k}",))
            if need_lo:
                lo = sc["lnlo"][k]
                vop(p, "tensor_sub", (lo, xf, hi), rd=(f"lnxf{k}", f"lnhi{k}"), wr=(f"lnlo{k}",))
            for dc in range(NCH):
                pe = PEL[dc % 2][:, 0:128]
                pk = f"PEL{dc % 2}"
                tr(p, pe, hi[:, 128 * dc:128 * (dc + 1)], sc["eyebl"],
                   rd=(f"lnhi{k}", "eyebl"), wr=(pk,))
                vop(p, "tensor_copy", (sc["stg_hi"][:, 512 * dc + 128 * j: 512 * dc + 128 * (j + 1)], pe),
                    rd=(pk,), wr=("stg_hi",))
                if need_lo:
                    pe2 = PEL[dc % 2][:, 128:256]
                    tr(p, pe2, sc["lnlo"][k][:, 128 * dc:128 * (dc + 1)], sc["eyebl"],
                       rd=(f"lnlo{k}", "eyebl"), wr=(pk,))
                    vop(p, "tensor_copy", (sc["stg_lo"][:, 512 * dc + 128 * j: 512 * dc + 128 * (j + 1)], pe2),
                        rd=(pk,), wr=("stg_lo",))
        for dc in range(NCH):
            dma(p, "y", C["xnf_hi"][dc][:, 512 * sup:512 * (sup + 1)],
                sc["stg_hi"][:, 512 * dc:512 * (dc + 1)], f"st_xh{sup % 2}",
                rd=("stg_hi",), wr=("xnh_d",))
            if need_lo:
                dma(p, "y", C["xnf_lo"][dc][:, 512 * sup:512 * (sup + 1)],
                    sc["stg_lo"][:, 512 * dc:512 * (dc + 1)], f"st_xl{sup % 2}",
                    rd=("stg_lo",), wr=("xnl_d",))


def emit_route(p, C, sc, kind):
    PC = C["PC"]
    ghi, glo = sc["ghi"], sc["glo"]
    vop(p, "tensor_copy", (ghi, sc["gsb"]), rd=("gsb",), wr=("ghi",))
    vop(p, "tensor_sub", (glo, sc["gsb"], ghi), rd=("gsb", "ghi"), wr=("glo",))
    lgt, lgc = sc["lgt"], sc["lgc"]
    hi5, lo5 = sc["rsl_hi"], sc["rsl_lo"]
    for t5 in range(4):
        t0 = 512 * t5
        for d in range(NCH):
            dma(p, "y", hi5[:, 512 * d:512 * (d + 1)],
                C["xnf_hi"][d][:, t0:t0 + 512], "r_hi",
                rd=("xnh_d",), wr=("rslh",))
            dma(p, "y", lo5[:, 512 * d:512 * (d + 1)],
                C["xnf_lo"][d][:, t0:t0 + 512], "r_lo",
                rd=("xnl_d",), wr=("rsll",))
        n3 = 3 * NCH
        k = 0
        for (gg, xx, gk, xk) in ((ghi, hi5, "ghi", "rslh"), (glo, hi5, "glo", "rslh"),
                                 (ghi, lo5, "ghi", "rsll")):
            for d in range(NCH):
                mm(p, PC[0:8, 0:512], gg[:, 8 * d:8 * d + 8], xx[:, 512 * d:512 * (d + 1)],
                   start=(k == 0), stop=(k == n3 - 1), rd=(gk, xk), wr=("PC0",))
                k += 1
        vop(p, "tensor_copy", (lgt, PC[0:8, 0:512]), rd=("PC0",), wr=("lgt",))
        for j in range(4):
            c = 4 * t5 + j
            tr(p, PC[:, 512:520], lgt[:, 128 * j:128 * (j + 1)], sc["eyef"][0:8, 0:8],
               rd=("lgt", "eyef"), wr=("PC1",))
            vop(p, "tensor_copy", (lgc, PC[:, 512:520]), rd=("PC1",), wr=("lgc",))
            mx8, idx = sc["mx8"], sc["idx8"]
            if kind == "sigmoid":
                sco = sc["sco"]
                act(p, sco, lgc, AF.Sigmoid, rd=("lgc",), wr=("sco",))
                vop(p, "max", (mx8, sco), rd=("sco",), wr=("mx8",))
                vop(p, "max_index", (idx, mx8, sco), rd=("sco", "mx8"), wr=("idx8",))
                vop(p, "tensor_copy", (sc["s0c"][:, c:c + 1], mx8[:, 0:1]), rd=("mx8",), wr=("s0c",))
                vop(p, "tensor_copy", (sc["s1c"][:, c:c + 1], mx8[:, 1:2]), rd=("mx8",), wr=("s1c",))
            else:
                mrow = sc["rm1"]
                vop(p, "tensor_reduce", (mrow, lgc, AX.X, ALU.max, False, False, True),
                    rd=("lgc",), wr=("rm1",))
                sco = sc["sco"]
                act(p, sco, lgc, AF.Exp, rd=("lgc", "rm1"), wr=("sco",), bias=mrow)
                vop(p, "tensor_reduce", (sc["rs1"], sco, AX.X, ALU.add, False, False, False),
                    rd=("sco",), wr=("rs1",))
                vop(p, "reciprocal", (sc["ri1"], sc["rs1"]), rd=("rs1",), wr=("ri1",))
                vop(p, "max", (mx8, sco), rd=("sco",), wr=("mx8",))
                vop(p, "max_index", (idx, mx8, sco), rd=("sco", "mx8"), wr=("idx8",))
                vop(p, "scalar_tensor_tensor",
                    (sc["s0c"][:, c:c + 1], mx8[:, 0:1], 0.5, sc["ri1"], ALU.mult, ALU.mult),
                    rd=("mx8", "ri1"), wr=("s0c",))
                vop(p, "scalar_tensor_tensor",
                    (sc["s1c"][:, c:c + 1], mx8[:, 1:2], 0.5, sc["ri1"], ALU.mult, ALU.mult),
                    rd=("mx8", "ri1"), wr=("s1c",))
            vop(p, "tensor_copy", (sc["e0c"][:, c:c + 1], idx[:, 0:1]), rd=("idx8",), wr=("e0c",))
            vop(p, "tensor_copy", (sc["e1c"][:, c:c + 1], idx[:, 1:2]), rd=("idx8",), wr=("e1c",))


def emit_ffn(p, C, sc, cfg):
    PA, PB, PC = C["PA"], C["PB"], C["PC"]
    tb = cfg["tb"]
    ntb = T // tb

    for pe, ex in enumerate(cfg["experts"]):
        H = ex["H"]
        nh = H // 128
        groups = ex["groups"]
        sig = ex.get("sigma")
        comb = ex.get("comb")
        if sig is not None:
            sgr = sc["sigr"]
            vop(p, "tensor_single_scalar", (sc["cm0"], sc["e0c"], float(sig), ALU.is_equal),
                rd=("e0c",), wr=("cm0",))
            vop(p, "tensor_mul", (sc["cm0"], sc["cm0"], sc["s0c"]), rd=("cm0", "s0c"), wr=("cm0",))
            vop(p, "tensor_single_scalar", (sc["cm1"], sc["e1c"], float(sig), ALU.is_equal),
                rd=("e1c",), wr=("cm1",))
            vop(p, "tensor_mul", (sc["cm1"], sc["cm1"], sc["s1c"]), rd=("cm1", "s1c"), wr=("cm1",))
            vop(p, "tensor_add", (sc["sigc"], sc["cm0"], sc["cm1"]), rd=("cm0", "cm1"), wr=("sigc",))
            for c in range(NCH):
                tr(p, PC[0:1, 512:640], sc["sigc"][:, c:c + 1], sc["eyef"],
                   rd=("sigc", "eyef"), wr=("PC1",))
                vop(p, "tensor_copy", (sgr[0:1, 128 * c:128 * (c + 1)], PC[0:1, 512:640]),
                    rd=("PC1",), wr=("sigr",))
            for t5 in range(4):
                mm(p, PC[:, 512:1024], sc["onesf"], sgr[0:1, 512 * t5:512 * (t5 + 1)],
                   start=True, stop=True, rd=("sigr", "onesf"), wr=("PC1",))
                vop(p, "tensor_copy", (sc["sigb"][:, 512 * t5:512 * (t5 + 1)], PC[:, 512:1024]),
                    rd=("PC1",), wr=("sigb",))
        if comb is not None:
            cht = sc["chat"]
            vop(p, "tensor_single_scalar", (sc["cm0"], sc["e0c"], float(comb), ALU.is_equal),
                rd=("e0c",), wr=("cm0",))
            vop(p, "tensor_mul", (sc["cm0"], sc["cm0"], sc["s0c"]), rd=("cm0", "s0c"), wr=("cm0",))
            vop(p, "tensor_single_scalar", (sc["cm1"], sc["e1c"], float(comb), ALU.is_equal),
                rd=("e1c",), wr=("cm1",))
            vop(p, "tensor_mul", (sc["cm1"], sc["cm1"], sc["s1c"]), rd=("cm1", "s1c"), wr=("cm1",))
            vop(p, "tensor_add", (cht, sc["cm0"], sc["cm1"]), rd=("cm0", "cm1"), wr=("chat",))

        pz1 = [PA[:, 0:512], PA[:, 512:1024], PA[:, 1024:1536], PB[:, 0:512]]
        pz3 = [PB[:, 512:1024], PC[:, 0:512], PC[:, 512:1024], C["PD"]]
        zk1 = ["PA0", "PA1", "PA2", "PB0"]
        zk3 = ["PB1", "PC0", "PC1", "PD0"]
        for tbi in range(ntb):
            for gi, (h0, gw) in enumerate(groups):
                nci = gw // 128
                w1s, w3s = sc["wsl1"], sc["wsl3"]
                dma(p, "g", w1s[:, 0:16 * gw], ex["w1s"](gi), "w1c", rd=(), wr=("w1s",))
                dma(p, "g", w3s[:, 0:16 * gw], ex["w3s"](gi), "w3c", rd=(), wr=("w3s",))
                for t5 in range(tb // 512):
                    t0 = tbi * tb + 512 * t5
                    for d in range(NCH):
                        xb = sc["xbuf"][d % 2]
                        dma(p, "y", xb, C["xnf_hi"][d][:, t0:t0 + 512],
                            f"xb{d % 2}", rd=("xnh_d",), wr=(f"xb{d % 2}",))
                        for ci in range(nci):
                            lsl = w1s[:, d * gw + 128 * ci: d * gw + 128 * (ci + 1)]
                            mm(p, pz1[ci], lsl, xb, start=(d == 0), stop=(d == NCH - 1),
                               rd=("w1s", f"xb{d % 2}"), wr=(zk1[ci],))
                            ls3 = w3s[:, d * gw + 128 * ci: d * gw + 128 * (ci + 1)]
                            mm(p, pz3[ci], ls3, xb, start=(d == 0), stop=(d == NCH - 1),
                               rd=("w3s", f"xb{d % 2}"), wr=(zk3[ci],))
                    for ci in range(nci):
                        hc = h0 // 128 + ci
                        hs = sc["hsum"][:, hc * tb + 512 * t5: hc * tb + 512 * (t5 + 1)]
                        if sig is not None:
                            sgs = sc["sigb"][:, t0:t0 + 512]
                            vop(p, "tensor_mul", (sc["zt1"], pz1[ci], sgs),
                                rd=(zk1[ci], "sigb"), wr=("zt1",))
                            act(p, sc["zt1"], sc["zt1"], AF.Silu, rd=("zt1",), wr=("zt1",))
                            vop(p, "tensor_mul", (sc["zt2"], pz3[ci], sgs),
                                rd=(zk3[ci], "sigb"), wr=("zt2",))
                            vop(p, "tensor_mul", (hs, sc["zt1"], sc["zt2"]),
                                rd=("zt1", "zt2"), wr=("hsum",))
                        else:
                            act(p, sc["zt1"], pz1[ci], AF.Silu, rd=(zk1[ci],), wr=("zt1",))
                            vop(p, "tensor_mul", (hs, sc["zt1"], pz3[ci]),
                                rd=("zt1", zk3[ci]), wr=("hsum",))
            for dt in range(4):
                w2s = sc["wsl2"]
                dma(p, "g", w2s[:, 0:nh * 512], ex["w2s"](dt), "w2c", rd=(), wr=("w2s",))
                for ts in range(tb // 128):
                    tc = (tbi * tb) // 128 + ts
                    py = PA[:, 0:512]
                    for hc in range(nh):
                        mm(p, py, sc["hsum"][:, hc * tb + 128 * ts: hc * tb + 128 * (ts + 1)],
                           w2s[:, 512 * hc:512 * (hc + 1)], start=(hc == 0), stop=(hc == nh - 1),
                           rd=("hsum", "w2s"), wr=("PA0",))
                    xs = C["x"][tc][:, 512 * dt:512 * (dt + 1)]
                    if comb is not None:
                        scalar = sc["chat"][:, tc:tc + 1]
                        p.op("v", lambda eng, o=xs, i=py, s=scalar:
                             eng.scalar_tensor_tensor(o, i, s, o, ALU.mult, ALU.add),
                             r=("PA0", f"x{tc}", "chat"), w=(f"x{tc}",))
                    else:
                        p.op("v", lambda eng, o=xs, i=py:
                             eng.scalar_tensor_tensor(o, i, 0.5, o, ALU.mult, ALU.add),
                             r=("PA0", f"x{tc}"), w=(f"x{tc}",))


def emit_attn_phaseA(p, C, sc, cfg):
    PA, PB = C["PA"], C["PB"]
    xsl = sc["xnslab"]
    pq = [PA[:, 0:512], PA[:, 512:1024], PA[:, 1024:1536], PB[:, 0:512]]
    pk = ["PA0", "PA1", "PA2", "PB0"]
    for half in range(2):
        toff = 1024 * half
        for d in range(NCH):
            dma(p, "y", xsl[:, 1024 * d:1024 * (d + 1)],
                C["xnf_hi"][d][:, toff:toff + 1024], "xnsl",
                rd=("xnh_d",), wr=("xnsl",))
        for proj, wap, bias, bk, scl, dst in (
                ("q", cfg["wqi"], cfg.get("bqs"), "bqs", RSQ, C["qd"]),
                ("k", cfg["wki"], cfg.get("bkc"), "bkc", 1.0, C["kd"])):
            for n in range(NCH):
                wb = n % 2
                ws = sc["wpj"][wb]
                dma(p, "g", ws, wap[n], f"wp{wb}", rd=(), wr=(f"wpj{wb}",))
                for t5 in range(2):
                    ps, psk = pq[t5], pk[t5]
                    for d in range(NCH):
                        mm(p, ps, ws[:, 128 * d:128 * (d + 1)],
                           xsl[:, 1024 * d + 512 * t5: 1024 * d + 512 * (t5 + 1)],
                           start=(d == 0), stop=(d == NCH - 1),
                           rd=(f"wpj{wb}", "xnsl"), wr=(psk,))
                    stg = sc["qkstg"][t5 % 2]
                    b = bias[:, n:n + 1] if bias is not None else 0.0
                    brd = (psk, bk) if bias is not None else (psk,)
                    act(p, stg, ps, AF.Identity, rd=brd, wr=(f"qkst{t5 % 2}",),
                        bias=b, scale=scl)
                    st_i = 2 * half + t5
                    if proj == "q":
                        dma(p, "y", C["qd"][st_i][:, n, :], stg,
                            f"qk{t5 % 2}", rd=(f"qkst{t5 % 2}",), wr=("q_d",))
                    else:
                        dma(p, "y", C["kd"][n][:, 512 * st_i:512 * (st_i + 1)], stg,
                            f"qk{t5 % 2}", rd=(f"qkst{t5 % 2}",), wr=("k_d",))
        for nt in range(4):
            wb = 0
            wv = sc["wvsl"][wb]
            dma(p, "g", wv, cfg["wvi"][nt], f"wv{wb}", rd=(), wr=(f"wvsl{wb}",))
            for ts in range(8):
                ps, psk = pq[ts % 4], pk[ts % 4]
                has_b = cfg.get("bvr") is not None
                for d in range(NCH):
                    mm(p, ps, xsl[:, 1024 * d + 128 * ts: 1024 * d + 128 * (ts + 1)],
                       wv[:, 512 * d:512 * (d + 1)], start=(d == 0),
                       stop=(d == NCH - 1 and not has_b),
                       rd=(f"wvsl{wb}", "xnsl"), wr=(psk,))
                if has_b:
                    mm(p, ps, sc["onesf"], cfg["bvr"][0:1, 512 * nt:512 * (nt + 1)],
                       start=False, stop=True, rd=("bvr", "onesf"), wr=(psk,))
                stg = sc["qkstg"][ts % 2]
                act(p, stg, ps, AF.Identity, rd=(psk,), wr=(f"qkst{ts % 2}",))
                tchunk = 8 * half + ts
                for hh in range(4):
                    dma(p, "y", C["vd"][4 * nt + hh][:, tchunk, :],
                        stg[:, 128 * hh:128 * (hh + 1)],
                        f"qk{ts % 2}", rd=(f"qkst{ts % 2}",), wr=("v_d",))


def emit_attn_phaseB(p, C, sc, cfg):
    PA, PB, PC, PEb = C["PA"], C["PB"], C["PC"], C["PEb"]
    alibi = cfg["alibi"]
    for st in range(4):
        qst = sc["qst"]
        dma(p, "y", qst, C["qd"][st].rearrange("p h t -> p (h t)"),
            "qstld", rd=("q_d",), wr=("qst",))
        for nt in range(4):
            pass
        for h in range(NCH):
            kmax = 512 * (st + 1)
            nkc = kmax // 128
            kh, vh = sc["kh"], sc["vh"]
            dma(p, "y", kh[:, 0:kmax], C["kd"][h][:, 0:kmax], "khld", rd=("k_d",), wr=("kh",))
            dma(p, "y", vh[:, 0:kmax].rearrange("p (c n) -> p c n", n=128),
                C["vd"][h][:, 0:kmax // 128, :],
                "vhld", rd=("v_d",), wr=("vh",))
            if alibi:
                dma(p, "y", sc["alr"][0:1, :], C["alibihi"][h:h + 1, :], "alr",
                    rd=(), wr=("alr",))
            spcs = [PA[:, 0:512], PA[:, 512:1024], PA[:, 1024:1536], PB[:, 0:512]]
            spk = ["PA0", "PA1", "PA2", "PB0"]
            for sub in range(4):
                q0 = 512 * st + 128 * sub
                km = q0 + 128
                nkb = (km + 511) // 512
                for kb in range(nkb):
                    w = min(512, km - 512 * kb)
                    mm(p, spcs[kb][:, 0:w], qst[:, 512 * h + 128 * sub: 512 * h + 128 * (sub + 1)],
                       kh[:, 512 * kb:512 * kb + w], start=True, stop=not alibi,
                       rd=("qst", "kh"), wr=(spk[kb],))
                    if alibi:
                        mm(p, spcs[kb][:, 0:w], sc["ones1b"],
                           sc["alr"][0:1, 512 * kb:512 * kb + w], start=False, stop=True,
                           rd=("alr", "ones1b"), wr=(spk[kb],))
                dkb = (km - 1) // 512
                off = q0 - 512 * dkb
                p.op("v", lambda eng, o=spcs[dkb][:, off:off + 128], t=sc["tri_s"]:
                     eng.tensor_add(o, o, t), r=(spk[dkb], "tri_s"), w=(spk[dkb],))
                rm = sc["rmx"]
                for kb in range(nkb):
                    w = min(512, km - 512 * kb)
                    vop(p, "tensor_reduce", (rm[:, kb:kb + 1], spcs[kb][:, 0:w], AX.X, ALU.max,
                                             False, False, False), rd=(spk[kb],), wr=("rmx",))
                vop(p, "tensor_reduce", (rm[:, 4:5], rm[:, 0:nkb], AX.X, ALU.max,
                                         False, False, False), rd=("rmx",), wr=("rmx",))
                vop(p, "tensor_scalar_mul", (sc["negm"], rm[:, 4:5], -1.0), rd=("rmx",), wr=("negm",))
                tr(p, PEb[0:1, 0:128], sc["negm"], sc["eyeb"], rd=("negm", "eyeb"), wr=("PEt",))
                vop(p, "tensor_copy", (sc["negmr"][0:1, 128 * sub:128 * (sub + 1)], PEb[0:1, 0:128]),
                    rd=("PEt",), wr=("negmr",))
            for kc in range(nkc):
                pT = sc["pT"][kc % 2]
                sTp = PB[:, 512:1024]
                mm(p, sTp, kh[:, 128 * kc:128 * (kc + 1)], qst[:, 512 * h:512 * (h + 1)],
                   start=True, stop=False, rd=("kh", "qst"), wr=("PB1",))
                mm(p, sTp, sc["ones1b"], sc["negmr"][0:1, :], start=False, stop=True,
                   rd=("negmr", "ones1b"), wr=("PB1",))
                if kc >= 4 * st:
                    m0 = 128 * kc - 512 * st
                    p.op("v", lambda eng, o=sTp[:, m0:m0 + 128], t=sc["tri_t"]:
                         eng.tensor_add(o, o, t), r=("PB1", "tri_t"), w=("PB1",))
                    if m0 > 0:
                        p.op("v", lambda eng, o=sTp[:, 0:m0]:
                             eng.tensor_scalar_add(o, o, NEG), r=("PB1",), w=("PB1",))
                else:
                    m0 = 0
                bias = sc["alc"][:, 16 * h + kc:16 * h + kc + 1] if alibi else 0.0
                act(p, pT, sTp, AF.Exp, rd=("PB1",) + (("alc",) if alibi else ()),
                    wr=(f"pT{kc % 2}",), bias=bias)
                mm(p, PC[:, 0:512], vh[:, 128 * kc:128 * (kc + 1)], pT,
                   start=(kc == 0), stop=(kc == nkc - 1), rd=("vh", f"pT{kc % 2}"), wr=("PC0",))
                mm(p, PC[0:1, 512:1024], sc["onescb"], pT,
                   start=(kc == 0), stop=(kc == nkc - 1), rd=(f"pT{kc % 2}", "onescb"), wr=("PC1",))
            vop(p, "tensor_copy", (sc["rsum"][0:1, :], PC[0:1, 512:1024]), rd=("PC1",), wr=("rsum",))
            vop(p, "reciprocal", (sc["rsum"][0:1, :], sc["rsum"][0:1, :]), rd=("rsum",), wr=("rsum",))
            mm(p, PC[:, 512:1024], sc["onesf"], sc["rsum"][0:1, :], start=True, stop=True,
               rd=("rsum", "onesf"), wr=("PC1",))
            vop(p, "tensor_copy", (sc["invb"], PC[:, 512:1024]), rd=("PC1",), wr=("invb",))
            p.op("v", lambda eng, o=sc["attnb"][:, 512 * h:512 * (h + 1)], i=PC[:, 0:512],
                 b=sc["invb"]: eng.tensor_mul(o, i, b), r=("PC0", "invb"), w=("attnb",))
        for nt in range(4):
            wb = 0
            wo = sc["wosl"][wb]
            dma(p, "g", wo, cfg["woi"][nt], f"wo{wb}", rd=(), wr=(f"wosl{wb}",))
            for ts in range(4):
                tc = 4 * st + ts
                py = PB[:, 512:1024]
                has_bo = cfg.get("bor") is not None
                for hc in range(NCH):
                    mm(p, py, sc["attnb"][:, 512 * hc + 128 * ts: 512 * hc + 128 * (ts + 1)],
                       wo[:, 512 * hc:512 * (hc + 1)], start=(hc == 0),
                       stop=(hc == NCH - 1 and not has_bo),
                       rd=("attnb", f"wosl{wb}"), wr=("PB1",))
                if has_bo:
                    mm(p, py, sc["onesf"], cfg["bor"][0:1, 512 * nt:512 * (nt + 1)],
                       start=False, stop=True, rd=("bor", "onesf"), wr=("PB1",))
                xs = C["x"][tc][:, 512 * nt:512 * (nt + 1)]
                p.op("v", lambda eng, o=xs, i=py: eng.tensor_add(o, i, o),
                     r=("PB1", f"x{tc}"), w=(f"x{tc}",))



def _psum(nc, ctx, pfx=""):
    return {
        "PA": ctx.enter_context(nc.psum_tensor(pfx + "PA", [128, 1536], F32))[:],
        "PB": ctx.enter_context(nc.psum_tensor(pfx + "PB", [128, 1024], F32))[:],
        "PC": ctx.enter_context(nc.psum_tensor(pfx + "PC", [128, 1024], F32))[:],
    }


def _sb(nc, ctx, name, shape, dt):
    return ctx.enter_context(nc.sbuf_tensor(name, shape, dt))[:]


def _x_tiles(nc, ctx):
    return [_sb(nc, ctx, f"x{c}", [128, T], F32) for c in range(NCH)]


def _ln_scope(nc, ctx, need_lo, pfx=""):
    sc = {}
    sc["PEL"] = [ctx.enter_context(nc.psum_tensor(pfx + f"PEL{k}", [128, 1024], BF16))[:]
                 for k in range(2)]
    sc["eyebl"] = _sb(nc, ctx, pfx + "eyebl", [128, 128], BF16)
    sc["stg_hi"] = _sb(nc, ctx, pfx + "stg_hi", [128, 16 * 512], BF16)
    if need_lo:
        sc["stg_lo"] = _sb(nc, ctx, pfx + "stg_lo", [128, 16 * 512], BF16)
    sc["lnstat"] = _sb(nc, ctx, pfx + "lnstat", [128, 24], F32)
    sc["lnstat2"] = _sb(nc, ctx, pfx + "lnstat2", [128, 24], F32)
    sc["lnmv"] = _sb(nc, ctx, pfx + "lnmv", [128, 32], F32)
    sc["lnv"] = _sb(nc, ctx, pfx + "lnv", [128, 16], F32)
    sc["lnr"] = _sb(nc, ctx, pfx + "lnr", [128, 16], F32)
    sc["lnnmr"] = _sb(nc, ctx, pfx + "lnnmr", [128, 16], F32)
    sc["lnt1"] = _sb(nc, ctx, pfx + "lnt1", [128, 16], F32)
    sc["lnxf"] = [_sb(nc, ctx, pfx + f"lnxf{k}", [128, T], F32) for k in range(2)]
    sc["lnhi"] = [_sb(nc, ctx, pfx + f"lnhi{k}", [128, T], BF16) for k in range(2)]
    if need_lo:
        sc["lnlo"] = [_sb(nc, ctx, pfx + f"lnlo{k}", [128, T], BF16) for k in range(2)]
    return sc


def _ffn_sublayer(nc, p, C, layer, aps, pfx):
    with ExitStack() as lctx:
        lsc = _ln_scope(nc, lctx, need_lo=True, pfx=pfx)
        emit_ln(p, C, lsc, need_lo=True)
        p.barrier()

    with ExitStack() as fctx:
        C.update(_psum(nc, fctx, pfx=pfx))
        C["PD"] = fctx.enter_context(nc.psum_tensor(pfx + "PD", [128, 512], F32))[:]
        sc = {}
        sc["eyef"] = _sb(nc, fctx, pfx + "eyefs", [128, 128], F32)
        dma(p, "y", sc["eyef"], aps["eyef"], "c_eyef", rd=(), wr=("eyef",))
        sc["onesf"] = _sb(nc, fctx, pfx + "onesf", [1, 128], F32)
        vop(p, "memset", (sc["onesf"], 1.0), rd=(), wr=("onesf",))
        for n in ("s0c", "s1c", "e0c", "e1c", "cm0", "cm1"):
            sc[n] = _sb(nc, fctx, pfx + n, [128, 16], F32)
        if layer == 0:
            sc["sigr"] = _sb(nc, fctx, pfx + "sigr", [1, T], F32)
            sc["sigb"] = _sb(nc, fctx, pfx + "sigb", [128, T], F32)
            sc["sigc"] = _sb(nc, fctx, pfx + "sigc", [128, 16], F32)
        else:
            sc["chat"] = _sb(nc, fctx, pfx + "chat", [128, 16], F32)
        kind = "sigmoid" if layer == 0 else "softmax"
        with ExitStack() as rctx:
            sc["gsb"] = _sb(nc, rctx, pfx + "gsb", [128, 128], F32)
            dma(p, "y", sc["gsb"], aps["gimg"], "c_gsb", rd=(), wr=("gsb",))
            sc["ghi"] = _sb(nc, rctx, pfx + "ghi", [128, 128], BF16)
            sc["glo"] = _sb(nc, rctx, pfx + "glo", [128, 128], BF16)
            sc["lgt"] = _sb(nc, rctx, pfx + "lgt", [8, 512], F32)
            sc["lgc"] = _sb(nc, rctx, pfx + "lgc", [128, 8], F32)
            sc["sco"] = _sb(nc, rctx, pfx + "sco", [128, 8], F32)
            sc["mx8"] = _sb(nc, rctx, pfx + "mx8", [128, 8], F32)
            sc["idx8"] = _sb(nc, rctx, pfx + "idx8", [128, 8], U32)
            sc["rm1"] = _sb(nc, rctx, pfx + "rm1", [128, 1], F32)
            sc["rs1"] = _sb(nc, rctx, pfx + "rs1", [128, 1], F32)
            sc["ri1"] = _sb(nc, rctx, pfx + "ri1", [128, 1], F32)
            sc["rsl_hi"] = _sb(nc, rctx, pfx + "rslhi", [128, 16 * 512], BF16)
            sc["rsl_lo"] = _sb(nc, rctx, pfx + "rsllo", [128, 16 * 512], BF16)
            emit_route(p, C, sc, kind)
            p.barrier()

        if layer == 0:
            tb, maxgw = 1024, 512
            hmax = 1024
            groups = [(0, 512), (512, 512)]
            experts = [dict(w1s=(lambda gi, e=e: aps["w1i"][e, gi]),
                            w3s=(lambda gi, e=e: aps["w3i"][e, gi]),
                            w2s=(lambda dt, e=e: aps["w2i"][e, dt]),
                            H=1024, groups=groups,
                            **({"sigma": e} if e < 8 else {})) for e in range(9)]
        else:
            tb, maxgw = 1024, 512
            hmax = 1408
            g14 = [(0, 512), (512, 512), (1024, 384)]
            experts = [dict(w1s=(lambda gi, e=e: aps["wgA"][e, gi] if gi < 2 else aps["wgB"][e]),
                            w3s=(lambda gi, e=e: aps["wuA"][e, gi] if gi < 2 else aps["wuB"][e]),
                            w2s=(lambda dt, e=e: aps["wdi"][e, dt]),
                            H=1408, groups=g14,
                            **({"comb": e} if e < 8 else {})) for e in range(10)]
        nhmax = hmax // 128
        sc["wsl1"] = _sb(nc, fctx, pfx + "wsl1", [128, 16 * maxgw], BF16)
        sc["wsl3"] = _sb(nc, fctx, pfx + "wsl3", [128, 16 * maxgw], BF16)
        sc["wsl2"] = _sb(nc, fctx, pfx + "wsl2", [128, nhmax * 512], BF16)
        sc["xbuf"] = [_sb(nc, fctx, pfx + f"xbuf{k}", [128, 512], BF16) for k in range(2)]
        sc["hsum"] = _sb(nc, fctx, pfx + "hsum", [128, nhmax * tb], BF16)
        sc["zt1"] = _sb(nc, fctx, pfx + "zt1", [128, 512], F32)
        sc["zt2"] = _sb(nc, fctx, pfx + "zt2", [128, 512], F32)
        cfg = dict(kind=kind, tb=tb, experts=experts)
        emit_ffn(p, C, sc, cfg)
        p.barrier()


def _attn_sublayer(nc, p, C, layer, aps, pfx):
    with ExitStack() as lctx:
        lsc = _ln_scope(nc, lctx, need_lo=False, pfx=pfx)
        emit_ln(p, C, lsc, need_lo=False)
        p.barrier()

    cfg = dict(alibi=(layer == 0), wqi=aps["wqi"], wki=aps["wki"],
               wvi=aps["wvi"], woi=aps["woi"])
    if layer == 0:
        C["alibihi"] = aps["alibihi"]
    with ExitStack() as actx:
        C.update(_psum(nc, actx, pfx=pfx))
        C["PEb"] = actx.enter_context(nc.psum_tensor(pfx + "PEb", [128, 1024], BF16))[:]
        sc = {}
        sc["onesf"] = _sb(nc, actx, pfx + "onesf", [1, 128], F32)
        vop(p, "memset", (sc["onesf"], 1.0), rd=(), wr=("onesf",))
        sc["ones1b"] = _sb(nc, actx, pfx + "ones1b", [1, 128], BF16)
        vop(p, "memset", (sc["ones1b"], 1.0), rd=(), wr=("ones1b",))
        sc["onescb"] = _sb(nc, actx, pfx + "onescb", [128, 1], BF16)
        vop(p, "memset", (sc["onescb"], 1.0), rd=(), wr=("onescb",))
        sc["tri_s"] = _sb(nc, actx, pfx + "tri_ss", [128, 128], F32)
        dma(p, "y", sc["tri_s"], aps["tri_s"], "c_tris", rd=(), wr=("tri_s",))
        sc["tri_t"] = _sb(nc, actx, pfx + "tri_ts", [128, 128], F32)
        dma(p, "y", sc["tri_t"], aps["tri_t"], "c_trit", rd=(), wr=("tri_t",))
        sc["eyeb"] = _sb(nc, actx, pfx + "eyebs", [128, 128], BF16)
        dma(p, "y", sc["eyeb"], aps["eyeb"], "c_eyeb", rd=(), wr=("eyeb",))
        if layer == 0:
            sc["bqs"] = _sb(nc, actx, pfx + "bqs", [128, 16], F32)
            dma(p, "y", sc["bqs"], aps["bqi"], "c_bq", rd=(), wr=("bqs0",))
            act(p, sc["bqs"], sc["bqs"], AF.Identity, rd=("bqs0",), wr=("bqs",), scale=RSQ)
            sc["bkc"] = _sb(nc, actx, pfx + "bkc", [128, 16], F32)
            dma(p, "y", sc["bkc"], aps["bki"], "c_bk", rd=(), wr=("bkc",))
            sc["alc"] = _sb(nc, actx, pfx + "alcs", [128, 256], F32)
            dma(p, "y", sc["alc"], aps["alc"], "c_alc", rd=(), wr=("alc",))
            sc["alr"] = _sb(nc, actx, pfx + "alrs", [1, T], BF16)
            cfg["bqs"], cfg["bkc"] = sc["bqs"], sc["bkc"]
        with ExitStack() as pactx:
            if layer == 0:
                sc["bvr"] = _sb(nc, pactx, pfx + "bvrs", [1, D], F32)
                dma(p, "y", sc["bvr"], aps["bvr"], "c_bv", rd=(), wr=("bvr",))
                cfg["bvr"] = sc["bvr"]
            sc["xnslab"] = _sb(nc, pactx, pfx + "xnslab", [128, 16 * 1024], BF16)
            sc["wpj"] = [_sb(nc, pactx, pfx + f"wpj{k}", [128, 16 * 128], BF16) for k in range(2)]
            sc["wvsl"] = [_sb(nc, pactx, pfx + "wvsl0", [128, 16 * 512], BF16)] * 2
            sc["qkstg"] = [_sb(nc, pactx, pfx + f"qkstg{k}", [128, 512], BF16) for k in range(2)]
            emit_attn_phaseA(p, C, sc, cfg)
            p.barrier()
        if layer == 0:
            sc["bor"] = _sb(nc, actx, pfx + "bors", [1, D], F32)
            dma(p, "y", sc["bor"], aps["bor"], "c_bo", rd=(), wr=("bor",))
            cfg["bor"] = sc["bor"]
        sc["qst"] = _sb(nc, actx, pfx + "qst", [128, 16 * 512], BF16)
        sc["kh"] = _sb(nc, actx, pfx + "kh", [128, T], BF16)
        sc["vh"] = _sb(nc, actx, pfx + "vh", [128, T], BF16)
        sc["pT"] = [_sb(nc, actx, pfx + f"pT{k}", [128, 512], BF16) for k in range(2)]
        sc["attnb"] = _sb(nc, actx, pfx + "attnb", [128, 16 * 512], BF16)
        sc["invb"] = _sb(nc, actx, pfx + "invb", [128, 512], F32)
        sc["rmx"] = _sb(nc, actx, pfx + "rmx", [128, 5], F32)
        sc["negm"] = _sb(nc, actx, pfx + "negm", [128, 1], BF16)
        sc["negmr"] = _sb(nc, actx, pfx + "negmr", [1, 512], BF16)
        sc["rsum"] = _sb(nc, actx, pfx + "rsum", [1, 512], F32)
        sc["wosl"] = [_sb(nc, actx, pfx + "wosl0", [128, 16 * 512], BF16)] * 2
        emit_attn_phaseB(p, C, sc, cfg)
        p.barrier()


def _decl_ffn_drams(nc, layer):
    if layer == 0:
        return dict(
            gimg=nc.dram_tensor("g0img", [128, 128], F32, kind="ExternalInput").ap(),
            eyef=nc.dram_tensor("eyef", [128, 128], F32, kind="ExternalInput").ap(),
            w1i=nc.dram_tensor("w1i", [9, 2, 128, 8192], F32, kind="ExternalInput").ap(),
            w3i=nc.dram_tensor("w3i", [9, 2, 128, 8192], F32, kind="ExternalInput").ap(),
            w2i=nc.dram_tensor("w2i", [9, 4, 128, 4096], F32, kind="ExternalInput").ap(),
        )
    return dict(
        gimg=nc.dram_tensor("g1img", [128, 128], F32, kind="ExternalInput").ap(),
        eyef=None,
        wgA=nc.dram_tensor("wgA", [10, 2, 128, 8192], F32, kind="ExternalInput").ap(),
        wgB=nc.dram_tensor("wgB", [10, 128, 6144], F32, kind="ExternalInput").ap(),
        wuA=nc.dram_tensor("wuA", [10, 2, 128, 8192], F32, kind="ExternalInput").ap(),
        wuB=nc.dram_tensor("wuB", [10, 128, 6144], F32, kind="ExternalInput").ap(),
        wdi=nc.dram_tensor("wdi", [10, 4, 128, 5632], F32, kind="ExternalInput").ap(),
    )


def _decl_attn_drams(nc, layer, shared):
    pre = "a%d_" % layer
    d = dict(
        wqi=nc.dram_tensor(pre + "wq", [16, 128, 2048], F32, kind="ExternalInput").ap(),
        wki=nc.dram_tensor(pre + "wk", [16, 128, 2048], F32, kind="ExternalInput").ap(),
        wvi=nc.dram_tensor(pre + "wv", [4, 128, 8192], F32, kind="ExternalInput").ap(),
        woi=nc.dram_tensor(pre + "wo", [4, 128, 8192], F32, kind="ExternalInput").ap(),
        tri_s=shared["tri_s"], tri_t=shared["tri_t"], eyeb=shared["eyeb"],
    )
    if layer == 0:
        d.update(
            bqi=nc.dram_tensor("bqi", [128, 16], F32, kind="ExternalInput").ap(),
            bki=nc.dram_tensor("bki", [128, 16], F32, kind="ExternalInput").ap(),
            bvr=nc.dram_tensor("bvr", [1, D], F32, kind="ExternalInput").ap(),
            bor=nc.dram_tensor("bor", [1, D], F32, kind="ExternalInput").ap(),
            alibihi=nc.dram_tensor("alibihi", [16, T], BF16, kind="ExternalInput").ap(),
            alc=nc.dram_tensor("alc", [128, 256], F32, kind="ExternalInput").ap(),
        )
    return d


def _decl_shared(nc):
    return dict(
        tri_s=nc.dram_tensor("tri_s", [128, 128], F32, kind="ExternalInput").ap(),
        tri_t=nc.dram_tensor("tri_t", [128, 128], F32, kind="ExternalInput").ap(),
        eyeb=nc.dram_tensor("eyeb", [128, 128], BF16, kind="ExternalInput").ap(),
    )


def _scratch(nc):
    return dict(
        xnf_hi=nc.dram_tensor("xnfh", [16, 128, T], BF16, kind="Internal").ap(),
        xnf_lo=nc.dram_tensor("xnfl", [16, 128, T], BF16, kind="Internal").ap(),
        qd=nc.dram_tensor("qd", [4, 128, 16, 512], BF16, kind="Internal").ap(),
        kd=nc.dram_tensor("kd", [16, 128, T], BF16, kind="Internal").ap(),
        vd=nc.dram_tensor("vd", [16, 128, 16, 128], BF16, kind="Internal").ap(),
    )


def build_fused():
    nc = bass.Bass("TRN2", target_bir_lowering=False)
    xin = nc.dram_tensor("xin", [T, D], F32, kind="ExternalInput").ap()
    xout = nc.dram_tensor("xout", [T, D], F32, kind="ExternalOutput").ap()
    f0 = _decl_ffn_drams(nc, 0)
    f1 = _decl_ffn_drams(nc, 1)
    f1["eyef"] = f0["eyef"]
    shared = _decl_shared(nc)

    a0 = _decl_attn_drams(nc, 0, shared)
    a1 = _decl_attn_drams(nc, 1, shared)
    scr = _scratch(nc)
    with ExitStack() as ctx:
        C = {}
        C["x"] = _x_tiles(nc, ctx)
        C["xin"], C["xout"] = xin, xout
        C["eyeb_d"] = shared["eyeb"]
        C.update(scr)
        p = Prog(nc, ctx)
        emit_load_x(p, C)
        _ffn_sublayer(nc, p, C, 0, f0, "s0")
        _attn_sublayer(nc, p, C, 0, a0, "s1")
        _ffn_sublayer(nc, p, C, 0, f0, "s2")
        _ffn_sublayer(nc, p, C, 1, f1, "s3")
        _attn_sublayer(nc, p, C, 1, a1, "s4")
        _ffn_sublayer(nc, p, C, 1, f1, "s5")
        emit_store_x(p, C)
        p.materialize()
    return nc


def build_ffn_program(layer):
    nc = bass.Bass("TRN2", target_bir_lowering=False)
    xin = nc.dram_tensor("xin", [T, D], F32, kind="ExternalInput").ap()
    xout = nc.dram_tensor("xout", [T, D], F32, kind="ExternalOutput").ap()
    aps = _decl_ffn_drams(nc, layer)
    if layer == 1:
        aps["eyef"] = nc.dram_tensor("eyef", [128, 128], F32, kind="ExternalInput").ap()
    eyeb_d = nc.dram_tensor("eyeb", [128, 128], BF16, kind="ExternalInput").ap()
    scr = _scratch(nc)
    with ExitStack() as ctx:
        C = {}
        C["x"] = _x_tiles(nc, ctx)
        C["xin"], C["xout"] = xin, xout
        C["eyeb_d"] = eyeb_d
        C.update(scr)
        p = Prog(nc, ctx)
        emit_load_x(p, C)
        _ffn_sublayer(nc, p, C, layer, aps, "s0")
        emit_store_x(p, C)
        p.materialize()
    return nc


def build_attn_program(layer):
    nc = bass.Bass("TRN2", target_bir_lowering=False)
    xin = nc.dram_tensor("xin", [T, D], F32, kind="ExternalInput").ap()
    xout = nc.dram_tensor("xout", [T, D], F32, kind="ExternalOutput").ap()
    shared = _decl_shared(nc)
    aps = _decl_attn_drams(nc, layer, shared)
    scr = _scratch(nc)
    with ExitStack() as ctx:
        C = {}
        C["x"] = _x_tiles(nc, ctx)
        C["xin"], C["xout"] = xin, xout
        C["eyeb_d"] = shared["eyeb"]
        C.update(scr)
        p = Prog(nc, ctx)
        emit_load_x(p, C)
        _attn_sublayer(nc, p, C, layer, aps, "s0")
        emit_store_x(p, C)
        p.materialize()
    return nc



def _consts():
    r = np.arange(128)
    eyef = np.eye(128, dtype=np.float32)
    eyeb = np.eye(128).astype(BF)
    tri_s = np.where(r[None, :] > r[:, None], NEG, 0.0).astype(np.float32)
    tri_t = np.where(r[None, :] < r[:, None], NEG, 0.0).astype(np.float32)
    start = 2.0 ** (-2.0 ** (-(np.log2(16) - 3)))
    slopes = np.array([start * start ** i for i in range(16)], dtype=np.float32)
    kk = np.arange(T, dtype=np.float32)
    alibi = (-slopes[:, None] * (T - 1.0 - kk)[None, :]).astype(np.float32)
    alibihi = np.ascontiguousarray(alibi.astype(BF))
    alc = np.zeros((128, 256), dtype=np.float32)
    for h in range(16):
        for c in range(16):
            alc[:, 16 * h + c] = alibi[h, 128 * c:128 * (c + 1)]
    return dict(eyef=eyef, eyeb=eyeb, tri_s=tri_s, tri_t=tri_t,
                alibihi=alibihi, alc=alc)


def _img_ffn1(W):
    E, Dm, H0 = W.shape
    g = H0 // 512
    return np.ascontiguousarray(
        W.reshape(E, 16, 128, g, 512).transpose(0, 3, 2, 1, 4).reshape(E, g, 128, 16 * 512))


def _img_n512(W, nh):
    E = W.shape[0]
    return np.ascontiguousarray(
        W.reshape(E, nh, 128, 4, 512).transpose(0, 3, 2, 1, 4).reshape(E, 4, 128, nh * 512))


def _prep(inputs):
    cst = _consts()
    W1 = np.concatenate([inputs["L0_ffn_w1"].transpose(0, 2, 1),
                         inputs["L0_ffn_ws1"].T[None]], axis=0)
    W3 = np.concatenate([inputs["L0_ffn_w3"].transpose(0, 2, 1),
                         inputs["L0_ffn_ws3"].T[None]], axis=0)
    W2 = np.concatenate([inputs["L0_ffn_w2"].transpose(0, 2, 1),
                         inputs["L0_ffn_ws2"].T[None]], axis=0)
    WG = np.concatenate([inputs["L1_ffn_We_gate"],
                         inputs["L1_ffn_Ws_gate"][:, 0:1408][None],
                         inputs["L1_ffn_Ws_gate"][:, 1408:2816][None]], axis=0)
    WU = np.concatenate([inputs["L1_ffn_We_up"],
                         inputs["L1_ffn_Ws_up"][:, 0:1408][None],
                         inputs["L1_ffn_Ws_up"][:, 1408:2816][None]], axis=0)
    WDn = np.concatenate([inputs["L1_ffn_We_down"],
                          inputs["L1_ffn_Ws_down"][0:1408][None],
                          inputs["L1_ffn_Ws_down"][1408:2816][None]], axis=0)

    def imgA(W):
        return np.ascontiguousarray(
            W[:, :, 0:1024].reshape(10, 16, 128, 2, 512).transpose(0, 3, 2, 1, 4)
            .reshape(10, 2, 128, 16 * 512))

    def imgB(W):
        return np.ascontiguousarray(
            W[:, :, 1024:1408].reshape(10, 16, 128, 384).transpose(0, 2, 1, 3)
            .reshape(10, 128, 16 * 384))

    def aimg128(W):
        return np.ascontiguousarray(
            W.reshape(16, 128, 16, 128).transpose(2, 1, 0, 3).reshape(16, 128, 2048))

    def aimg512(W):
        return np.ascontiguousarray(
            W.reshape(16, 128, 4, 512).transpose(2, 1, 0, 3).reshape(4, 128, 16 * 512))

    h = dict(
        eyef=cst["eyef"], eyeb=cst["eyeb"], tri_s=cst["tri_s"], tri_t=cst["tri_t"],
        alibihi=cst["alibihi"], alc=cst["alc"],
        g0img=np.ascontiguousarray(
            inputs["L0_ffn_gate_w"].T.reshape(16, 128, 8).transpose(1, 0, 2).reshape(128, 128)),
        w1i=_img_ffn1(W1), w3i=_img_ffn1(W3), w2i=_img_n512(W2, 8),
        a0_wq=aimg128(inputs["L0_attn_Wq"]), a0_wk=aimg128(inputs["L0_attn_Wk"]),
        a0_wv=aimg512(inputs["L0_attn_Wv"]), a0_wo=aimg512(inputs["L0_attn_Wo"]),
        bqi=np.ascontiguousarray(inputs["L0_attn_bq"].reshape(16, 128).T),
        bki=np.ascontiguousarray(inputs["L0_attn_bk"].reshape(16, 128).T),
        bvr=np.ascontiguousarray(inputs["L0_attn_bv"].reshape(1, D)),
        bor=np.ascontiguousarray(inputs["L0_attn_bo"].reshape(1, D)),
        g1img=np.ascontiguousarray(
            inputs["L1_ffn_W_router"].reshape(16, 128, 8).transpose(1, 0, 2).reshape(128, 128)),
        wgA=imgA(WG), wgB=imgB(WG), wuA=imgA(WU), wuB=imgB(WU), wdi=_img_n512(WDn, 11),
        a1_wq=aimg128(np.ascontiguousarray(inputs["L1_attn_Wq"].T)),
        a1_wk=aimg128(np.ascontiguousarray(inputs["L1_attn_Wk"].T)),
        a1_wv=aimg512(np.ascontiguousarray(inputs["L1_attn_Wv"].T)),
        a1_wo=aimg512(np.ascontiguousarray(inputs["L1_attn_Wo"].T)),
    )
    return h


_PKEYS = {
    "f0": ["g0img", "eyef", "eyeb", "w1i", "w3i", "w2i"],
    "f1": ["g1img", "eyef", "eyeb", "wgA", "wgB", "wuA", "wuB", "wdi"],
    "a0": ["a0_wq", "a0_wk", "a0_wv", "a0_wo", "tri_s", "tri_t", "eyeb",
           "bqi", "bki", "bvr", "bor", "alibihi", "alc"],
    "a1": ["a1_wq", "a1_wk", "a1_wv", "a1_wo", "tri_s", "tri_t", "eyeb"],
}

_PROGS = {}


def _get_prog(name):
    if name not in _PROGS:
        if name == "fused":
            _PROGS[name] = build_fused()
        elif name == "f0":
            _PROGS[name] = build_ffn_program(0)
        elif name == "f1":
            _PROGS[name] = build_ffn_program(1)
        elif name == "a0":
            _PROGS[name] = build_attn_program(0)
        elif name == "a1":
            _PROGS[name] = build_attn_program(1)
    return _PROGS[name]


def kernel(**inputs):
    h = _prep(inputs)
    nc = _get_prog("fused")
    in_maps = [dict(h, xin=np.ascontiguousarray(inputs["L0_attn_x"][b])) for b in range(8)]
    res = run_bass_kernel_spmd(nc, in_maps, core_ids=list(range(8)))
    return np.stack([res.results[c]["xout"] for c in range(8)], axis=0).astype(np.float32)


# version 07
# speedup vs baseline: 1.3154x; 1.1544x over previous
import numpy as np
import ml_dtypes
from contextlib import ExitStack

import concourse.bass as bass
import concourse.mybir as mybir
from concourse.bass_utils import run_bass_kernel_spmd

F32 = mybir.dt.float32
BF16 = mybir.dt.bfloat16
U32 = mybir.dt.uint32
AF = mybir.ActivationFunctionType
ALU = mybir.AluOpType
AX = mybir.AxisListType
BF = ml_dtypes.bfloat16

T = 2048
D = 2048
NCH = 16
NEG = -1.0e30
EPS = 1e-5
RSQ = 1.0 / float(np.sqrt(128.0))
SAME_ENGINE_SYNC = True


class Prog:
    ENG = {"t": "tensor", "v": "vector", "s": "scalar", "g": "gpsimd", "y": "sync"}

    def __init__(self, nc, ctx, same_engine_sync=SAME_ENGINE_SYNC):
        self.nc = nc
        self.ctx = ctx
        self.same = same_engine_sync
        self.ops = []
        self.esem = {e: ctx.enter_context(nc.semaphore("se_" + e)) for e in self.ENG}
        self.chsem = {}
        self.chn = {}
        self.wr = {}
        self.rd = {}
        self.last_eng = {}
        self.last_ch = {}

    def ch(self, name):
        if name not in self.chsem:
            self.chsem[name] = self.ctx.enter_context(self.nc.semaphore("ch_" + name))
            self.chn[name] = 0
        return name

    def op(self, eng, fn, r=(), w=(), ch=None, extra=()):
        idx = len(self.ops)
        deps = set(extra)
        for k in r:
            deps.update(self.wr.get(k, ()))
            self.rd.setdefault(k, []).append(idx)
        for k in w:
            rs = self.rd.get(k)
            if rs:
                deps.update(rs)
                deps.update(self.wr.get(k, ()))
                self.rd[k] = []
                self.wr[k] = [idx]
            else:
                self.wr.setdefault(k, []).append(idx)
        deps.discard(idx)
        chw = {}
        for dd in deps:
            dch = self.ops[dd]["ch"]
            if dch is not None:
                chw[dch] = 16 * self.chn[dch]
        rec = {"i": idx, "e": eng, "fn": fn, "d": sorted(deps), "ch": ch, "ci": None,
               "inc": None, "w8": (), "ec": 0, "chw": chw}
        if ch is not None:
            self.ch(ch)
            self.chn[ch] += 1
            rec["ci"] = self.chn[ch]
            self.last_ch[ch] = idx
        self.last_eng[eng] = idx
        self.ops.append(rec)
        return idx

    def barrier(self):
        prev = list(self.last_eng.values()) + list(self.last_ch.values())
        for e in self.ENG:
            self.op(e, lambda eng: eng.nop(), extra=prev)

    def materialize(self):
        dep_tgt = set()
        for rec in self.ops:
            dep_tgt.update(rec["d"])
        cnt = {e: 0 for e in self.ENG}
        for rec in self.ops:
            if rec["ch"] is None:
                if rec["i"] in dep_tgt:
                    cnt[rec["e"]] += 1
                    rec["inc"] = cnt[rec["e"]]
                rec["ec"] = cnt[rec["e"]]
        guar = {e: {} for e in self.ENG}
        snap = {}
        for rec in self.ops:
            e = rec["e"]
            g = guar[e]
            need = {}
            for d in rec["d"]:
                dep = self.ops[d]
                if dep["ch"] is not None:
                    src, val = ("D", dep["ch"]), rec["chw"][dep["ch"]]
                else:
                    if dep["e"] == e and (e == "t" or not self.same):
                        continue
                    src, val = ("E", dep["e"]), dep["inc"]
                if need.get(src, 0) < val:
                    need[src] = val
            waits = []
            for src, val in need.items():
                if g.get(src, 0) >= val:
                    continue
                waits.append((src, val))
                g[src] = val
            if waits:
                for d in rec["d"]:
                    for s2, v2 in snap.get(d, {}).items():
                        if g.get(s2, 0) < v2:
                            g[s2] = v2
            rec["w8"] = waits
            if rec["i"] in dep_tgt:
                gg = dict(g)
                if rec["ch"] is None:
                    gg[("E", e)] = rec["ec"]
                else:
                    gg[("D", rec["ch"])] = 16 * rec["ci"]
                snap[rec["i"]] = gg

        per_eng = {e: [rec for rec in self.ops if rec["e"] == e] for e in self.ENG}
        final_ch = dict(self.chn)
        blk = self.ctx.enter_context(self.nc.Block())
        esem, chsem = self.esem, self.chsem

        def mk(e, recs):
            def body(eng):
                for rec in recs:
                    w8 = rec["w8"]
                    for (src, val) in w8[1:]:
                        sem = chsem[src[1]] if src[0] == "D" else esem[src[1]]
                        eng.wait_ge(sem, val)
                    inst = rec["fn"](eng)
                    if w8:
                        src, val = w8[0]
                        sem = chsem[src[1]] if src[0] == "D" else esem[src[1]]
                        inst._wait_ge(sem, val)
                    if rec["ch"] is not None:
                        inst.then_inc(chsem[rec["ch"]], 16)
                    elif rec["inc"] is not None:
                        inst.then_inc(esem[rec["e"]], 1)
                if e == "y":
                    for c, n in final_ch.items():
                        if n:
                            eng.wait_ge(chsem[c], 16 * n)
            return body

        for e, attr in self.ENG.items():
            getattr(blk, attr)(mk(e, per_eng[e]))



def mm(p, out, l, r, start, stop, rd, wr):
    p.op("t", lambda eng, o=out, ll=l, rr=r, a=start, b=stop:
         eng.matmul(o, ll, rr, start=a, stop=b), r=rd, w=wr)


def tr(p, out, in_, ident, rd, wr):
    p.op("t", lambda eng, o=out, i=in_, e=ident: eng.transpose(o, i, e), r=rd, w=wr)


def act(p, out, in_, func, rd, wr, bias=0.0, scale=1.0, accum=None):
    def f(eng, o=out, i=in_, fu=func, b=bias, s=scale, a=accum):
        if a is None:
            return eng.activation(out=o, in_=i, func=fu, bias=b, scale=s)
        return eng.activation(out=o, in_=i, func=fu, bias=b, scale=s, accum_out=a)
    p.op("s", f, r=rd, w=wr)


def vop(p, meth, args, rd, wr):
    p.op("v", lambda eng, m=meth, a=args: getattr(eng, m)(*a), r=rd, w=wr)


def img3(sb2d, dr2d, width):
    o = sb2d.rearrange("p (c h) -> p c h", h=width)
    i = dr2d.rearrange("(c p) h -> p c h", p=128)
    return o, i


def dma(p, eng, out, in_, ch, rd, wr, transpose=False):
    if transpose:
        p.op(eng, lambda e, o=out, i=in_: e.dma_start_transpose(out=o, in_=i),
             r=rd, w=wr, ch=ch)
    else:
        p.op(eng, lambda e, o=out, i=in_: e.dma_start(out=o, in_=i),
             r=rd, w=wr, ch=ch)



def emit_load_x(p, C):
    for c in range(NCH):
        dma(p, "y", C["x"][c], C["xin"][128 * c:128 * (c + 1), :], "xin",
            rd=(), wr=(f"x{c}",))


def emit_store_x(p, C):
    for c in range(NCH):
        dma(p, "y", C["xout"][128 * c:128 * (c + 1), :], C["x"][c], "xout",
            rd=(f"x{c}",), wr=())


def emit_ln(p, C, sc, need_lo):
    PEL = sc["PEL"]
    dma(p, "y", sc["eyebl"], C["eyeb_d"], "c_eyebl", rd=(), wr=("eyebl",))
    stats, mv = sc["lnstat"], sc["lnmv"]
    vv, rbuf, nmr, t1 = sc["lnv"], sc["lnr"], sc["lnnmr"], sc["lnt1"]
    for c in range(NCH):
        sk = f"lnst{c % 2}"
        st = sc["lnstat"] if c % 2 == 0 else sc["lnstat2"]
        for j in range(4):
            vop(p, "bn_stats", (st[:, 6 * j:6 * j + 6], C["x"][c][:, 512 * j:512 * (j + 1)]),
                rd=(f"x{c}",), wr=(sk,))
        vop(p, "bn_aggr", (mv[:, 2 * c:2 * c + 2], st), rd=(sk,), wr=("lnmv",))
    for c in range(NCH):
        vop(p, "tensor_scalar_add", (vv[:, c:c + 1], mv[:, 2 * c + 1:2 * c + 2], EPS),
            rd=("lnmv",), wr=("lnv",))
    act(p, t1, vv, AF.Sqrt, rd=("lnv",), wr=("lnt1",))
    vop(p, "reciprocal", (rbuf, t1), rd=("lnt1",), wr=("lnr",))
    vop(p, "tensor_mul", (t1, rbuf, rbuf), rd=("lnr",), wr=("lnt1",))
    vop(p, "tensor_mul", (t1, t1, vv), rd=("lnt1", "lnv"), wr=("lnt1",))
    vop(p, "tensor_scalar", (t1, t1, -0.5, 1.5, ALU.mult, ALU.add), rd=("lnt1",), wr=("lnt1",))
    vop(p, "tensor_mul", (rbuf, rbuf, t1), rd=("lnr", "lnt1"), wr=("lnr",))
    for c in range(NCH):
        vop(p, "scalar_tensor_tensor",
            (nmr[:, c:c + 1], mv[:, 2 * c:2 * c + 1], -1.0, rbuf[:, c:c + 1], ALU.mult, ALU.mult),
            rd=("lnmv", "lnr"), wr=("lnnmr",))
    for sup in range(4):
        for j in range(4):
            c = 4 * sup + j
            k = c % 2
            xf, hi = sc["lnxf"][k], sc["lnhi"][k]
            act(p, xf, C["x"][c], AF.Identity, rd=(f"x{c}", "lnr", "lnnmr"),
                wr=(f"lnxf{k}",), bias=nmr[:, c:c + 1], scale=rbuf[:, c:c + 1])
            vop(p, "tensor_copy", (hi, xf), rd=(f"lnxf{k}",), wr=(f"lnhi{k}",))
            if need_lo:
                lo = sc["lnlo"][k]
                vop(p, "tensor_sub", (lo, xf, hi), rd=(f"lnxf{k}", f"lnhi{k}"), wr=(f"lnlo{k}",))
            for dc in range(NCH):
                pe = PEL[dc % 2][:, 0:128]
                pk = f"PEL{dc % 2}"
                tr(p, pe, hi[:, 128 * dc:128 * (dc + 1)], sc["eyebl"],
                   rd=(f"lnhi{k}", "eyebl"), wr=(pk,))
                vop(p, "tensor_copy", (sc["stg_hi"][:, 512 * dc + 128 * j: 512 * dc + 128 * (j + 1)], pe),
                    rd=(pk,), wr=("stg_hi",))
                if need_lo:
                    pe2 = PEL[dc % 2][:, 128:256]
                    tr(p, pe2, sc["lnlo"][k][:, 128 * dc:128 * (dc + 1)], sc["eyebl"],
                       rd=(f"lnlo{k}", "eyebl"), wr=(pk,))
                    vop(p, "tensor_copy", (sc["stg_lo"][:, 512 * dc + 128 * j: 512 * dc + 128 * (j + 1)], pe2),
                        rd=(pk,), wr=("stg_lo",))
        for dc in range(NCH):
            dma(p, "y", C["xnf_hi"][dc][:, 512 * sup:512 * (sup + 1)],
                sc["stg_hi"][:, 512 * dc:512 * (dc + 1)], f"st_xh{sup % 2}",
                rd=("stg_hi",), wr=("xnh_d",))
            if need_lo:
                dma(p, "y", C["xnf_lo"][dc][:, 512 * sup:512 * (sup + 1)],
                    sc["stg_lo"][:, 512 * dc:512 * (dc + 1)], f"st_xl{sup % 2}",
                    rd=("stg_lo",), wr=("xnl_d",))


def emit_route(p, C, sc, kind):
    PC = C["PC"]
    ghi, glo = sc["ghi"], sc["glo"]
    vop(p, "tensor_copy", (ghi, sc["gsb"]), rd=("gsb",), wr=("ghi",))
    vop(p, "tensor_sub", (glo, sc["gsb"], ghi), rd=("gsb", "ghi"), wr=("glo",))
    lgt, lgc = sc["lgt"], sc["lgc"]
    hi5, lo5 = sc["rsl_hi"], sc["rsl_lo"]
    for t5 in range(4):
        t0 = 512 * t5
        for d in range(NCH):
            dma(p, "y", hi5[:, 512 * d:512 * (d + 1)],
                C["xnf_hi"][d][:, t0:t0 + 512], "r_hi",
                rd=("xnh_d",), wr=("rslh",))
            dma(p, "y", lo5[:, 512 * d:512 * (d + 1)],
                C["xnf_lo"][d][:, t0:t0 + 512], "r_lo",
                rd=("xnl_d",), wr=("rsll",))
        n3 = 3 * NCH
        k = 0
        for (gg, xx, gk, xk) in ((ghi, hi5, "ghi", "rslh"), (glo, hi5, "glo", "rslh"),
                                 (ghi, lo5, "ghi", "rsll")):
            for d in range(NCH):
                mm(p, PC[0:8, 0:512], gg[:, 8 * d:8 * d + 8], xx[:, 512 * d:512 * (d + 1)],
                   start=(k == 0), stop=(k == n3 - 1), rd=(gk, xk), wr=("PC0",))
                k += 1
        vop(p, "tensor_copy", (lgt, PC[0:8, 0:512]), rd=("PC0",), wr=("lgt",))
        for j in range(4):
            c = 4 * t5 + j
            tr(p, PC[:, 512:520], lgt[:, 128 * j:128 * (j + 1)], sc["eyef"][0:8, 0:8],
               rd=("lgt", "eyef"), wr=("PC1",))
            vop(p, "tensor_copy", (lgc, PC[:, 512:520]), rd=("PC1",), wr=("lgc",))
            mx8, idx = sc["mx8"], sc["idx8"]
            if kind == "sigmoid":
                sco = sc["sco"]
                act(p, sco, lgc, AF.Sigmoid, rd=("lgc",), wr=("sco",))
                vop(p, "max", (mx8, sco), rd=("sco",), wr=("mx8",))
                vop(p, "max_index", (idx, mx8, sco), rd=("sco", "mx8"), wr=("idx8",))
                vop(p, "tensor_copy", (sc["s0c"][:, c:c + 1], mx8[:, 0:1]), rd=("mx8",), wr=("s0c",))
                vop(p, "tensor_copy", (sc["s1c"][:, c:c + 1], mx8[:, 1:2]), rd=("mx8",), wr=("s1c",))
            else:
                mrow = sc["rm1"]
                vop(p, "tensor_reduce", (mrow, lgc, AX.X, ALU.max, False, False, True),
                    rd=("lgc",), wr=("rm1",))
                sco = sc["sco"]
                act(p, sco, lgc, AF.Exp, rd=("lgc", "rm1"), wr=("sco",), bias=mrow)
                vop(p, "tensor_reduce", (sc["rs1"], sco, AX.X, ALU.add, False, False, False),
                    rd=("sco",), wr=("rs1",))
                vop(p, "reciprocal", (sc["ri1"], sc["rs1"]), rd=("rs1",), wr=("ri1",))
                vop(p, "max", (mx8, sco), rd=("sco",), wr=("mx8",))
                vop(p, "max_index", (idx, mx8, sco), rd=("sco", "mx8"), wr=("idx8",))
                vop(p, "scalar_tensor_tensor",
                    (sc["s0c"][:, c:c + 1], mx8[:, 0:1], 0.5, sc["ri1"], ALU.mult, ALU.mult),
                    rd=("mx8", "ri1"), wr=("s0c",))
                vop(p, "scalar_tensor_tensor",
                    (sc["s1c"][:, c:c + 1], mx8[:, 1:2], 0.5, sc["ri1"], ALU.mult, ALU.mult),
                    rd=("mx8", "ri1"), wr=("s1c",))
            vop(p, "tensor_copy", (sc["e0c"][:, c:c + 1], idx[:, 0:1]), rd=("idx8",), wr=("e0c",))
            vop(p, "tensor_copy", (sc["e1c"][:, c:c + 1], idx[:, 1:2]), rd=("idx8",), wr=("e1c",))


def emit_ffn(p, C, sc, cfg):
    PA, PB, PC = C["PA"], C["PB"], C["PC"]
    ZS = [([PA[:, 0:512], PA[:, 512:1024]], ["PA0", "PA1"],
           [PB[:, 512:1024], PC[:, 0:512]], ["PB1", "PC0"]),
          ([PA[:, 1024:1536], PB[:, 0:512]], ["PA2", "PB0"],
           [PC[:, 512:1024], C["PD"]], ["PC1", "PD0"])]
    kind = cfg["kind"]
    experts = cfg["experts"]
    use_cache = cfg.get("cache", False)
    populate = cfg.get("populate", False)

    if kind == "sigmoid":
        sca = sc["sigca"]
        for e in range(8):
            vop(p, "tensor_single_scalar", (sc["cm0"], sc["e0c"], float(e), ALU.is_equal),
                rd=("e0c",), wr=("cm0",))
            vop(p, "tensor_mul", (sc["cm0"], sc["cm0"], sc["s0c"]), rd=("cm0", "s0c"), wr=("cm0",))
            vop(p, "tensor_single_scalar", (sc["cm1"], sc["e1c"], float(e), ALU.is_equal),
                rd=("e1c",), wr=("cm1",))
            vop(p, "tensor_mul", (sc["cm1"], sc["cm1"], sc["s1c"]), rd=("cm1", "s1c"), wr=("cm1",))
            vop(p, "tensor_add", (sca[:, 16 * e:16 * (e + 1)], sc["cm0"], sc["cm1"]),
                rd=("cm0", "cm1"), wr=("sigca",))
    else:
        sca = sc["chata"]
        for e in range(8):
            vop(p, "tensor_single_scalar", (sc["cm0"], sc["e0c"], float(e), ALU.is_equal),
                rd=("e0c",), wr=("cm0",))
            vop(p, "tensor_mul", (sc["cm0"], sc["cm0"], sc["s0c"]), rd=("cm0", "s0c"), wr=("cm0",))
            vop(p, "tensor_single_scalar", (sc["cm1"], sc["e1c"], float(e), ALU.is_equal),
                rd=("e1c",), wr=("cm1",))
            vop(p, "tensor_mul", (sc["cm1"], sc["cm1"], sc["s1c"]), rd=("cm1", "s1c"), wr=("cm1",))
            vop(p, "tensor_add", (sca[:, 16 * e:16 * (e + 1)], sc["cm0"], sc["cm1"]),
                rd=("cm0", "cm1"), wr=("chata",))

    gcnt = 0
    wcnt = 0
    for tbi in range(4):
        t0 = 512 * tbi
        xnb = sc["xnblk"]
        for d in range(NCH):
            dma(p, "y", xnb[:, 512 * d:512 * (d + 1)], C["xnf_hi"][d][:, t0:t0 + 512],
                "xnb", rd=("xnh_d",), wr=("xnblk",))
        for pe, ex in enumerate(experts):
            H = ex["H"]
            nh = H // 128
            groups = ex["groups"]
            sig = ex.get("sigma")
            comb = ex.get("comb")
            if sig is not None:
                for j in range(4):
                    tc = 4 * tbi + j
                    tr(p, PC[0:1, 512:640], sca[:, 16 * sig + tc:16 * sig + tc + 1],
                       sc["eyef"], rd=("sigca", "eyef"), wr=("PC1",))
                    vop(p, "tensor_copy", (sc["sigr"][0:1, 128 * j:128 * (j + 1)], PC[0:1, 512:640]),
                        rd=("PC1",), wr=("sigr",))
                mm(p, PC[:, 512:1024], sc["onesf"], sc["sigr"][0:1, 0:512],
                   start=True, stop=True, rd=("sigr", "onesf"), wr=("PC1",))
                vop(p, "tensor_copy", (sc["sigb"], PC[:, 512:1024]), rd=("PC1",), wr=("sigb",))
            for gi, (h0, gw) in enumerate(groups):
                nci = gw // 128
                wb = wcnt % 2
                wcnt += 1
                w1s, w3s = sc["wsl1"][wb], sc["wsl3"][wb]
                if use_cache and not (populate and tbi == 0):
                    dma(p, "y", w1s[:, 0:16 * gw], ex["c1"](gi), f"w1c{wb}", rd=("wc1",), wr=(f"w1s{wb}",))
                    dma(p, "y", w3s[:, 0:16 * gw], ex["c3"](gi), f"w3c{wb}", rd=("wc3",), wr=(f"w3s{wb}",))
                else:
                    dma(p, "g", w1s[:, 0:16 * gw], ex["w1s"](gi), f"w1c{wb}", rd=(), wr=(f"w1s{wb}",))
                    dma(p, "g", w3s[:, 0:16 * gw], ex["w3s"](gi), f"w3c{wb}", rd=(), wr=(f"w3s{wb}",))
                    if use_cache and populate and tbi == 0:
                        dma(p, "y", ex["c1"](gi), w1s[:, 0:16 * gw], f"p1c{wb}",
                            rd=(f"w1s{wb}",), wr=("wc1",))
                        dma(p, "y", ex["c3"](gi), w3s[:, 0:16 * gw], f"p3c{wb}",
                            rd=(f"w3s{wb}",), wr=("wc3",))
                pz1, zk1, pz3, zk3 = ZS[gcnt % 2]
                gcnt += 1
                for d in range(NCH):
                    for ci in range(nci):
                        lsl = w1s[:, d * gw + 128 * ci: d * gw + 128 * (ci + 1)]
                        mm(p, pz1[ci], lsl, xnb[:, 512 * d:512 * (d + 1)],
                           start=(d == 0), stop=(d == NCH - 1),
                           rd=(f"w1s{wb}", "xnblk"), wr=(zk1[ci],))
                        ls3 = w3s[:, d * gw + 128 * ci: d * gw + 128 * (ci + 1)]
                        mm(p, pz3[ci], ls3, xnb[:, 512 * d:512 * (d + 1)],
                           start=(d == 0), stop=(d == NCH - 1),
                           rd=(f"w3s{wb}", "xnblk"), wr=(zk3[ci],))
                for ci in range(nci):
                    hc = h0 // 128 + ci
                    hs = sc["hsum"][:, 512 * hc:512 * (hc + 1)]
                    if sig is not None:
                        vop(p, "tensor_mul", (sc["zt1"], pz1[ci], sc["sigb"]),
                            rd=(zk1[ci], "sigb"), wr=("zt1",))
                        act(p, sc["zt1"], sc["zt1"], AF.Silu, rd=("zt1",), wr=("zt1",))
                        vop(p, "tensor_mul", (sc["zt2"], pz3[ci], sc["sigb"]),
                            rd=(zk3[ci], "sigb"), wr=("zt2",))
                        vop(p, "tensor_mul", (hs, sc["zt1"], sc["zt2"]),
                            rd=("zt1", "zt2"), wr=("hsum",))
                    else:
                        act(p, sc["zt1"], pz1[ci], AF.Silu, rd=(zk1[ci],), wr=("zt1",))
                        vop(p, "tensor_mul", (hs, sc["zt1"], pz3[ci]),
                            rd=("zt1", zk3[ci]), wr=("hsum",))
            for dt in range(4):
                w2s = sc["wsl2"]
                if use_cache and not (populate and tbi == 0):
                    dma(p, "y", w2s[:, 0:nh * 512], ex["c2"](dt), "w2c", rd=("wc2",), wr=("w2s",))
                else:
                    dma(p, "g", w2s[:, 0:nh * 512], ex["w2s"](dt), "w2c", rd=(), wr=("w2s",))
                    if use_cache and populate and tbi == 0:
                        dma(p, "y", ex["c2"](dt), w2s[:, 0:nh * 512], "p2c",
                            rd=("w2s",), wr=("wc2",))
                for ts in range(4):
                    tc = 4 * tbi + ts
                    py = PA[:, 0:512] if dt % 2 == 0 else PA[:, 1024:1536]
                    pyk = "PA0" if dt % 2 == 0 else "PA2"
                    for hc in range(nh):
                        mm(p, py, sc["hsum"][:, 512 * hc + 128 * ts: 512 * hc + 128 * (ts + 1)],
                           w2s[:, 512 * hc:512 * (hc + 1)], start=(hc == 0), stop=(hc == nh - 1),
                           rd=("hsum", "w2s"), wr=(pyk,))
                    xs = C["x"][tc][:, 512 * dt:512 * (dt + 1)]
                    if comb is not None:
                        scalar = sca[:, 16 * comb + tc:16 * comb + tc + 1]
                        p.op("v", lambda eng, o=xs, i=py, sl=scalar:
                             eng.scalar_tensor_tensor(o, i, sl, o, ALU.mult, ALU.add),
                             r=(pyk, f"x{tc}", "chata"), w=(f"x{tc}",))
                    else:
                        p.op("v", lambda eng, o=xs, i=py:
                             eng.scalar_tensor_tensor(o, i, 0.5, o, ALU.mult, ALU.add),
                             r=(pyk, f"x{tc}"), w=(f"x{tc}",))


def emit_attn_phaseA(p, C, sc, cfg):
    PA, PB = C["PA"], C["PB"]
    xsl = sc["xnslab"]
    pq = [PA[:, 0:512], PA[:, 512:1024], PA[:, 1024:1536], PB[:, 0:512]]
    pk = ["PA0", "PA1", "PA2", "PB0"]
    for half in range(2):
        toff = 1024 * half
        for d in range(NCH):
            dma(p, "y", xsl[:, 1024 * d:1024 * (d + 1)],
                C["xnf_hi"][d][:, toff:toff + 1024], "xnsl",
                rd=("xnh_d",), wr=("xnsl",))
        for proj, wap, bias, bk, scl, dst in (
                ("q", cfg["wqi"], cfg.get("bqs"), "bqs", RSQ, C["qd"]),
                ("k", cfg["wki"], cfg.get("bkc"), "bkc", 1.0, C["kd"])):
            for n in range(NCH):
                wb = n % 2
                ws = sc["wpj"][wb]
                dma(p, "g", ws, wap[n], f"wp{wb}", rd=(), wr=(f"wpj{wb}",))
                for t5 in range(2):
                    ps, psk = pq[t5], pk[t5]
                    for d in range(NCH):
                        mm(p, ps, ws[:, 128 * d:128 * (d + 1)],
                           xsl[:, 1024 * d + 512 * t5: 1024 * d + 512 * (t5 + 1)],
                           start=(d == 0), stop=(d == NCH - 1),
                           rd=(f"wpj{wb}", "xnsl"), wr=(psk,))
                    stg = sc["qkstg"][t5 % 2]
                    b = bias[:, n:n + 1] if bias is not None else 0.0
                    brd = (psk, bk) if bias is not None else (psk,)
                    act(p, stg, ps, AF.Identity, rd=brd, wr=(f"qkst{t5 % 2}",),
                        bias=b, scale=scl)
                    st_i = 2 * half + t5
                    if proj == "q":
                        dma(p, "y", C["qd"][st_i][:, n, :], stg,
                            f"qk{t5 % 2}", rd=(f"qkst{t5 % 2}",), wr=("q_d",))
                    else:
                        dma(p, "y", C["kd"][n][:, 512 * st_i:512 * (st_i + 1)], stg,
                            f"qk{t5 % 2}", rd=(f"qkst{t5 % 2}",), wr=("k_d",))
        for nt in range(4):
            wb = 0
            wv = sc["wvsl"][wb]
            dma(p, "g", wv, cfg["wvi"][nt], f"wv{wb}", rd=(), wr=(f"wvsl{wb}",))
            for ts in range(8):
                ps, psk = pq[ts % 4], pk[ts % 4]
                has_b = cfg.get("bvr") is not None
                for d in range(NCH):
                    mm(p, ps, xsl[:, 1024 * d + 128 * ts: 1024 * d + 128 * (ts + 1)],
                       wv[:, 512 * d:512 * (d + 1)], start=(d == 0),
                       stop=(d == NCH - 1 and not has_b),
                       rd=(f"wvsl{wb}", "xnsl"), wr=(psk,))
                if has_b:
                    mm(p, ps, sc["onesf"], cfg["bvr"][0:1, 512 * nt:512 * (nt + 1)],
                       start=False, stop=True, rd=("bvr", "onesf"), wr=(psk,))
                stg = sc["qkstg"][ts % 2]
                act(p, stg, ps, AF.Identity, rd=(psk,), wr=(f"qkst{ts % 2}",))
                tchunk = 8 * half + ts
                for hh in range(4):
                    dma(p, "y", C["vd"][4 * nt + hh][:, tchunk, :],
                        stg[:, 128 * hh:128 * (hh + 1)],
                        f"qk{ts % 2}", rd=(f"qkst{ts % 2}",), wr=("v_d",))


def emit_attn_phaseB(p, C, sc, cfg):
    PA, PB, PC, PEb = C["PA"], C["PB"], C["PC"], C["PEb"]
    alibi = cfg["alibi"]
    if cfg.get("bor") is not None:
        for nt in range(4):
            dma(p, "y", sc["rowb"][0:1, :], cfg["bor"][0:1, 512 * nt:512 * (nt + 1)],
                "c_bo", rd=(), wr=("rowb",))
            mm(p, PB[:, 512:1024], sc["onesf"], sc["rowb"][0:1, :], start=True, stop=True,
               rd=("rowb", "onesf"), wr=("PB1",))
            vop(p, "tensor_copy", (sc["bob"], PB[:, 512:1024]), rd=("PB1",), wr=("bob",))
            for tc in range(NCH):
                xs = C["x"][tc][:, 512 * nt:512 * (nt + 1)]
                p.op("v", lambda eng, o=xs, b=sc["bob"]: eng.tensor_add(o, o, b),
                     r=("bob", f"x{tc}"), w=(f"x{tc}",))
    ptc = 0
    for st in range(4):
        qst = sc["qst"]
        dma(p, "y", qst, C["qd"][st].rearrange("p h t -> p (h t)"),
            "qstld", rd=("q_d",), wr=("qst",))
        for h in range(NCH):
            hb = h % 2
            kmax = 512 * (st + 1)
            nkc = kmax // 128
            kh, vh = sc["kh"][0], sc["vh"][hb]
            dma(p, "y", kh[:, 0:kmax], C["kd"][h][:, 0:kmax], "khld",
                rd=("k_d",), wr=("kh0",))
            dma(p, "y", vh[:, 0:kmax].rearrange("p (c n) -> p c n", n=128),
                C["vd"][h][:, 0:kmax // 128, :],
                f"vhld{hb}", rd=("v_d",), wr=(f"vh{hb}",))
            if alibi:
                dma(p, "y", sc["alr"][hb][0:1, :], C["alibihi"][h:h + 1, :], f"alr{hb}",
                    rd=(), wr=(f"alr{hb}",))
            spcs = [PA[:, 0:512], PA[:, 512:1024], PA[:, 1024:1536], PC[:, 512:1024]]
            spk = ["PA0", "PA1", "PA2", "PC1"]
            for sub in range(4):
                q0 = 512 * st + 128 * sub
                km = q0 + 128
                nkb = (km + 511) // 512
                for kb in range(nkb):
                    w = min(512, km - 512 * kb)
                    mm(p, spcs[kb][:, 0:w], qst[:, 512 * h + 128 * sub: 512 * h + 128 * (sub + 1)],
                       kh[:, 512 * kb:512 * kb + w], start=True, stop=not alibi,
                       rd=("qst", "kh0"), wr=(spk[kb],))
                    if alibi:
                        mm(p, spcs[kb][:, 0:w], sc["ones1b"],
                           sc["alr"][hb][0:1, 512 * kb:512 * kb + w], start=False, stop=True,
                           rd=(f"alr{hb}", "ones1b"), wr=(spk[kb],))
                dkb = (km - 1) // 512
                off = q0 - 512 * dkb
                p.op("v", lambda eng, o=spcs[dkb][:, off:off + 128], t=sc["tri_s"]:
                     eng.tensor_add(o, o, t), r=(spk[dkb], "tri_s"), w=(spk[dkb],))
                rm = sc["rmx"]
                for kb in range(nkb):
                    w = min(512, km - 512 * kb)
                    vop(p, "tensor_reduce", (rm[:, kb:kb + 1], spcs[kb][:, 0:w], AX.X, ALU.max,
                                             False, False, False), rd=(spk[kb],), wr=("rmx",))
                vop(p, "tensor_reduce", (rm[:, 4:5], rm[:, 0:nkb], AX.X, ALU.max,
                                         False, False, False), rd=("rmx",), wr=("rmx",))
                vop(p, "tensor_scalar_mul", (sc["negm"], rm[:, 4:5], -1.0), rd=("rmx",), wr=("negm",))
                tr(p, PEb[0:1, 0:128], sc["negm"], sc["eyeb"], rd=("negm", "eyeb"), wr=("PEt",))
                vop(p, "tensor_copy", (sc["negmr"][0:1, 128 * sub:128 * (sub + 1)], PEb[0:1, 0:128]),
                    rd=("PEt",), wr=("negmr",))
            for kc in range(nkc):
                pT = sc["pT"][ptc % 3]
                pTk = f"pT{ptc % 3}"
                ptc += 1
                sTp = PB[:, 0:512]
                mm(p, sTp, kh[:, 128 * kc:128 * (kc + 1)], qst[:, 512 * h:512 * (h + 1)],
                   start=True, stop=False, rd=("kh0", "qst"), wr=("PB0",))
                mm(p, sTp, sc["ones1b"], sc["negmr"][0:1, :], start=False, stop=True,
                   rd=("negmr", "ones1b"), wr=("PB0",))
                if kc >= 4 * st:
                    m0 = 128 * kc - 512 * st
                    p.op("v", lambda eng, o=sTp[:, m0:m0 + 128], t=sc["tri_t"]:
                         eng.tensor_add(o, o, t), r=("PB0", "tri_t"), w=("PB0",))
                    if m0 > 0:
                        p.op("v", lambda eng, o=sTp[:, 0:m0]:
                             eng.tensor_scalar_add(o, o, NEG), r=("PB0",), w=("PB0",))
                bias = sc["alc"][:, 16 * h + kc:16 * h + kc + 1] if alibi else 0.0
                act(p, pT, sTp, AF.Exp, rd=("PB0",) + (("alc",) if alibi else ()),
                    wr=(pTk,), bias=bias)
                mm(p, PC[:, 0:512], vh[:, 128 * kc:128 * (kc + 1)], pT,
                   start=(kc == 0), stop=(kc == nkc - 1), rd=(f"vh{hb}", pTk), wr=("PC0",))
                mm(p, PB[0:1, 512:1024], sc["onescb"], pT,
                   start=(kc == 0), stop=(kc == nkc - 1), rd=(pTk, "onescb"), wr=("PB1",))
            vop(p, "tensor_copy", (sc["rsum"][0:1, :], PB[0:1, 512:1024]), rd=("PB1",), wr=("rsum",))
            vop(p, "reciprocal", (sc["rsum"][0:1, :], sc["rsum"][0:1, :]), rd=("rsum",), wr=("rsum",))
            mm(p, PB[:, 512:1024], sc["onesf"], sc["rsum"][0:1, :], start=True, stop=True,
               rd=("rsum", "onesf"), wr=("PB1",))
            vop(p, "tensor_copy", (sc["invb"], PB[:, 512:1024]), rd=("PB1",), wr=("invb",))
            p.op("v", lambda eng, o=sc["attnb"][:, 512 * h:512 * (h + 1)], i=PC[:, 0:512],
                 b=sc["invb"]: eng.tensor_mul(o, i, b), r=("PC0", "invb"), w=("attnb",))
        for nt in range(4):
            wb = 0
            wo = sc["wosl"][wb]
            dma(p, "g", wo, cfg["woi"][nt], f"wo{wb}", rd=(), wr=(f"wosl{wb}",))
            for ts in range(4):
                tc = 4 * st + ts
                py = PB[:, 512:1024]
                for hc in range(NCH):
                    mm(p, py, sc["attnb"][:, 512 * hc + 128 * ts: 512 * hc + 128 * (ts + 1)],
                       wo[:, 512 * hc:512 * (hc + 1)], start=(hc == 0),
                       stop=(hc == NCH - 1),
                       rd=("attnb", f"wosl{wb}"), wr=("PB1",))
                xs = C["x"][tc][:, 512 * nt:512 * (nt + 1)]
                p.op("v", lambda eng, o=xs, i=py: eng.tensor_add(o, i, o),
                     r=("PB1", f"x{tc}"), w=(f"x{tc}",))



def _psum(nc, ctx, pfx=""):
    return {
        "PA": ctx.enter_context(nc.psum_tensor(pfx + "PA", [128, 1536], F32))[:],
        "PB": ctx.enter_context(nc.psum_tensor(pfx + "PB", [128, 1024], F32))[:],
        "PC": ctx.enter_context(nc.psum_tensor(pfx + "PC", [128, 1024], F32))[:],
    }


def _sb(nc, ctx, name, shape, dt):
    return ctx.enter_context(nc.sbuf_tensor(name, shape, dt))[:]


def _x_tiles(nc, ctx):
    return [_sb(nc, ctx, f"x{c}", [128, T], F32) for c in range(NCH)]


def _ln_scope(nc, ctx, need_lo, pfx=""):
    sc = {}
    sc["PEL"] = [ctx.enter_context(nc.psum_tensor(pfx + f"PEL{k}", [128, 1024], BF16))[:]
                 for k in range(2)]
    sc["eyebl"] = _sb(nc, ctx, pfx + "eyebl", [128, 128], BF16)
    sc["stg_hi"] = _sb(nc, ctx, pfx + "stg_hi", [128, 16 * 512], BF16)
    if need_lo:
        sc["stg_lo"] = _sb(nc, ctx, pfx + "stg_lo", [128, 16 * 512], BF16)
    sc["lnstat"] = _sb(nc, ctx, pfx + "lnstat", [128, 24], F32)
    sc["lnstat2"] = _sb(nc, ctx, pfx + "lnstat2", [128, 24], F32)
    sc["lnmv"] = _sb(nc, ctx, pfx + "lnmv", [128, 32], F32)
    sc["lnv"] = _sb(nc, ctx, pfx + "lnv", [128, 16], F32)
    sc["lnr"] = _sb(nc, ctx, pfx + "lnr", [128, 16], F32)
    sc["lnnmr"] = _sb(nc, ctx, pfx + "lnnmr", [128, 16], F32)
    sc["lnt1"] = _sb(nc, ctx, pfx + "lnt1", [128, 16], F32)
    sc["lnxf"] = [_sb(nc, ctx, pfx + f"lnxf{k}", [128, T], F32) for k in range(2)]
    sc["lnhi"] = [_sb(nc, ctx, pfx + f"lnhi{k}", [128, T], BF16) for k in range(2)]
    if need_lo:
        sc["lnlo"] = [_sb(nc, ctx, pfx + f"lnlo{k}", [128, T], BF16) for k in range(2)]
    return sc


def _ffn_sublayer(nc, p, C, layer, aps, pfx, populate=True):
    with ExitStack() as lctx:
        lsc = _ln_scope(nc, lctx, need_lo=True, pfx=pfx)
        emit_ln(p, C, lsc, need_lo=True)
        p.barrier()

    with ExitStack() as fctx:
        C.update(_psum(nc, fctx, pfx=pfx))
        C["PD"] = fctx.enter_context(nc.psum_tensor(pfx + "PD", [128, 512], F32))[:]
        sc = {}
        sc["eyef"] = _sb(nc, fctx, pfx + "eyefs", [128, 128], F32)
        dma(p, "y", sc["eyef"], aps["eyef"], "c_eyef", rd=(), wr=("eyef",))
        sc["onesf"] = _sb(nc, fctx, pfx + "onesf", [1, 128], F32)
        vop(p, "memset", (sc["onesf"], 1.0), rd=(), wr=("onesf",))
        for n in ("s0c", "s1c", "e0c", "e1c", "cm0", "cm1"):
            sc[n] = _sb(nc, fctx, pfx + n, [128, 16], F32)
        kind = "sigmoid" if layer == 0 else "softmax"
        with ExitStack() as rctx:
            sc["gsb"] = _sb(nc, rctx, pfx + "gsb", [128, 128], F32)
            dma(p, "y", sc["gsb"], aps["gimg"], "c_gsb", rd=(), wr=("gsb",))
            sc["ghi"] = _sb(nc, rctx, pfx + "ghi", [128, 128], BF16)
            sc["glo"] = _sb(nc, rctx, pfx + "glo", [128, 128], BF16)
            sc["lgt"] = _sb(nc, rctx, pfx + "lgt", [8, 512], F32)
            sc["lgc"] = _sb(nc, rctx, pfx + "lgc", [128, 8], F32)
            sc["sco"] = _sb(nc, rctx, pfx + "sco", [128, 8], F32)
            sc["mx8"] = _sb(nc, rctx, pfx + "mx8", [128, 8], F32)
            sc["idx8"] = _sb(nc, rctx, pfx + "idx8", [128, 8], U32)
            sc["rm1"] = _sb(nc, rctx, pfx + "rm1", [128, 1], F32)
            sc["rs1"] = _sb(nc, rctx, pfx + "rs1", [128, 1], F32)
            sc["ri1"] = _sb(nc, rctx, pfx + "ri1", [128, 1], F32)
            sc["rsl_hi"] = _sb(nc, rctx, pfx + "rslhi", [128, 16 * 512], BF16)
            sc["rsl_lo"] = _sb(nc, rctx, pfx + "rsllo", [128, 16 * 512], BF16)
            emit_route(p, C, sc, kind)
            p.barrier()

        if layer == 0:
            hmax = 1024
            groups = [(256 * i, 256) for i in range(4)]
            experts = [dict(w1s=(lambda gi, e=e: aps["w1i"][e, gi]),
                            w3s=(lambda gi, e=e: aps["w3i"][e, gi]),
                            w2s=(lambda dt, e=e: aps["w2i"][e, dt]),
                            c1=(lambda gi, e=e: aps["cw1"][e, gi]),
                            c3=(lambda gi, e=e: aps["cw3"][e, gi]),
                            c2=(lambda dt, e=e: aps["cw2"][e, dt]),
                            H=1024, groups=groups,
                            **({"sigma": e} if e < 8 else {})) for e in range(9)]
        else:
            hmax = 1408
            g14 = [(256 * i, 256) for i in range(5)] + [(1280, 128)]
            experts = [dict(w1s=(lambda gi, e=e: aps["wgA"][e, gi] if gi < 5 else aps["wgB"][e]),
                            w3s=(lambda gi, e=e: aps["wuA"][e, gi] if gi < 5 else aps["wuB"][e]),
                            w2s=(lambda dt, e=e: aps["wdi"][e, dt]),
                            c1=(lambda gi, e=e: aps["cgA"][e, gi] if gi < 5 else aps["cgB"][e]),
                            c3=(lambda gi, e=e: aps["cuA"][e, gi] if gi < 5 else aps["cuB"][e]),
                            c2=(lambda dt, e=e: aps["cwd"][e, dt]),
                            H=1408, groups=g14,
                            **({"comb": e} if e < 8 else {})) for e in range(10)]
        nhmax = hmax // 128
        sc["xnblk"] = _sb(nc, fctx, pfx + "xnblk", [128, 16 * 512], BF16)
        sc["wsl1"] = [_sb(nc, fctx, pfx + f"wsl1{k}", [128, 16 * 256], BF16) for k in range(2)]
        sc["wsl3"] = [_sb(nc, fctx, pfx + f"wsl3{k}", [128, 16 * 256], BF16) for k in range(2)]
        sc["wsl2"] = _sb(nc, fctx, pfx + "wsl2", [128, nhmax * 512], BF16)
        sc["hsum"] = _sb(nc, fctx, pfx + "hsum", [128, nhmax * 512], BF16)
        sc["zt1"] = _sb(nc, fctx, pfx + "zt1", [128, 512], F32)
        if layer == 0:
            sc["zt2"] = _sb(nc, fctx, pfx + "zt2", [128, 512], F32)
            sc["sigca"] = _sb(nc, fctx, pfx + "sigca", [128, 128], F32)
            sc["sigr"] = _sb(nc, fctx, pfx + "sigr2", [1, 512], F32)
            sc["sigb"] = _sb(nc, fctx, pfx + "sigb2", [128, 512], F32)
        else:
            sc["chata"] = _sb(nc, fctx, pfx + "chata", [128, 128], F32)
        cfg = dict(kind=kind, experts=experts, cache=True, populate=populate)
        emit_ffn(p, C, sc, cfg)
        p.barrier()


def _attn_sublayer(nc, p, C, layer, aps, pfx):
    with ExitStack() as lctx:
        lsc = _ln_scope(nc, lctx, need_lo=False, pfx=pfx)
        emit_ln(p, C, lsc, need_lo=False)
        p.barrier()

    cfg = dict(alibi=(layer == 0), wqi=aps["wqi"], wki=aps["wki"],
               wvi=aps["wvi"], woi=aps["woi"])
    if layer == 0:
        C["alibihi"] = aps["alibihi"]
    with ExitStack() as actx:
        C.update(_psum(nc, actx, pfx=pfx))
        C["PEb"] = actx.enter_context(nc.psum_tensor(pfx + "PEb", [128, 1024], BF16))[:]
        sc = {}
        sc["onesf"] = _sb(nc, actx, pfx + "onesf", [1, 128], F32)
        vop(p, "memset", (sc["onesf"], 1.0), rd=(), wr=("onesf",))
        sc["ones1b"] = _sb(nc, actx, pfx + "ones1b", [1, 128], BF16)
        vop(p, "memset", (sc["ones1b"], 1.0), rd=(), wr=("ones1b",))
        sc["onescb"] = _sb(nc, actx, pfx + "onescb", [128, 1], BF16)
        vop(p, "memset", (sc["onescb"], 1.0), rd=(), wr=("onescb",))
        sc["tri_s"] = _sb(nc, actx, pfx + "tri_ss", [128, 128], F32)
        dma(p, "y", sc["tri_s"], aps["tri_s"], "c_tris", rd=(), wr=("tri_s",))
        sc["tri_t"] = _sb(nc, actx, pfx + "tri_ts", [128, 128], F32)
        dma(p, "y", sc["tri_t"], aps["tri_t"], "c_trit", rd=(), wr=("tri_t",))
        sc["eyeb"] = _sb(nc, actx, pfx + "eyebs", [128, 128], BF16)
        dma(p, "y", sc["eyeb"], aps["eyeb"], "c_eyeb", rd=(), wr=("eyeb",))
        if layer == 0:
            sc["bqs"] = _sb(nc, actx, pfx + "bqs", [128, 16], F32)
            dma(p, "y", sc["bqs"], aps["bqi"], "c_bq", rd=(), wr=("bqs0",))
            act(p, sc["bqs"], sc["bqs"], AF.Identity, rd=("bqs0",), wr=("bqs",), scale=RSQ)
            sc["bkc"] = _sb(nc, actx, pfx + "bkc", [128, 16], F32)
            dma(p, "y", sc["bkc"], aps["bki"], "c_bk", rd=(), wr=("bkc",))
            sc["alc"] = _sb(nc, actx, pfx + "alcs", [128, 256], F32)
            dma(p, "y", sc["alc"], aps["alc"], "c_alc", rd=(), wr=("alc",))
            sc["alr"] = [_sb(nc, actx, pfx + "alrs0", [1, T], BF16)] * 2
            cfg["bqs"], cfg["bkc"] = sc["bqs"], sc["bkc"]
        with ExitStack() as pactx:
            if layer == 0:
                sc["bvr"] = _sb(nc, pactx, pfx + "bvrs", [1, D], F32)
                dma(p, "y", sc["bvr"], aps["bvr"], "c_bv", rd=(), wr=("bvr",))
                cfg["bvr"] = sc["bvr"]
            sc["xnslab"] = _sb(nc, pactx, pfx + "xnslab", [128, 16 * 1024], BF16)
            sc["wpj"] = [_sb(nc, pactx, pfx + f"wpj{k}", [128, 16 * 128], BF16) for k in range(2)]
            sc["wvsl"] = [_sb(nc, pactx, pfx + "wvsl0", [128, 16 * 512], BF16)] * 2
            sc["qkstg"] = [_sb(nc, pactx, pfx + f"qkstg{k}", [128, 512], BF16) for k in range(2)]
            emit_attn_phaseA(p, C, sc, cfg)
            p.barrier()
        if layer == 0:
            sc["rowb"] = _sb(nc, actx, pfx + "rowb", [1, 512], F32)
            sc["bob"] = _sb(nc, actx, pfx + "bob", [128, 512], F32)
            cfg["bor"] = aps["bor"]
        sc["qst"] = _sb(nc, actx, pfx + "qst", [128, 16 * 512], BF16)
        khb = _sb(nc, actx, pfx + "kh0", [128, T], BF16)
        sc["kh"] = [khb, khb]
        sc["vh"] = [_sb(nc, actx, pfx + f"vh{k}", [128, T], BF16) for k in range(2)]
        pTb = [_sb(nc, actx, pfx + f"pT{k}", [128, 512], BF16) for k in range(3)]
        sc["pT"] = pTb + [pTb[0]]
        sc["attnb"] = _sb(nc, actx, pfx + "attnb", [128, 16 * 512], BF16)
        sc["invb"] = _sb(nc, actx, pfx + "invb", [128, 512], F32)
        sc["rmx"] = _sb(nc, actx, pfx + "rmx", [128, 5], F32)
        sc["negm"] = _sb(nc, actx, pfx + "negm", [128, 1], BF16)
        sc["negmr"] = _sb(nc, actx, pfx + "negmr", [1, 512], BF16)
        sc["rsum"] = _sb(nc, actx, pfx + "rsum", [1, 512], F32)
        sc["wosl"] = [_sb(nc, actx, pfx + "wosl0", [128, 16 * 512], BF16)] * 2
        emit_attn_phaseB(p, C, sc, cfg)
        p.barrier()


def _decl_ffn_drams(nc, layer):
    if layer == 0:
        return dict(
            gimg=nc.dram_tensor("g0img", [128, 128], F32, kind="ExternalInput").ap(),
            eyef=nc.dram_tensor("eyef", [128, 128], F32, kind="ExternalInput").ap(),
            w1i=nc.dram_tensor("w1i", [9, 4, 128, 4096], F32, kind="ExternalInput").ap(),
            w3i=nc.dram_tensor("w3i", [9, 4, 128, 4096], F32, kind="ExternalInput").ap(),
            w2i=nc.dram_tensor("w2i", [9, 4, 128, 4096], F32, kind="ExternalInput").ap(),
            cw1=nc.dram_tensor("cw1", [9, 4, 128, 4096], BF16, kind="Internal").ap(),
            cw3=nc.dram_tensor("cw3", [9, 4, 128, 4096], BF16, kind="Internal").ap(),
            cw2=nc.dram_tensor("cw2", [9, 4, 128, 4096], BF16, kind="Internal").ap(),
        )
    return dict(
        gimg=nc.dram_tensor("g1img", [128, 128], F32, kind="ExternalInput").ap(),
        eyef=None,
        wgA=nc.dram_tensor("wgA", [10, 5, 128, 4096], F32, kind="ExternalInput").ap(),
        wgB=nc.dram_tensor("wgB", [10, 128, 2048], F32, kind="ExternalInput").ap(),
        wuA=nc.dram_tensor("wuA", [10, 5, 128, 4096], F32, kind="ExternalInput").ap(),
        wuB=nc.dram_tensor("wuB", [10, 128, 2048], F32, kind="ExternalInput").ap(),
        wdi=nc.dram_tensor("wdi", [10, 4, 128, 5632], F32, kind="ExternalInput").ap(),
        cgA=nc.dram_tensor("cgA", [10, 5, 128, 4096], BF16, kind="Internal").ap(),
        cgB=nc.dram_tensor("cgB", [10, 128, 2048], BF16, kind="Internal").ap(),
        cuA=nc.dram_tensor("cuA", [10, 5, 128, 4096], BF16, kind="Internal").ap(),
        cuB=nc.dram_tensor("cuB", [10, 128, 2048], BF16, kind="Internal").ap(),
        cwd=nc.dram_tensor("cwd", [10, 4, 128, 5632], BF16, kind="Internal").ap(),
    )


def _decl_attn_drams(nc, layer, shared):
    pre = "a%d_" % layer
    d = dict(
        wqi=nc.dram_tensor(pre + "wq", [16, 128, 2048], F32, kind="ExternalInput").ap(),
        wki=nc.dram_tensor(pre + "wk", [16, 128, 2048], F32, kind="ExternalInput").ap(),
        wvi=nc.dram_tensor(pre + "wv", [4, 128, 8192], F32, kind="ExternalInput").ap(),
        woi=nc.dram_tensor(pre + "wo", [4, 128, 8192], F32, kind="ExternalInput").ap(),
        tri_s=shared["tri_s"], tri_t=shared["tri_t"], eyeb=shared["eyeb"],
    )
    if layer == 0:
        d.update(
            bqi=nc.dram_tensor("bqi", [128, 16], F32, kind="ExternalInput").ap(),
            bki=nc.dram_tensor("bki", [128, 16], F32, kind="ExternalInput").ap(),
            bvr=nc.dram_tensor("bvr", [1, D], F32, kind="ExternalInput").ap(),
            bor=nc.dram_tensor("bor", [1, D], F32, kind="ExternalInput").ap(),
            alibihi=nc.dram_tensor("alibihi", [16, T], BF16, kind="ExternalInput").ap(),
            alc=nc.dram_tensor("alc", [128, 256], F32, kind="ExternalInput").ap(),
        )
    return d


def _decl_shared(nc):
    return dict(
        tri_s=nc.dram_tensor("tri_s", [128, 128], F32, kind="ExternalInput").ap(),
        tri_t=nc.dram_tensor("tri_t", [128, 128], F32, kind="ExternalInput").ap(),
        eyeb=nc.dram_tensor("eyeb", [128, 128], BF16, kind="ExternalInput").ap(),
    )


def _scratch(nc):
    return dict(
        xnf_hi=nc.dram_tensor("xnfh", [16, 128, T], BF16, kind="Internal").ap(),
        xnf_lo=nc.dram_tensor("xnfl", [16, 128, T], BF16, kind="Internal").ap(),
        qd=nc.dram_tensor("qd", [4, 128, 16, 512], BF16, kind="Internal").ap(),
        kd=nc.dram_tensor("kd", [16, 128, T], BF16, kind="Internal").ap(),
        vd=nc.dram_tensor("vd", [16, 128, 16, 128], BF16, kind="Internal").ap(),
    )


def build_fused():
    nc = bass.Bass("TRN2", target_bir_lowering=False)
    xin = nc.dram_tensor("xin", [T, D], F32, kind="ExternalInput").ap()
    xout = nc.dram_tensor("xout", [T, D], F32, kind="ExternalOutput").ap()
    f0 = _decl_ffn_drams(nc, 0)
    f1 = _decl_ffn_drams(nc, 1)
    f1["eyef"] = f0["eyef"]
    shared = _decl_shared(nc)

    a0 = _decl_attn_drams(nc, 0, shared)
    a1 = _decl_attn_drams(nc, 1, shared)
    scr = _scratch(nc)
    with ExitStack() as ctx:
        C = {}
        C["x"] = _x_tiles(nc, ctx)
        C["xin"], C["xout"] = xin, xout
        C["eyeb_d"] = shared["eyeb"]
        C.update(scr)
        p = Prog(nc, ctx)
        emit_load_x(p, C)
        _ffn_sublayer(nc, p, C, 0, f0, "s0", populate=True)
        _attn_sublayer(nc, p, C, 0, a0, "s1")
        _ffn_sublayer(nc, p, C, 0, f0, "s2", populate=False)
        _ffn_sublayer(nc, p, C, 1, f1, "s3", populate=True)
        _attn_sublayer(nc, p, C, 1, a1, "s4")
        _ffn_sublayer(nc, p, C, 1, f1, "s5", populate=False)
        emit_store_x(p, C)
        p.materialize()
    return nc


def build_ffn_program(layer):
    nc = bass.Bass("TRN2", target_bir_lowering=False)
    xin = nc.dram_tensor("xin", [T, D], F32, kind="ExternalInput").ap()
    xout = nc.dram_tensor("xout", [T, D], F32, kind="ExternalOutput").ap()
    aps = _decl_ffn_drams(nc, layer)
    if layer == 1:
        aps["eyef"] = nc.dram_tensor("eyef", [128, 128], F32, kind="ExternalInput").ap()
    eyeb_d = nc.dram_tensor("eyeb", [128, 128], BF16, kind="ExternalInput").ap()
    scr = _scratch(nc)
    with ExitStack() as ctx:
        C = {}
        C["x"] = _x_tiles(nc, ctx)
        C["xin"], C["xout"] = xin, xout
        C["eyeb_d"] = eyeb_d
        C.update(scr)
        p = Prog(nc, ctx)
        emit_load_x(p, C)
        _ffn_sublayer(nc, p, C, layer, aps, "s0")
        emit_store_x(p, C)
        p.materialize()
    return nc


def build_attn_program(layer):
    nc = bass.Bass("TRN2", target_bir_lowering=False)
    xin = nc.dram_tensor("xin", [T, D], F32, kind="ExternalInput").ap()
    xout = nc.dram_tensor("xout", [T, D], F32, kind="ExternalOutput").ap()
    shared = _decl_shared(nc)
    aps = _decl_attn_drams(nc, layer, shared)
    scr = _scratch(nc)
    with ExitStack() as ctx:
        C = {}
        C["x"] = _x_tiles(nc, ctx)
        C["xin"], C["xout"] = xin, xout
        C["eyeb_d"] = shared["eyeb"]
        C.update(scr)
        p = Prog(nc, ctx)
        emit_load_x(p, C)
        _attn_sublayer(nc, p, C, layer, aps, "s0")
        emit_store_x(p, C)
        p.materialize()
    return nc



def _consts():
    r = np.arange(128)
    eyef = np.eye(128, dtype=np.float32)
    eyeb = np.eye(128).astype(BF)
    tri_s = np.where(r[None, :] > r[:, None], NEG, 0.0).astype(np.float32)
    tri_t = np.where(r[None, :] < r[:, None], NEG, 0.0).astype(np.float32)
    start = 2.0 ** (-2.0 ** (-(np.log2(16) - 3)))
    slopes = np.array([start * start ** i for i in range(16)], dtype=np.float32)
    kk = np.arange(T, dtype=np.float32)
    alibi = (-slopes[:, None] * (T - 1.0 - kk)[None, :]).astype(np.float32)
    alibihi = np.ascontiguousarray(alibi.astype(BF))
    alc = np.zeros((128, 256), dtype=np.float32)
    for h in range(16):
        for c in range(16):
            alc[:, 16 * h + c] = alibi[h, 128 * c:128 * (c + 1)]
    return dict(eyef=eyef, eyeb=eyeb, tri_s=tri_s, tri_t=tri_t,
                alibihi=alibihi, alc=alc)


def _img_ffn1(W):
    E, Dm, H0 = W.shape
    g = H0 // 256
    return np.ascontiguousarray(
        W.reshape(E, 16, 128, g, 256).transpose(0, 3, 2, 1, 4).reshape(E, g, 128, 16 * 256))


def _img_n512(W, nh):
    E = W.shape[0]
    return np.ascontiguousarray(
        W.reshape(E, nh, 128, 4, 512).transpose(0, 3, 2, 1, 4).reshape(E, 4, 128, nh * 512))


def _prep(inputs):
    cst = _consts()
    W1 = np.concatenate([inputs["L0_ffn_w1"].transpose(0, 2, 1),
                         inputs["L0_ffn_ws1"].T[None]], axis=0)
    W3 = np.concatenate([inputs["L0_ffn_w3"].transpose(0, 2, 1),
                         inputs["L0_ffn_ws3"].T[None]], axis=0)
    W2 = np.concatenate([inputs["L0_ffn_w2"].transpose(0, 2, 1),
                         inputs["L0_ffn_ws2"].T[None]], axis=0)
    WG = np.concatenate([inputs["L1_ffn_We_gate"],
                         inputs["L1_ffn_Ws_gate"][:, 0:1408][None],
                         inputs["L1_ffn_Ws_gate"][:, 1408:2816][None]], axis=0)
    WU = np.concatenate([inputs["L1_ffn_We_up"],
                         inputs["L1_ffn_Ws_up"][:, 0:1408][None],
                         inputs["L1_ffn_Ws_up"][:, 1408:2816][None]], axis=0)
    WDn = np.concatenate([inputs["L1_ffn_We_down"],
                          inputs["L1_ffn_Ws_down"][0:1408][None],
                          inputs["L1_ffn_Ws_down"][1408:2816][None]], axis=0)

    def imgA(W):
        return np.ascontiguousarray(
            W[:, :, 0:1280].reshape(10, 16, 128, 5, 256).transpose(0, 3, 2, 1, 4)
            .reshape(10, 5, 128, 16 * 256))

    def imgB(W):
        return np.ascontiguousarray(
            W[:, :, 1280:1408].reshape(10, 16, 128, 128).transpose(0, 2, 1, 3)
            .reshape(10, 128, 16 * 128))

    def aimg128(W):
        return np.ascontiguousarray(
            W.reshape(16, 128, 16, 128).transpose(2, 1, 0, 3).reshape(16, 128, 2048))

    def aimg512(W):
        return np.ascontiguousarray(
            W.reshape(16, 128, 4, 512).transpose(2, 1, 0, 3).reshape(4, 128, 16 * 512))

    h = dict(
        eyef=cst["eyef"], eyeb=cst["eyeb"], tri_s=cst["tri_s"], tri_t=cst["tri_t"],
        alibihi=cst["alibihi"], alc=cst["alc"],
        g0img=np.ascontiguousarray(
            inputs["L0_ffn_gate_w"].T.reshape(16, 128, 8).transpose(1, 0, 2).reshape(128, 128)),
        w1i=_img_ffn1(W1), w3i=_img_ffn1(W3), w2i=_img_n512(W2, 8),
        a0_wq=aimg128(inputs["L0_attn_Wq"]), a0_wk=aimg128(inputs["L0_attn_Wk"]),
        a0_wv=aimg512(inputs["L0_attn_Wv"]), a0_wo=aimg512(inputs["L0_attn_Wo"]),
        bqi=np.ascontiguousarray(inputs["L0_attn_bq"].reshape(16, 128).T),
        bki=np.ascontiguousarray(inputs["L0_attn_bk"].reshape(16, 128).T),
        bvr=np.ascontiguousarray(inputs["L0_attn_bv"].reshape(1, D)),
        bor=np.ascontiguousarray(inputs["L0_attn_bo"].reshape(1, D)),
        g1img=np.ascontiguousarray(
            inputs["L1_ffn_W_router"].reshape(16, 128, 8).transpose(1, 0, 2).reshape(128, 128)),
        wgA=imgA(WG), wgB=imgB(WG), wuA=imgA(WU), wuB=imgB(WU), wdi=_img_n512(WDn, 11),
        a1_wq=aimg128(np.ascontiguousarray(inputs["L1_attn_Wq"].T)),
        a1_wk=aimg128(np.ascontiguousarray(inputs["L1_attn_Wk"].T)),
        a1_wv=aimg512(np.ascontiguousarray(inputs["L1_attn_Wv"].T)),
        a1_wo=aimg512(np.ascontiguousarray(inputs["L1_attn_Wo"].T)),
    )
    return h


_PKEYS = {
    "f0": ["g0img", "eyef", "eyeb", "w1i", "w3i", "w2i"],
    "f1": ["g1img", "eyef", "eyeb", "wgA", "wgB", "wuA", "wuB", "wdi"],
    "a0": ["a0_wq", "a0_wk", "a0_wv", "a0_wo", "tri_s", "tri_t", "eyeb",
           "bqi", "bki", "bvr", "bor", "alibihi", "alc"],
    "a1": ["a1_wq", "a1_wk", "a1_wv", "a1_wo", "tri_s", "tri_t", "eyeb"],
}

_PROGS = {}


def _get_prog(name):
    if name not in _PROGS:
        if name == "fused":
            _PROGS[name] = build_fused()
        elif name == "f0":
            _PROGS[name] = build_ffn_program(0)
        elif name == "f1":
            _PROGS[name] = build_ffn_program(1)
        elif name == "a0":
            _PROGS[name] = build_attn_program(0)
        elif name == "a1":
            _PROGS[name] = build_attn_program(1)
    return _PROGS[name]


def kernel(**inputs):
    h = _prep(inputs)
    nc = _get_prog("fused")
    in_maps = [dict(h, xin=np.ascontiguousarray(inputs["L0_attn_x"][b])) for b in range(8)]
    res = run_bass_kernel_spmd(nc, in_maps, core_ids=list(range(8)))
    return np.stack([res.results[c]["xout"] for c in range(8)], axis=0).astype(np.float32)


# version 08 (final)
# speedup vs baseline: 1.3619x; 1.0353x over previous
import numpy as np
import ml_dtypes
from contextlib import ExitStack

import concourse.bass as bass
import concourse.mybir as mybir
from concourse.bass_utils import run_bass_kernel_spmd

F32 = mybir.dt.float32
BF16 = mybir.dt.bfloat16
U32 = mybir.dt.uint32
AF = mybir.ActivationFunctionType
ALU = mybir.AluOpType
AX = mybir.AxisListType
BF = ml_dtypes.bfloat16

T = 2048
D = 2048
NCH = 16
NEG = -1.0e30
EPS = 1e-5
RSQ = 1.0 / float(np.sqrt(128.0))
SAME_ENGINE_SYNC = True


class Prog:
    ENG = {"t": "tensor", "v": "vector", "s": "scalar", "g": "gpsimd", "y": "sync"}

    def __init__(self, nc, ctx, same_engine_sync=SAME_ENGINE_SYNC):
        self.nc = nc
        self.ctx = ctx
        self.same = same_engine_sync
        self.ops = []
        self.esem = {e: ctx.enter_context(nc.semaphore("se_" + e)) for e in self.ENG}
        self.chsem = {}
        self.chn = {}
        self.wr = {}
        self.rd = {}
        self.last_eng = {}
        self.last_ch = {}

    def ch(self, name):
        if name not in self.chsem:
            self.chsem[name] = self.ctx.enter_context(self.nc.semaphore("ch_" + name))
            self.chn[name] = 0
        return name

    def op(self, eng, fn, r=(), w=(), ch=None, extra=()):
        idx = len(self.ops)
        deps = set(extra)
        for k in r:
            deps.update(self.wr.get(k, ()))
            self.rd.setdefault(k, []).append(idx)
        for k in w:
            rs = self.rd.get(k)
            if rs:
                deps.update(rs)
                deps.update(self.wr.get(k, ()))
                self.rd[k] = []
                self.wr[k] = [idx]
            else:
                self.wr.setdefault(k, []).append(idx)
        deps.discard(idx)
        chw = {}
        for dd in deps:
            dch = self.ops[dd]["ch"]
            if dch is not None:
                chw[dch] = 16 * self.chn[dch]
        rec = {"i": idx, "e": eng, "fn": fn, "d": sorted(deps), "ch": ch, "ci": None,
               "inc": None, "w8": (), "ec": 0, "chw": chw}
        if ch is not None:
            self.ch(ch)
            self.chn[ch] += 1
            rec["ci"] = self.chn[ch]
            self.last_ch[ch] = idx
        self.last_eng[eng] = idx
        self.ops.append(rec)
        return idx

    def barrier(self):
        prev = list(self.last_eng.values()) + list(self.last_ch.values())
        for e in self.ENG:
            self.op(e, lambda eng: eng.nop(), extra=prev)

    def materialize(self):
        dep_tgt = set()
        for rec in self.ops:
            dep_tgt.update(rec["d"])
        cnt = {e: 0 for e in self.ENG}
        for rec in self.ops:
            if rec["ch"] is None:
                if rec["i"] in dep_tgt:
                    cnt[rec["e"]] += 1
                    rec["inc"] = cnt[rec["e"]]
                rec["ec"] = cnt[rec["e"]]
        guar = {e: {} for e in self.ENG}
        snap = {}
        for rec in self.ops:
            e = rec["e"]
            g = guar[e]
            need = {}
            for d in rec["d"]:
                dep = self.ops[d]
                if dep["ch"] is not None:
                    src, val = ("D", dep["ch"]), rec["chw"][dep["ch"]]
                else:
                    if dep["e"] == e and (e == "t" or not self.same):
                        continue
                    src, val = ("E", dep["e"]), dep["inc"]
                if need.get(src, 0) < val:
                    need[src] = val
            waits = []
            for src, val in need.items():
                if g.get(src, 0) >= val:
                    continue
                waits.append((src, val))
                g[src] = val
            if waits:
                for d in rec["d"]:
                    for s2, v2 in snap.get(d, {}).items():
                        if g.get(s2, 0) < v2:
                            g[s2] = v2
            rec["w8"] = waits
            if rec["i"] in dep_tgt:
                gg = dict(g)
                if rec["ch"] is None:
                    gg[("E", e)] = rec["ec"]
                else:
                    gg[("D", rec["ch"])] = 16 * rec["ci"]
                snap[rec["i"]] = gg

        per_eng = {e: [rec for rec in self.ops if rec["e"] == e] for e in self.ENG}
        final_ch = dict(self.chn)
        blk = self.ctx.enter_context(self.nc.Block())
        esem, chsem = self.esem, self.chsem

        def mk(e, recs):
            def body(eng):
                for rec in recs:
                    w8 = rec["w8"]
                    for (src, val) in w8[1:]:
                        sem = chsem[src[1]] if src[0] == "D" else esem[src[1]]
                        eng.wait_ge(sem, val)
                    inst = rec["fn"](eng)
                    if w8:
                        src, val = w8[0]
                        sem = chsem[src[1]] if src[0] == "D" else esem[src[1]]
                        inst._wait_ge(sem, val)
                    if rec["ch"] is not None:
                        inst.then_inc(chsem[rec["ch"]], 16)
                    elif rec["inc"] is not None:
                        inst.then_inc(esem[rec["e"]], 1)
                if e == "y":
                    for c, n in final_ch.items():
                        if n:
                            eng.wait_ge(chsem[c], 16 * n)
            return body

        for e, attr in self.ENG.items():
            getattr(blk, attr)(mk(e, per_eng[e]))



def mm(p, out, l, r, start, stop, rd, wr):
    p.op("t", lambda eng, o=out, ll=l, rr=r, a=start, b=stop:
         eng.matmul(o, ll, rr, start=a, stop=b), r=rd, w=wr)


def tr(p, out, in_, ident, rd, wr):
    p.op("t", lambda eng, o=out, i=in_, e=ident: eng.transpose(o, i, e), r=rd, w=wr)


def act(p, out, in_, func, rd, wr, bias=0.0, scale=1.0, accum=None):
    def f(eng, o=out, i=in_, fu=func, b=bias, s=scale, a=accum):
        if a is None:
            return eng.activation(out=o, in_=i, func=fu, bias=b, scale=s)
        return eng.activation(out=o, in_=i, func=fu, bias=b, scale=s, accum_out=a)
    p.op("s", f, r=rd, w=wr)


def vop(p, meth, args, rd, wr):
    p.op("v", lambda eng, m=meth, a=args: getattr(eng, m)(*a), r=rd, w=wr)


def img3(sb2d, dr2d, width):
    o = sb2d.rearrange("p (c h) -> p c h", h=width)
    i = dr2d.rearrange("(c p) h -> p c h", p=128)
    return o, i


def dma(p, eng, out, in_, ch, rd, wr, transpose=False):
    if transpose:
        p.op(eng, lambda e, o=out, i=in_: e.dma_start_transpose(out=o, in_=i),
             r=rd, w=wr, ch=ch)
    else:
        p.op(eng, lambda e, o=out, i=in_: e.dma_start(out=o, in_=i),
             r=rd, w=wr, ch=ch)



def emit_load_x(p, C):
    for c in range(NCH):
        dma(p, "y", C["x"][c], C["xin"][128 * c:128 * (c + 1), :], "xin",
            rd=(), wr=(f"x{c}",))


def emit_store_x(p, C):
    for c in range(NCH):
        dma(p, "y", C["xout"][128 * c:128 * (c + 1), :], C["x"][c], "xout",
            rd=(f"x{c}",), wr=())


def emit_ln(p, C, sc, need_lo):
    PEL = sc["PEL"]
    dma(p, "y", sc["eyebl"], C["eyeb_d"], "c_eyebl", rd=(), wr=("eyebl",))
    stats, mv = sc["lnstat"], sc["lnmv"]
    vv, rbuf, nmr, t1 = sc["lnv"], sc["lnr"], sc["lnnmr"], sc["lnt1"]
    for c in range(NCH):
        sk = f"lnst{c % 2}"
        st = sc["lnstat"] if c % 2 == 0 else sc["lnstat2"]
        for j in range(4):
            vop(p, "bn_stats", (st[:, 6 * j:6 * j + 6], C["x"][c][:, 512 * j:512 * (j + 1)]),
                rd=(f"x{c}",), wr=(sk,))
        vop(p, "bn_aggr", (mv[:, 2 * c:2 * c + 2], st), rd=(sk,), wr=("lnmv",))
    for c in range(NCH):
        vop(p, "tensor_scalar_add", (vv[:, c:c + 1], mv[:, 2 * c + 1:2 * c + 2], EPS),
            rd=("lnmv",), wr=("lnv",))
    act(p, t1, vv, AF.Sqrt, rd=("lnv",), wr=("lnt1",))
    vop(p, "reciprocal", (rbuf, t1), rd=("lnt1",), wr=("lnr",))
    vop(p, "tensor_mul", (t1, rbuf, rbuf), rd=("lnr",), wr=("lnt1",))
    vop(p, "tensor_mul", (t1, t1, vv), rd=("lnt1", "lnv"), wr=("lnt1",))
    vop(p, "tensor_scalar", (t1, t1, -0.5, 1.5, ALU.mult, ALU.add), rd=("lnt1",), wr=("lnt1",))
    vop(p, "tensor_mul", (rbuf, rbuf, t1), rd=("lnr", "lnt1"), wr=("lnr",))
    for c in range(NCH):
        vop(p, "scalar_tensor_tensor",
            (nmr[:, c:c + 1], mv[:, 2 * c:2 * c + 1], -1.0, rbuf[:, c:c + 1], ALU.mult, ALU.mult),
            rd=("lnmv", "lnr"), wr=("lnnmr",))
    for sup in range(4):
        for j in range(4):
            c = 4 * sup + j
            k = c % 2
            xf, hi = sc["lnxf"][k], sc["lnhi"][k]
            act(p, xf, C["x"][c], AF.Identity, rd=(f"x{c}", "lnr", "lnnmr"),
                wr=(f"lnxf{k}",), bias=nmr[:, c:c + 1], scale=rbuf[:, c:c + 1])
            vop(p, "tensor_copy", (hi, xf), rd=(f"lnxf{k}",), wr=(f"lnhi{k}",))
            if need_lo:
                lo = sc["lnlo"][k]
                vop(p, "tensor_sub", (lo, xf, hi), rd=(f"lnxf{k}", f"lnhi{k}"), wr=(f"lnlo{k}",))
            for dc in range(NCH):
                pe = PEL[dc % 2][:, 0:128]
                pk = f"PEL{dc % 2}"
                tr(p, pe, hi[:, 128 * dc:128 * (dc + 1)], sc["eyebl"],
                   rd=(f"lnhi{k}", "eyebl"), wr=(pk,))
                vop(p, "tensor_copy", (sc["stg_hi"][:, 512 * dc + 128 * j: 512 * dc + 128 * (j + 1)], pe),
                    rd=(pk,), wr=("stg_hi",))
                if need_lo:
                    pe2 = PEL[dc % 2][:, 128:256]
                    tr(p, pe2, sc["lnlo"][k][:, 128 * dc:128 * (dc + 1)], sc["eyebl"],
                       rd=(f"lnlo{k}", "eyebl"), wr=(pk,))
                    vop(p, "tensor_copy", (sc["stg_lo"][:, 512 * dc + 128 * j: 512 * dc + 128 * (j + 1)], pe2),
                        rd=(pk,), wr=("stg_lo",))
        for dc in range(NCH):
            dma(p, "y", C["xnf_hi"][dc][:, 512 * sup:512 * (sup + 1)],
                sc["stg_hi"][:, 512 * dc:512 * (dc + 1)], f"st_xh{sup % 2}",
                rd=("stg_hi",), wr=("xnh_d",))
            if need_lo:
                dma(p, "y", C["xnf_lo"][dc][:, 512 * sup:512 * (sup + 1)],
                    sc["stg_lo"][:, 512 * dc:512 * (dc + 1)], f"st_xl{sup % 2}",
                    rd=("stg_lo",), wr=("xnl_d",))


def emit_route(p, C, sc, kind):
    PC = C["PC"]
    ghi, glo = sc["ghi"], sc["glo"]
    vop(p, "tensor_copy", (ghi, sc["gsb"]), rd=("gsb",), wr=("ghi",))
    vop(p, "tensor_sub", (glo, sc["gsb"], ghi), rd=("gsb", "ghi"), wr=("glo",))
    lgt, lgc = sc["lgt"], sc["lgc"]
    hi5, lo5 = sc["rsl_hi"], sc["rsl_lo"]
    for t5 in range(4):
        t0 = 512 * t5
        for d in range(NCH):
            dma(p, "y", hi5[:, 512 * d:512 * (d + 1)],
                C["xnf_hi"][d][:, t0:t0 + 512], "r_hi",
                rd=("xnh_d",), wr=("rslh",))
            dma(p, "y", lo5[:, 512 * d:512 * (d + 1)],
                C["xnf_lo"][d][:, t0:t0 + 512], "r_lo",
                rd=("xnl_d",), wr=("rsll",))
        n3 = 3 * NCH
        k = 0
        for (gg, xx, gk, xk) in ((ghi, hi5, "ghi", "rslh"), (glo, hi5, "glo", "rslh"),
                                 (ghi, lo5, "ghi", "rsll")):
            for d in range(NCH):
                mm(p, PC[0:8, 0:512], gg[:, 8 * d:8 * d + 8], xx[:, 512 * d:512 * (d + 1)],
                   start=(k == 0), stop=(k == n3 - 1), rd=(gk, xk), wr=("PC0",))
                k += 1
        vop(p, "tensor_copy", (lgt, PC[0:8, 0:512]), rd=("PC0",), wr=("lgt",))
        for j in range(4):
            c = 4 * t5 + j
            tr(p, PC[:, 512:520], lgt[:, 128 * j:128 * (j + 1)], sc["eyef"][0:8, 0:8],
               rd=("lgt", "eyef"), wr=("PC1",))
            vop(p, "tensor_copy", (lgc, PC[:, 512:520]), rd=("PC1",), wr=("lgc",))
            mx8, idx = sc["mx8"], sc["idx8"]
            if kind == "sigmoid":
                sco = sc["sco"]
                act(p, sco, lgc, AF.Sigmoid, rd=("lgc",), wr=("sco",))
                vop(p, "max", (mx8, sco), rd=("sco",), wr=("mx8",))
                vop(p, "max_index", (idx, mx8, sco), rd=("sco", "mx8"), wr=("idx8",))
                vop(p, "tensor_copy", (sc["s0c"][:, c:c + 1], mx8[:, 0:1]), rd=("mx8",), wr=("s0c",))
                vop(p, "tensor_copy", (sc["s1c"][:, c:c + 1], mx8[:, 1:2]), rd=("mx8",), wr=("s1c",))
            else:
                mrow = sc["rm1"]
                vop(p, "tensor_reduce", (mrow, lgc, AX.X, ALU.max, False, False, True),
                    rd=("lgc",), wr=("rm1",))
                sco = sc["sco"]
                act(p, sco, lgc, AF.Exp, rd=("lgc", "rm1"), wr=("sco",), bias=mrow)
                vop(p, "tensor_reduce", (sc["rs1"], sco, AX.X, ALU.add, False, False, False),
                    rd=("sco",), wr=("rs1",))
                vop(p, "reciprocal", (sc["ri1"], sc["rs1"]), rd=("rs1",), wr=("ri1",))
                vop(p, "max", (mx8, sco), rd=("sco",), wr=("mx8",))
                vop(p, "max_index", (idx, mx8, sco), rd=("sco", "mx8"), wr=("idx8",))
                vop(p, "scalar_tensor_tensor",
                    (sc["s0c"][:, c:c + 1], mx8[:, 0:1], 0.5, sc["ri1"], ALU.mult, ALU.mult),
                    rd=("mx8", "ri1"), wr=("s0c",))
                vop(p, "scalar_tensor_tensor",
                    (sc["s1c"][:, c:c + 1], mx8[:, 1:2], 0.5, sc["ri1"], ALU.mult, ALU.mult),
                    rd=("mx8", "ri1"), wr=("s1c",))
            vop(p, "tensor_copy", (sc["e0c"][:, c:c + 1], idx[:, 0:1]), rd=("idx8",), wr=("e0c",))
            vop(p, "tensor_copy", (sc["e1c"][:, c:c + 1], idx[:, 1:2]), rd=("idx8",), wr=("e1c",))


def emit_ffn(p, C, sc, cfg):
    PA, PB, PC = C["PA"], C["PB"], C["PC"]
    ZS = [([PA[:, 0:512], PA[:, 512:1024]], ["PA0", "PA1"],
           [PB[:, 512:1024], PC[:, 0:512]], ["PB1", "PC0"]),
          ([PA[:, 1024:1536], PB[:, 0:512]], ["PA2", "PB0"],
           [PC[:, 512:1024], C["PD"]], ["PC1", "PD0"])]
    kind = cfg["kind"]
    experts = cfg["experts"]
    use_cache = cfg.get("cache", False)
    populate = cfg.get("populate", False)

    if kind == "sigmoid":
        sca = sc["sigca"]
        for e in range(8):
            vop(p, "tensor_single_scalar", (sc["cm0"], sc["e0c"], float(e), ALU.is_equal),
                rd=("e0c",), wr=("cm0",))
            vop(p, "tensor_mul", (sc["cm0"], sc["cm0"], sc["s0c"]), rd=("cm0", "s0c"), wr=("cm0",))
            vop(p, "tensor_single_scalar", (sc["cm1"], sc["e1c"], float(e), ALU.is_equal),
                rd=("e1c",), wr=("cm1",))
            vop(p, "tensor_mul", (sc["cm1"], sc["cm1"], sc["s1c"]), rd=("cm1", "s1c"), wr=("cm1",))
            vop(p, "tensor_add", (sca[:, 16 * e:16 * (e + 1)], sc["cm0"], sc["cm1"]),
                rd=("cm0", "cm1"), wr=("sigca",))
    else:
        sca = sc["chata"]
        for e in range(8):
            vop(p, "tensor_single_scalar", (sc["cm0"], sc["e0c"], float(e), ALU.is_equal),
                rd=("e0c",), wr=("cm0",))
            vop(p, "tensor_mul", (sc["cm0"], sc["cm0"], sc["s0c"]), rd=("cm0", "s0c"), wr=("cm0",))
            vop(p, "tensor_single_scalar", (sc["cm1"], sc["e1c"], float(e), ALU.is_equal),
                rd=("e1c",), wr=("cm1",))
            vop(p, "tensor_mul", (sc["cm1"], sc["cm1"], sc["s1c"]), rd=("cm1", "s1c"), wr=("cm1",))
            vop(p, "tensor_add", (sca[:, 16 * e:16 * (e + 1)], sc["cm0"], sc["cm1"]),
                rd=("cm0", "cm1"), wr=("chata",))

    gcnt = 0
    wcnt = 0
    for tbi in range(4):
        t0 = 512 * tbi
        xnb = sc["xnblk"]
        for d in range(NCH):
            dma(p, "y", xnb[:, 512 * d:512 * (d + 1)], C["xnf_hi"][d][:, t0:t0 + 512],
                "xnb", rd=("xnh_d",), wr=("xnblk",))
        for pe, ex in enumerate(experts):
            H = ex["H"]
            nh = H // 128
            groups = ex["groups"]
            sig = ex.get("sigma")
            comb = ex.get("comb")
            if sig is not None:
                for j in range(4):
                    tc = 4 * tbi + j
                    tr(p, PC[0:1, 512:640], sca[:, 16 * sig + tc:16 * sig + tc + 1],
                       sc["eyef"], rd=("sigca", "eyef"), wr=("PC1",))
                    act(p, sc["sigr"][0:1, 128 * j:128 * (j + 1)], PC[0:1, 512:640], AF.Identity,
                        rd=("PC1",), wr=("sigr",))
                mm(p, PC[:, 512:1024], sc["onesf"], sc["sigr"][0:1, 0:512],
                   start=True, stop=True, rd=("sigr", "onesf"), wr=("PC1",))
                act(p, sc["sigb"], PC[:, 512:1024], AF.Identity, rd=("PC1",), wr=("sigb",))
            for gi, (h0, gw) in enumerate(groups):
                nci = gw // 128
                wb = wcnt % 2
                wcnt += 1
                w1s, w3s = sc["wsl1"][wb], sc["wsl3"][wb]
                if use_cache and not (populate and tbi == 0):
                    dma(p, "y", w1s[:, 0:16 * gw], ex["c1"](gi), f"w1c{wb}", rd=("wc1",), wr=(f"w1s{wb}",))
                    dma(p, "y", w3s[:, 0:16 * gw], ex["c3"](gi), f"w3c{wb}", rd=("wc3",), wr=(f"w3s{wb}",))
                else:
                    dma(p, "g", w1s[:, 0:16 * gw], ex["w1s"](gi), f"w1c{wb}", rd=(), wr=(f"w1s{wb}",))
                    dma(p, "g", w3s[:, 0:16 * gw], ex["w3s"](gi), f"w3c{wb}", rd=(), wr=(f"w3s{wb}",))
                    if use_cache and populate and tbi == 0:
                        dma(p, "y", ex["c1"](gi), w1s[:, 0:16 * gw], f"p1c{wb}",
                            rd=(f"w1s{wb}",), wr=("wc1",))
                        dma(p, "y", ex["c3"](gi), w3s[:, 0:16 * gw], f"p3c{wb}",
                            rd=(f"w3s{wb}",), wr=("wc3",))
                pz1, zk1, pz3, zk3 = ZS[gcnt % 2]
                gcnt += 1
                for d in range(NCH):
                    for ci in range(nci):
                        lsl = w1s[:, d * gw + 128 * ci: d * gw + 128 * (ci + 1)]
                        mm(p, pz1[ci], lsl, xnb[:, 512 * d:512 * (d + 1)],
                           start=(d == 0), stop=(d == NCH - 1),
                           rd=(f"w1s{wb}", "xnblk"), wr=(zk1[ci],))
                        ls3 = w3s[:, d * gw + 128 * ci: d * gw + 128 * (ci + 1)]
                        mm(p, pz3[ci], ls3, xnb[:, 512 * d:512 * (d + 1)],
                           start=(d == 0), stop=(d == NCH - 1),
                           rd=(f"w3s{wb}", "xnblk"), wr=(zk3[ci],))
                for ci in range(nci):
                    hc = h0 // 128 + ci
                    hs = sc["hsum"][:, 512 * hc:512 * (hc + 1)]
                    if sig is not None:
                        vop(p, "tensor_mul", (sc["zt1"], pz1[ci], sc["sigb"]),
                            rd=(zk1[ci], "sigb"), wr=("zt1",))
                        act(p, sc["zt1"], sc["zt1"], AF.Silu, rd=("zt1",), wr=("zt1",))
                        vop(p, "tensor_mul", (sc["zt2"], pz3[ci], sc["sigb"]),
                            rd=(zk3[ci], "sigb"), wr=("zt2",))
                        vop(p, "tensor_mul", (hs, sc["zt1"], sc["zt2"]),
                            rd=("zt1", "zt2"), wr=("hsum",))
                    else:
                        act(p, sc["zt1"], pz1[ci], AF.Silu, rd=(zk1[ci],), wr=("zt1",))
                        vop(p, "tensor_mul", (hs, sc["zt1"], pz3[ci]),
                            rd=("zt1", zk3[ci]), wr=("hsum",))
            for dt in range(4):
                w2s = sc["wsl2"]
                if use_cache and not (populate and tbi == 0):
                    dma(p, "y", w2s[:, 0:nh * 512], ex["c2"](dt), "w2c", rd=("wc2",), wr=("w2s",))
                else:
                    dma(p, "g", w2s[:, 0:nh * 512], ex["w2s"](dt), "w2c", rd=(), wr=("w2s",))
                    if use_cache and populate and tbi == 0:
                        dma(p, "y", ex["c2"](dt), w2s[:, 0:nh * 512], "p2c",
                            rd=("w2s",), wr=("wc2",))
                for ts in range(4):
                    tc = 4 * tbi + ts
                    py = PA[:, 0:512] if dt % 2 == 0 else PA[:, 1024:1536]
                    pyk = "PA0" if dt % 2 == 0 else "PA2"
                    for hc in range(nh):
                        mm(p, py, sc["hsum"][:, 512 * hc + 128 * ts: 512 * hc + 128 * (ts + 1)],
                           w2s[:, 512 * hc:512 * (hc + 1)], start=(hc == 0), stop=(hc == nh - 1),
                           rd=("hsum", "w2s"), wr=(pyk,))
                    xs = C["x"][tc][:, 512 * dt:512 * (dt + 1)]
                    if comb is not None:
                        scalar = sca[:, 16 * comb + tc:16 * comb + tc + 1]
                        p.op("v", lambda eng, o=xs, i=py, sl=scalar:
                             eng.scalar_tensor_tensor(o, i, sl, o, ALU.mult, ALU.add),
                             r=(pyk, f"x{tc}", "chata"), w=(f"x{tc}",))
                    else:
                        p.op("v", lambda eng, o=xs, i=py:
                             eng.scalar_tensor_tensor(o, i, 0.5, o, ALU.mult, ALU.add),
                             r=(pyk, f"x{tc}"), w=(f"x{tc}",))


def emit_attn_phaseA(p, C, sc, cfg):
    PA, PB = C["PA"], C["PB"]
    xsl = sc["xnslab"]
    pq = [PA[:, 0:512], PA[:, 512:1024], PA[:, 1024:1536], PB[:, 0:512]]
    pk = ["PA0", "PA1", "PA2", "PB0"]
    for half in range(2):
        toff = 1024 * half
        for d in range(NCH):
            dma(p, "y", xsl[:, 1024 * d:1024 * (d + 1)],
                C["xnf_hi"][d][:, toff:toff + 1024], "xnsl",
                rd=("xnh_d",), wr=("xnsl",))
        for proj, wap, bias, bk, scl, dst in (
                ("q", cfg["wqi"], cfg.get("bqs"), "bqs", RSQ, C["qd"]),
                ("k", cfg["wki"], cfg.get("bkc"), "bkc", 1.0, C["kd"])):
            for n in range(NCH):
                wb = n % 2
                ws = sc["wpj"][wb]
                dma(p, "g", ws, wap[n], f"wp{wb}", rd=(), wr=(f"wpj{wb}",))
                for t5 in range(2):
                    ps, psk = pq[t5], pk[t5]
                    for d in range(NCH):
                        mm(p, ps, ws[:, 128 * d:128 * (d + 1)],
                           xsl[:, 1024 * d + 512 * t5: 1024 * d + 512 * (t5 + 1)],
                           start=(d == 0), stop=(d == NCH - 1),
                           rd=(f"wpj{wb}", "xnsl"), wr=(psk,))
                    stg = sc["qkstg"][t5 % 2]
                    b = bias[:, n:n + 1] if bias is not None else 0.0
                    brd = (psk, bk) if bias is not None else (psk,)
                    act(p, stg, ps, AF.Identity, rd=brd, wr=(f"qkst{t5 % 2}",),
                        bias=b, scale=scl)
                    st_i = 2 * half + t5
                    if proj == "q":
                        dma(p, "y", C["qd"][st_i][:, n, :], stg,
                            f"qk{t5 % 2}", rd=(f"qkst{t5 % 2}",), wr=("q_d",))
                    else:
                        dma(p, "y", C["kd"][n][:, 512 * st_i:512 * (st_i + 1)], stg,
                            f"qk{t5 % 2}", rd=(f"qkst{t5 % 2}",), wr=("k_d",))
        for nt in range(4):
            wb = 0
            wv = sc["wvsl"][wb]
            dma(p, "g", wv, cfg["wvi"][nt], f"wv{wb}", rd=(), wr=(f"wvsl{wb}",))
            for ts in range(8):
                ps, psk = pq[ts % 4], pk[ts % 4]
                has_b = cfg.get("bvr") is not None
                for d in range(NCH):
                    mm(p, ps, xsl[:, 1024 * d + 128 * ts: 1024 * d + 128 * (ts + 1)],
                       wv[:, 512 * d:512 * (d + 1)], start=(d == 0),
                       stop=(d == NCH - 1 and not has_b),
                       rd=(f"wvsl{wb}", "xnsl"), wr=(psk,))
                if has_b:
                    mm(p, ps, sc["onesf"], cfg["bvr"][0:1, 512 * nt:512 * (nt + 1)],
                       start=False, stop=True, rd=("bvr", "onesf"), wr=(psk,))
                stg = sc["qkstg"][ts % 2]
                act(p, stg, ps, AF.Identity, rd=(psk,), wr=(f"qkst{ts % 2}",))
                tchunk = 8 * half + ts
                for hh in range(4):
                    dma(p, "y", C["vd"][4 * nt + hh][:, tchunk, :],
                        stg[:, 128 * hh:128 * (hh + 1)],
                        f"qk{ts % 2}", rd=(f"qkst{ts % 2}",), wr=("v_d",))


def emit_attn_phaseB(p, C, sc, cfg):
    PA, PB, PC, PEb = C["PA"], C["PB"], C["PC"], C["PEb"]
    alibi = cfg["alibi"]
    if cfg.get("bor") is not None:
        for nt in range(4):
            dma(p, "y", sc["rowb"][0:1, :], cfg["bor"][0:1, 512 * nt:512 * (nt + 1)],
                "c_bo", rd=(), wr=("rowb",))
            mm(p, PB[:, 512:1024], sc["onesf"], sc["rowb"][0:1, :], start=True, stop=True,
               rd=("rowb", "onesf"), wr=("PB1",))
            vop(p, "tensor_copy", (sc["bob"], PB[:, 512:1024]), rd=("PB1",), wr=("bob",))
            for tc in range(NCH):
                xs = C["x"][tc][:, 512 * nt:512 * (nt + 1)]
                p.op("v", lambda eng, o=xs, b=sc["bob"]: eng.tensor_add(o, o, b),
                     r=("bob", f"x{tc}"), w=(f"x{tc}",))
    ptc = 0
    for st in range(4):
        qst = sc["qst"]
        dma(p, "y", qst, C["qd"][st].rearrange("p h t -> p (h t)"),
            "qstld", rd=("q_d",), wr=("qst",))
        for h in range(NCH):
            hb = h % 2
            kmax = 512 * (st + 1)
            nkc = kmax // 128
            kh, vh = sc["kh"][0], sc["vh"][hb]
            dma(p, "y", kh[:, 0:kmax], C["kd"][h][:, 0:kmax], "khld",
                rd=("k_d",), wr=("kh0",))
            dma(p, "y", vh[:, 0:kmax].rearrange("p (c n) -> p c n", n=128),
                C["vd"][h][:, 0:kmax // 128, :],
                f"vhld{hb}", rd=("v_d",), wr=(f"vh{hb}",))
            if alibi:
                dma(p, "y", sc["alr"][hb][0:1, :], C["alibihi"][h:h + 1, :], f"alr{hb}",
                    rd=(), wr=(f"alr{hb}",))
            spcs = [PA[:, 0:512], PA[:, 512:1024], PA[:, 1024:1536], PC[:, 512:1024]]
            spk = ["PA0", "PA1", "PA2", "PC1"]
            for sub in range(4):
                q0 = 512 * st + 128 * sub
                km = q0 + 128
                nkb = (km + 511) // 512
                for kb in range(nkb):
                    w = min(512, km - 512 * kb)
                    mm(p, spcs[kb][:, 0:w], qst[:, 512 * h + 128 * sub: 512 * h + 128 * (sub + 1)],
                       kh[:, 512 * kb:512 * kb + w], start=True, stop=not alibi,
                       rd=("qst", "kh0"), wr=(spk[kb],))
                    if alibi:
                        mm(p, spcs[kb][:, 0:w], sc["ones1b"],
                           sc["alr"][hb][0:1, 512 * kb:512 * kb + w], start=False, stop=True,
                           rd=(f"alr{hb}", "ones1b"), wr=(spk[kb],))
                dkb = (km - 1) // 512
                off = q0 - 512 * dkb
                p.op("v", lambda eng, o=spcs[dkb][:, off:off + 128], t=sc["tri_s"]:
                     eng.tensor_add(o, o, t), r=(spk[dkb], "tri_s"), w=(spk[dkb],))
                rm = sc["rmx"]
                wpa = min(km, 1536)
                rdk = tuple(spk[kb] for kb in range(min(nkb, 3)))
                vop(p, "tensor_reduce", (rm[:, 0:1], PA[:, 0:wpa], AX.X, ALU.max,
                                         False, False, False), rd=rdk, wr=("rmx",))
                if km > 1536:
                    vop(p, "tensor_reduce", (rm[:, 1:2], spcs[3][:, 0:km - 1536], AX.X, ALU.max,
                                             False, False, False), rd=("PC1",), wr=("rmx",))
                    vop(p, "tensor_reduce", (rm[:, 4:5], rm[:, 0:2], AX.X, ALU.max,
                                             False, False, False), rd=("rmx",), wr=("rmx",))
                    vop(p, "tensor_scalar_mul", (sc["negm"], rm[:, 4:5], -1.0), rd=("rmx",), wr=("negm",))
                else:
                    vop(p, "tensor_scalar_mul", (sc["negm"], rm[:, 0:1], -1.0), rd=("rmx",), wr=("negm",))
                tr(p, PEb[0:1, 0:128], sc["negm"], sc["eyeb"], rd=("negm", "eyeb"), wr=("PEt",))
                act(p, sc["negmr"][0:1, 128 * sub:128 * (sub + 1)], PEb[0:1, 0:128], AF.Identity,
                    rd=("PEt",), wr=("negmr",))
            for kc in range(nkc):
                pT = sc["pT"][ptc % 3]
                pTk = f"pT{ptc % 3}"
                ptc += 1
                sTp = PB[:, 0:512]
                mm(p, sTp, kh[:, 128 * kc:128 * (kc + 1)], qst[:, 512 * h:512 * (h + 1)],
                   start=True, stop=False, rd=("kh0", "qst"), wr=("PB0",))
                mm(p, sTp, sc["ones1b"], sc["negmr"][0:1, :], start=False, stop=True,
                   rd=("negmr", "ones1b"), wr=("PB0",))
                if kc >= 4 * st:
                    m0 = 128 * kc - 512 * st
                    p.op("v", lambda eng, o=sTp[:, m0:m0 + 128], t=sc["tri_t"]:
                         eng.tensor_add(o, o, t), r=("PB0", "tri_t"), w=("PB0",))
                    if m0 > 0:
                        p.op("v", lambda eng, o=sTp[:, 0:m0]:
                             eng.tensor_scalar_add(o, o, NEG), r=("PB0",), w=("PB0",))
                bias = sc["alc"][:, 16 * h + kc:16 * h + kc + 1] if alibi else 0.0
                act(p, pT, sTp, AF.Exp, rd=("PB0",) + (("alc",) if alibi else ()),
                    wr=(pTk,), bias=bias)
                mm(p, PC[:, 0:512], vh[:, 128 * kc:128 * (kc + 1)], pT,
                   start=(kc == 0), stop=(kc == nkc - 1), rd=(f"vh{hb}", pTk), wr=("PC0",))
                mm(p, PB[0:1, 512:1024], sc["onescb"], pT,
                   start=(kc == 0), stop=(kc == nkc - 1), rd=(pTk, "onescb"), wr=("PB1",))
            act(p, sc["rsum"][0:1, :], PB[0:1, 512:1024], AF.Identity, rd=("PB1",), wr=("rsum",))
            vop(p, "reciprocal", (sc["rsum"][0:1, :], sc["rsum"][0:1, :]), rd=("rsum",), wr=("rsum",))
            mm(p, PB[:, 512:1024], sc["onesf"], sc["rsum"][0:1, :], start=True, stop=True,
               rd=("rsum", "onesf"), wr=("PB1",))
            act(p, sc["invb"], PB[:, 512:1024], AF.Identity, rd=("PB1",), wr=("invb",))
            p.op("v", lambda eng, o=sc["attnb"][:, 512 * h:512 * (h + 1)], i=PC[:, 0:512],
                 b=sc["invb"]: eng.tensor_mul(o, i, b), r=("PC0", "invb"), w=("attnb",))
        for nt in range(4):
            wb = 0
            wo = sc["wosl"][wb]
            dma(p, "g", wo, cfg["woi"][nt], f"wo{wb}", rd=(), wr=(f"wosl{wb}",))
            for ts in range(4):
                tc = 4 * st + ts
                py = PB[:, 512:1024]
                for hc in range(NCH):
                    mm(p, py, sc["attnb"][:, 512 * hc + 128 * ts: 512 * hc + 128 * (ts + 1)],
                       wo[:, 512 * hc:512 * (hc + 1)], start=(hc == 0),
                       stop=(hc == NCH - 1),
                       rd=("attnb", f"wosl{wb}"), wr=("PB1",))
                xs = C["x"][tc][:, 512 * nt:512 * (nt + 1)]
                p.op("v", lambda eng, o=xs, i=py: eng.tensor_add(o, i, o),
                     r=("PB1", f"x{tc}"), w=(f"x{tc}",))



def _psum(nc, ctx, pfx=""):
    return {
        "PA": ctx.enter_context(nc.psum_tensor(pfx + "PA", [128, 1536], F32))[:],
        "PB": ctx.enter_context(nc.psum_tensor(pfx + "PB", [128, 1024], F32))[:],
        "PC": ctx.enter_context(nc.psum_tensor(pfx + "PC", [128, 1024], F32))[:],
    }


def _sb(nc, ctx, name, shape, dt):
    return ctx.enter_context(nc.sbuf_tensor(name, shape, dt))[:]


def _x_tiles(nc, ctx):
    return [_sb(nc, ctx, f"x{c}", [128, T], F32) for c in range(NCH)]


def _ln_scope(nc, ctx, need_lo, pfx=""):
    sc = {}
    sc["PEL"] = [ctx.enter_context(nc.psum_tensor(pfx + f"PEL{k}", [128, 1024], BF16))[:]
                 for k in range(2)]
    sc["eyebl"] = _sb(nc, ctx, pfx + "eyebl", [128, 128], BF16)
    sc["stg_hi"] = _sb(nc, ctx, pfx + "stg_hi", [128, 16 * 512], BF16)
    if need_lo:
        sc["stg_lo"] = _sb(nc, ctx, pfx + "stg_lo", [128, 16 * 512], BF16)
    sc["lnstat"] = _sb(nc, ctx, pfx + "lnstat", [128, 24], F32)
    sc["lnstat2"] = _sb(nc, ctx, pfx + "lnstat2", [128, 24], F32)
    sc["lnmv"] = _sb(nc, ctx, pfx + "lnmv", [128, 32], F32)
    sc["lnv"] = _sb(nc, ctx, pfx + "lnv", [128, 16], F32)
    sc["lnr"] = _sb(nc, ctx, pfx + "lnr", [128, 16], F32)
    sc["lnnmr"] = _sb(nc, ctx, pfx + "lnnmr", [128, 16], F32)
    sc["lnt1"] = _sb(nc, ctx, pfx + "lnt1", [128, 16], F32)
    sc["lnxf"] = [_sb(nc, ctx, pfx + f"lnxf{k}", [128, T], F32) for k in range(2)]
    sc["lnhi"] = [_sb(nc, ctx, pfx + f"lnhi{k}", [128, T], BF16) for k in range(2)]
    if need_lo:
        sc["lnlo"] = [_sb(nc, ctx, pfx + f"lnlo{k}", [128, T], BF16) for k in range(2)]
    return sc


def _ffn_sublayer(nc, p, C, layer, aps, pfx, populate=True):
    with ExitStack() as lctx:
        lsc = _ln_scope(nc, lctx, need_lo=True, pfx=pfx)
        emit_ln(p, C, lsc, need_lo=True)
        p.barrier()

    with ExitStack() as fctx:
        C.update(_psum(nc, fctx, pfx=pfx))
        C["PD"] = fctx.enter_context(nc.psum_tensor(pfx + "PD", [128, 512], F32))[:]
        sc = {}
        sc["eyef"] = _sb(nc, fctx, pfx + "eyefs", [128, 128], F32)
        dma(p, "y", sc["eyef"], aps["eyef"], "c_eyef", rd=(), wr=("eyef",))
        sc["onesf"] = _sb(nc, fctx, pfx + "onesf", [1, 128], F32)
        vop(p, "memset", (sc["onesf"], 1.0), rd=(), wr=("onesf",))
        for n in ("s0c", "s1c", "e0c", "e1c", "cm0", "cm1"):
            sc[n] = _sb(nc, fctx, pfx + n, [128, 16], F32)
        kind = "sigmoid" if layer == 0 else "softmax"
        with ExitStack() as rctx:
            sc["gsb"] = _sb(nc, rctx, pfx + "gsb", [128, 128], F32)
            dma(p, "y", sc["gsb"], aps["gimg"], "c_gsb", rd=(), wr=("gsb",))
            sc["ghi"] = _sb(nc, rctx, pfx + "ghi", [128, 128], BF16)
            sc["glo"] = _sb(nc, rctx, pfx + "glo", [128, 128], BF16)
            sc["lgt"] = _sb(nc, rctx, pfx + "lgt", [8, 512], F32)
            sc["lgc"] = _sb(nc, rctx, pfx + "lgc", [128, 8], F32)
            sc["sco"] = _sb(nc, rctx, pfx + "sco", [128, 8], F32)
            sc["mx8"] = _sb(nc, rctx, pfx + "mx8", [128, 8], F32)
            sc["idx8"] = _sb(nc, rctx, pfx + "idx8", [128, 8], U32)
            sc["rm1"] = _sb(nc, rctx, pfx + "rm1", [128, 1], F32)
            sc["rs1"] = _sb(nc, rctx, pfx + "rs1", [128, 1], F32)
            sc["ri1"] = _sb(nc, rctx, pfx + "ri1", [128, 1], F32)
            sc["rsl_hi"] = _sb(nc, rctx, pfx + "rslhi", [128, 16 * 512], BF16)
            sc["rsl_lo"] = _sb(nc, rctx, pfx + "rsllo", [128, 16 * 512], BF16)
            emit_route(p, C, sc, kind)
            p.barrier()

        if layer == 0:
            hmax = 1024
            groups = [(256 * i, 256) for i in range(4)]
            experts = [dict(w1s=(lambda gi, e=e: aps["w1i"][e, gi]),
                            w3s=(lambda gi, e=e: aps["w3i"][e, gi]),
                            w2s=(lambda dt, e=e: aps["w2i"][e, dt]),
                            c1=(lambda gi, e=e: aps["cw1"][e, gi]),
                            c3=(lambda gi, e=e: aps["cw3"][e, gi]),
                            c2=(lambda dt, e=e: aps["cw2"][e, dt]),
                            H=1024, groups=groups,
                            **({"sigma": e} if e < 8 else {})) for e in range(9)]
        else:
            hmax = 1408
            g14 = [(256 * i, 256) for i in range(5)] + [(1280, 128)]
            experts = [dict(w1s=(lambda gi, e=e: aps["wgA"][e, gi] if gi < 5 else aps["wgB"][e]),
                            w3s=(lambda gi, e=e: aps["wuA"][e, gi] if gi < 5 else aps["wuB"][e]),
                            w2s=(lambda dt, e=e: aps["wdi"][e, dt]),
                            c1=(lambda gi, e=e: aps["cgA"][e, gi] if gi < 5 else aps["cgB"][e]),
                            c3=(lambda gi, e=e: aps["cuA"][e, gi] if gi < 5 else aps["cuB"][e]),
                            c2=(lambda dt, e=e: aps["cwd"][e, dt]),
                            H=1408, groups=g14,
                            **({"comb": e} if e < 8 else {})) for e in range(10)]
        nhmax = hmax // 128
        sc["xnblk"] = _sb(nc, fctx, pfx + "xnblk", [128, 16 * 512], BF16)
        sc["wsl1"] = [_sb(nc, fctx, pfx + f"wsl1{k}", [128, 16 * 256], BF16) for k in range(2)]
        sc["wsl3"] = [_sb(nc, fctx, pfx + f"wsl3{k}", [128, 16 * 256], BF16) for k in range(2)]
        sc["wsl2"] = _sb(nc, fctx, pfx + "wsl2", [128, nhmax * 512], BF16)
        sc["hsum"] = _sb(nc, fctx, pfx + "hsum", [128, nhmax * 512], BF16)
        sc["zt1"] = _sb(nc, fctx, pfx + "zt1", [128, 512], F32)
        if layer == 0:
            sc["zt2"] = _sb(nc, fctx, pfx + "zt2", [128, 512], F32)
            sc["sigca"] = _sb(nc, fctx, pfx + "sigca", [128, 128], F32)
            sc["sigr"] = _sb(nc, fctx, pfx + "sigr2", [1, 512], F32)
            sc["sigb"] = _sb(nc, fctx, pfx + "sigb2", [128, 512], F32)
        else:
            sc["chata"] = _sb(nc, fctx, pfx + "chata", [128, 128], F32)
        cfg = dict(kind=kind, experts=experts, cache=True, populate=populate)
        emit_ffn(p, C, sc, cfg)
        p.barrier()


def _attn_sublayer(nc, p, C, layer, aps, pfx):
    with ExitStack() as lctx:
        lsc = _ln_scope(nc, lctx, need_lo=False, pfx=pfx)
        emit_ln(p, C, lsc, need_lo=False)
        p.barrier()

    cfg = dict(alibi=(layer == 0), wqi=aps["wqi"], wki=aps["wki"],
               wvi=aps["wvi"], woi=aps["woi"])
    if layer == 0:
        C["alibihi"] = aps["alibihi"]
    with ExitStack() as actx:
        C.update(_psum(nc, actx, pfx=pfx))
        C["PEb"] = actx.enter_context(nc.psum_tensor(pfx + "PEb", [128, 1024], BF16))[:]
        sc = {}
        sc["onesf"] = _sb(nc, actx, pfx + "onesf", [1, 128], F32)
        vop(p, "memset", (sc["onesf"], 1.0), rd=(), wr=("onesf",))
        sc["ones1b"] = _sb(nc, actx, pfx + "ones1b", [1, 128], BF16)
        vop(p, "memset", (sc["ones1b"], 1.0), rd=(), wr=("ones1b",))
        sc["onescb"] = _sb(nc, actx, pfx + "onescb", [128, 1], BF16)
        vop(p, "memset", (sc["onescb"], 1.0), rd=(), wr=("onescb",))
        sc["tri_s"] = _sb(nc, actx, pfx + "tri_ss", [128, 128], F32)
        dma(p, "y", sc["tri_s"], aps["tri_s"], "c_tris", rd=(), wr=("tri_s",))
        sc["tri_t"] = _sb(nc, actx, pfx + "tri_ts", [128, 128], F32)
        dma(p, "y", sc["tri_t"], aps["tri_t"], "c_trit", rd=(), wr=("tri_t",))
        sc["eyeb"] = _sb(nc, actx, pfx + "eyebs", [128, 128], BF16)
        dma(p, "y", sc["eyeb"], aps["eyeb"], "c_eyeb", rd=(), wr=("eyeb",))
        if layer == 0:
            sc["bqs"] = _sb(nc, actx, pfx + "bqs", [128, 16], F32)
            dma(p, "y", sc["bqs"], aps["bqi"], "c_bq", rd=(), wr=("bqs0",))
            act(p, sc["bqs"], sc["bqs"], AF.Identity, rd=("bqs0",), wr=("bqs",), scale=RSQ)
            sc["bkc"] = _sb(nc, actx, pfx + "bkc", [128, 16], F32)
            dma(p, "y", sc["bkc"], aps["bki"], "c_bk", rd=(), wr=("bkc",))
            sc["alc"] = _sb(nc, actx, pfx + "alcs", [128, 256], F32)
            dma(p, "y", sc["alc"], aps["alc"], "c_alc", rd=(), wr=("alc",))
            sc["alr"] = [_sb(nc, actx, pfx + "alrs0", [1, T], BF16)] * 2
            cfg["bqs"], cfg["bkc"] = sc["bqs"], sc["bkc"]
        with ExitStack() as pactx:
            if layer == 0:
                sc["bvr"] = _sb(nc, pactx, pfx + "bvrs", [1, D], F32)
                dma(p, "y", sc["bvr"], aps["bvr"], "c_bv", rd=(), wr=("bvr",))
                cfg["bvr"] = sc["bvr"]
            sc["xnslab"] = _sb(nc, pactx, pfx + "xnslab", [128, 16 * 1024], BF16)
            sc["wpj"] = [_sb(nc, pactx, pfx + f"wpj{k}", [128, 16 * 128], BF16) for k in range(2)]
            sc["wvsl"] = [_sb(nc, pactx, pfx + "wvsl0", [128, 16 * 512], BF16)] * 2
            sc["qkstg"] = [_sb(nc, pactx, pfx + f"qkstg{k}", [128, 512], BF16) for k in range(2)]
            emit_attn_phaseA(p, C, sc, cfg)
            p.barrier()
        if layer == 0:
            sc["rowb"] = _sb(nc, actx, pfx + "rowb", [1, 512], F32)
            sc["bob"] = _sb(nc, actx, pfx + "bob", [128, 512], F32)
            cfg["bor"] = aps["bor"]
        sc["qst"] = _sb(nc, actx, pfx + "qst", [128, 16 * 512], BF16)
        khb = _sb(nc, actx, pfx + "kh0", [128, T], BF16)
        sc["kh"] = [khb, khb]
        sc["vh"] = [_sb(nc, actx, pfx + f"vh{k}", [128, T], BF16) for k in range(2)]
        pTb = [_sb(nc, actx, pfx + f"pT{k}", [128, 512], BF16) for k in range(3)]
        sc["pT"] = pTb + [pTb[0]]
        sc["attnb"] = _sb(nc, actx, pfx + "attnb", [128, 16 * 512], BF16)
        sc["invb"] = _sb(nc, actx, pfx + "invb", [128, 512], F32)
        sc["rmx"] = _sb(nc, actx, pfx + "rmx", [128, 5], F32)
        sc["negm"] = _sb(nc, actx, pfx + "negm", [128, 1], BF16)
        sc["negmr"] = _sb(nc, actx, pfx + "negmr", [1, 512], BF16)
        sc["rsum"] = _sb(nc, actx, pfx + "rsum", [1, 512], F32)
        sc["wosl"] = [_sb(nc, actx, pfx + "wosl0", [128, 16 * 512], BF16)] * 2
        emit_attn_phaseB(p, C, sc, cfg)
        p.barrier()


def _decl_ffn_drams(nc, layer):
    if layer == 0:
        return dict(
            gimg=nc.dram_tensor("g0img", [128, 128], F32, kind="ExternalInput").ap(),
            eyef=nc.dram_tensor("eyef", [128, 128], F32, kind="ExternalInput").ap(),
            w1i=nc.dram_tensor("w1i", [9, 4, 128, 4096], F32, kind="ExternalInput").ap(),
            w3i=nc.dram_tensor("w3i", [9, 4, 128, 4096], F32, kind="ExternalInput").ap(),
            w2i=nc.dram_tensor("w2i", [9, 4, 128, 4096], F32, kind="ExternalInput").ap(),
            cw1=nc.dram_tensor("cw1", [9, 4, 128, 4096], BF16, kind="Internal").ap(),
            cw3=nc.dram_tensor("cw3", [9, 4, 128, 4096], BF16, kind="Internal").ap(),
            cw2=nc.dram_tensor("cw2", [9, 4, 128, 4096], BF16, kind="Internal").ap(),
        )
    return dict(
        gimg=nc.dram_tensor("g1img", [128, 128], F32, kind="ExternalInput").ap(),
        eyef=None,
        wgA=nc.dram_tensor("wgA", [10, 5, 128, 4096], F32, kind="ExternalInput").ap(),
        wgB=nc.dram_tensor("wgB", [10, 128, 2048], F32, kind="ExternalInput").ap(),
        wuA=nc.dram_tensor("wuA", [10, 5, 128, 4096], F32, kind="ExternalInput").ap(),
        wuB=nc.dram_tensor("wuB", [10, 128, 2048], F32, kind="ExternalInput").ap(),
        wdi=nc.dram_tensor("wdi", [10, 4, 128, 5632], F32, kind="ExternalInput").ap(),
        cgA=nc.dram_tensor("cgA", [10, 5, 128, 4096], BF16, kind="Internal").ap(),
        cgB=nc.dram_tensor("cgB", [10, 128, 2048], BF16, kind="Internal").ap(),
        cuA=nc.dram_tensor("cuA", [10, 5, 128, 4096], BF16, kind="Internal").ap(),
        cuB=nc.dram_tensor("cuB", [10, 128, 2048], BF16, kind="Internal").ap(),
        cwd=nc.dram_tensor("cwd", [10, 4, 128, 5632], BF16, kind="Internal").ap(),
    )


def _decl_attn_drams(nc, layer, shared):
    pre = "a%d_" % layer
    d = dict(
        wqi=nc.dram_tensor(pre + "wq", [16, 128, 2048], F32, kind="ExternalInput").ap(),
        wki=nc.dram_tensor(pre + "wk", [16, 128, 2048], F32, kind="ExternalInput").ap(),
        wvi=nc.dram_tensor(pre + "wv", [4, 128, 8192], F32, kind="ExternalInput").ap(),
        woi=nc.dram_tensor(pre + "wo", [4, 128, 8192], F32, kind="ExternalInput").ap(),
        tri_s=shared["tri_s"], tri_t=shared["tri_t"], eyeb=shared["eyeb"],
    )
    if layer == 0:
        d.update(
            bqi=nc.dram_tensor("bqi", [128, 16], F32, kind="ExternalInput").ap(),
            bki=nc.dram_tensor("bki", [128, 16], F32, kind="ExternalInput").ap(),
            bvr=nc.dram_tensor("bvr", [1, D], F32, kind="ExternalInput").ap(),
            bor=nc.dram_tensor("bor", [1, D], F32, kind="ExternalInput").ap(),
            alibihi=nc.dram_tensor("alibihi", [16, T], BF16, kind="ExternalInput").ap(),
            alc=nc.dram_tensor("alc", [128, 256], F32, kind="ExternalInput").ap(),
        )
    return d


def _decl_shared(nc):
    return dict(
        tri_s=nc.dram_tensor("tri_s", [128, 128], F32, kind="ExternalInput").ap(),
        tri_t=nc.dram_tensor("tri_t", [128, 128], F32, kind="ExternalInput").ap(),
        eyeb=nc.dram_tensor("eyeb", [128, 128], BF16, kind="ExternalInput").ap(),
    )


def _scratch(nc):
    return dict(
        xnf_hi=nc.dram_tensor("xnfh", [16, 128, T], BF16, kind="Internal").ap(),
        xnf_lo=nc.dram_tensor("xnfl", [16, 128, T], BF16, kind="Internal").ap(),
        qd=nc.dram_tensor("qd", [4, 128, 16, 512], BF16, kind="Internal").ap(),
        kd=nc.dram_tensor("kd", [16, 128, T], BF16, kind="Internal").ap(),
        vd=nc.dram_tensor("vd", [16, 128, 16, 128], BF16, kind="Internal").ap(),
    )


def build_fused():
    nc = bass.Bass("TRN2", target_bir_lowering=False)
    xin = nc.dram_tensor("xin", [T, D], F32, kind="ExternalInput").ap()
    xout = nc.dram_tensor("xout", [T, D], F32, kind="ExternalOutput").ap()
    f0 = _decl_ffn_drams(nc, 0)
    f1 = _decl_ffn_drams(nc, 1)
    f1["eyef"] = f0["eyef"]
    shared = _decl_shared(nc)

    a0 = _decl_attn_drams(nc, 0, shared)
    a1 = _decl_attn_drams(nc, 1, shared)
    scr = _scratch(nc)
    with ExitStack() as ctx:
        C = {}
        C["x"] = _x_tiles(nc, ctx)
        C["xin"], C["xout"] = xin, xout
        C["eyeb_d"] = shared["eyeb"]
        C.update(scr)
        p = Prog(nc, ctx)
        emit_load_x(p, C)
        _ffn_sublayer(nc, p, C, 0, f0, "s0", populate=True)
        _attn_sublayer(nc, p, C, 0, a0, "s1")
        _ffn_sublayer(nc, p, C, 0, f0, "s2", populate=False)
        _ffn_sublayer(nc, p, C, 1, f1, "s3", populate=True)
        _attn_sublayer(nc, p, C, 1, a1, "s4")
        _ffn_sublayer(nc, p, C, 1, f1, "s5", populate=False)
        emit_store_x(p, C)
        p.materialize()
    return nc


def build_ffn_program(layer):
    nc = bass.Bass("TRN2", target_bir_lowering=False)
    xin = nc.dram_tensor("xin", [T, D], F32, kind="ExternalInput").ap()
    xout = nc.dram_tensor("xout", [T, D], F32, kind="ExternalOutput").ap()
    aps = _decl_ffn_drams(nc, layer)
    if layer == 1:
        aps["eyef"] = nc.dram_tensor("eyef", [128, 128], F32, kind="ExternalInput").ap()
    eyeb_d = nc.dram_tensor("eyeb", [128, 128], BF16, kind="ExternalInput").ap()
    scr = _scratch(nc)
    with ExitStack() as ctx:
        C = {}
        C["x"] = _x_tiles(nc, ctx)
        C["xin"], C["xout"] = xin, xout
        C["eyeb_d"] = eyeb_d
        C.update(scr)
        p = Prog(nc, ctx)
        emit_load_x(p, C)
        _ffn_sublayer(nc, p, C, layer, aps, "s0")
        emit_store_x(p, C)
        p.materialize()
    return nc


def build_attn_program(layer):
    nc = bass.Bass("TRN2", target_bir_lowering=False)
    xin = nc.dram_tensor("xin", [T, D], F32, kind="ExternalInput").ap()
    xout = nc.dram_tensor("xout", [T, D], F32, kind="ExternalOutput").ap()
    shared = _decl_shared(nc)
    aps = _decl_attn_drams(nc, layer, shared)
    scr = _scratch(nc)
    with ExitStack() as ctx:
        C = {}
        C["x"] = _x_tiles(nc, ctx)
        C["xin"], C["xout"] = xin, xout
        C["eyeb_d"] = shared["eyeb"]
        C.update(scr)
        p = Prog(nc, ctx)
        emit_load_x(p, C)
        _attn_sublayer(nc, p, C, layer, aps, "s0")
        emit_store_x(p, C)
        p.materialize()
    return nc



def _consts():
    r = np.arange(128)
    eyef = np.eye(128, dtype=np.float32)
    eyeb = np.eye(128).astype(BF)
    tri_s = np.where(r[None, :] > r[:, None], NEG, 0.0).astype(np.float32)
    tri_t = np.where(r[None, :] < r[:, None], NEG, 0.0).astype(np.float32)
    start = 2.0 ** (-2.0 ** (-(np.log2(16) - 3)))
    slopes = np.array([start * start ** i for i in range(16)], dtype=np.float32)
    kk = np.arange(T, dtype=np.float32)
    alibi = (-slopes[:, None] * (T - 1.0 - kk)[None, :]).astype(np.float32)
    alibihi = np.ascontiguousarray(alibi.astype(BF))
    alc = np.zeros((128, 256), dtype=np.float32)
    for h in range(16):
        for c in range(16):
            alc[:, 16 * h + c] = alibi[h, 128 * c:128 * (c + 1)]
    return dict(eyef=eyef, eyeb=eyeb, tri_s=tri_s, tri_t=tri_t,
                alibihi=alibihi, alc=alc)


def _img_ffn1(W):
    E, Dm, H0 = W.shape
    g = H0 // 256
    return np.ascontiguousarray(
        W.reshape(E, 16, 128, g, 256).transpose(0, 3, 2, 1, 4).reshape(E, g, 128, 16 * 256))


def _img_n512(W, nh):
    E = W.shape[0]
    return np.ascontiguousarray(
        W.reshape(E, nh, 128, 4, 512).transpose(0, 3, 2, 1, 4).reshape(E, 4, 128, nh * 512))


def _prep(inputs):
    cst = _consts()
    W1 = np.concatenate([inputs["L0_ffn_w1"].transpose(0, 2, 1),
                         inputs["L0_ffn_ws1"].T[None]], axis=0)
    W3 = np.concatenate([inputs["L0_ffn_w3"].transpose(0, 2, 1),
                         inputs["L0_ffn_ws3"].T[None]], axis=0)
    W2 = np.concatenate([inputs["L0_ffn_w2"].transpose(0, 2, 1),
                         inputs["L0_ffn_ws2"].T[None]], axis=0)
    WG = np.concatenate([inputs["L1_ffn_We_gate"],
                         inputs["L1_ffn_Ws_gate"][:, 0:1408][None],
                         inputs["L1_ffn_Ws_gate"][:, 1408:2816][None]], axis=0)
    WU = np.concatenate([inputs["L1_ffn_We_up"],
                         inputs["L1_ffn_Ws_up"][:, 0:1408][None],
                         inputs["L1_ffn_Ws_up"][:, 1408:2816][None]], axis=0)
    WDn = np.concatenate([inputs["L1_ffn_We_down"],
                          inputs["L1_ffn_Ws_down"][0:1408][None],
                          inputs["L1_ffn_Ws_down"][1408:2816][None]], axis=0)

    def imgA(W):
        return np.ascontiguousarray(
            W[:, :, 0:1280].reshape(10, 16, 128, 5, 256).transpose(0, 3, 2, 1, 4)
            .reshape(10, 5, 128, 16 * 256))

    def imgB(W):
        return np.ascontiguousarray(
            W[:, :, 1280:1408].reshape(10, 16, 128, 128).transpose(0, 2, 1, 3)
            .reshape(10, 128, 16 * 128))

    def aimg128(W):
        return np.ascontiguousarray(
            W.reshape(16, 128, 16, 128).transpose(2, 1, 0, 3).reshape(16, 128, 2048))

    def aimg512(W):
        return np.ascontiguousarray(
            W.reshape(16, 128, 4, 512).transpose(2, 1, 0, 3).reshape(4, 128, 16 * 512))

    h = dict(
        eyef=cst["eyef"], eyeb=cst["eyeb"], tri_s=cst["tri_s"], tri_t=cst["tri_t"],
        alibihi=cst["alibihi"], alc=cst["alc"],
        g0img=np.ascontiguousarray(
            inputs["L0_ffn_gate_w"].T.reshape(16, 128, 8).transpose(1, 0, 2).reshape(128, 128)),
        w1i=_img_ffn1(W1), w3i=_img_ffn1(W3), w2i=_img_n512(W2, 8),
        a0_wq=aimg128(inputs["L0_attn_Wq"]), a0_wk=aimg128(inputs["L0_attn_Wk"]),
        a0_wv=aimg512(inputs["L0_attn_Wv"]), a0_wo=aimg512(inputs["L0_attn_Wo"]),
        bqi=np.ascontiguousarray(inputs["L0_attn_bq"].reshape(16, 128).T),
        bki=np.ascontiguousarray(inputs["L0_attn_bk"].reshape(16, 128).T),
        bvr=np.ascontiguousarray(inputs["L0_attn_bv"].reshape(1, D)),
        bor=np.ascontiguousarray(inputs["L0_attn_bo"].reshape(1, D)),
        g1img=np.ascontiguousarray(
            inputs["L1_ffn_W_router"].reshape(16, 128, 8).transpose(1, 0, 2).reshape(128, 128)),
        wgA=imgA(WG), wgB=imgB(WG), wuA=imgA(WU), wuB=imgB(WU), wdi=_img_n512(WDn, 11),
        a1_wq=aimg128(np.ascontiguousarray(inputs["L1_attn_Wq"].T)),
        a1_wk=aimg128(np.ascontiguousarray(inputs["L1_attn_Wk"].T)),
        a1_wv=aimg512(np.ascontiguousarray(inputs["L1_attn_Wv"].T)),
        a1_wo=aimg512(np.ascontiguousarray(inputs["L1_attn_Wo"].T)),
    )
    return h


_PKEYS = {
    "f0": ["g0img", "eyef", "eyeb", "w1i", "w3i", "w2i"],
    "f1": ["g1img", "eyef", "eyeb", "wgA", "wgB", "wuA", "wuB", "wdi"],
    "a0": ["a0_wq", "a0_wk", "a0_wv", "a0_wo", "tri_s", "tri_t", "eyeb",
           "bqi", "bki", "bvr", "bor", "alibihi", "alc"],
    "a1": ["a1_wq", "a1_wk", "a1_wv", "a1_wo", "tri_s", "tri_t", "eyeb"],
}

_PROGS = {}


def _get_prog(name):
    if name not in _PROGS:
        if name == "fused":
            _PROGS[name] = build_fused()
        elif name == "f0":
            _PROGS[name] = build_ffn_program(0)
        elif name == "f1":
            _PROGS[name] = build_ffn_program(1)
        elif name == "a0":
            _PROGS[name] = build_attn_program(0)
        elif name == "a1":
            _PROGS[name] = build_attn_program(1)
    return _PROGS[name]


def kernel(**inputs):
    h = _prep(inputs)
    nc = _get_prog("fused")
    in_maps = [dict(h, xin=np.ascontiguousarray(inputs["L0_attn_x"][b])) for b in range(8)]
    res = run_bass_kernel_spmd(nc, in_maps, core_ids=list(range(8)))
    return np.stack([res.results[c]["xout"] for c in range(8)], axis=0).astype(np.float32)
